# version 16
# baseline (speedup 1.0000x reference)
import math
import os
import numpy as np
import ml_dtypes
from contextlib import ExitStack

import concourse.bass as bass
import concourse.mybir as mybir
from concourse.bass_utils import run_bass_kernel_spmd

F32 = mybir.dt.float32
BF16 = mybir.dt.bfloat16
AF = mybir.ActivationFunctionType
ALU = mybir.AluOpType
AXX = mybir.AxisListType.X

C = 1024
H = 8
SCALE = 1.0 / math.sqrt(128.0)
EPS = 1e-5

ENGINES = ("sync", "tensor", "scalar", "vector", "gpsimd")


class Prog:
    def __init__(self, nc):
        self.nc = nc
        self.q = {e: [] for e in ENGINES}
        self.sems = {}
        self.cnt = {}
        self.last_wait = {}

    def sem(self, name, es):
        h = es.enter_context(self.nc.semaphore(name))
        self.sems[name] = h
        self.cnt[name] = 0
        return name

    def op(self, eng, fn, waits=(), inc=None, incn=1):
        wl = []
        for s, v in waits:
            if v is None or v <= 0:
                continue
            k = (eng, s)
            if self.last_wait.get(k, -1) >= v:
                continue
            self.last_wait[k] = v
            wl.append((self.sems[s], v))
        if inc is not None:
            self.cnt[inc] += incn
        self.q[eng].append((fn, wl, self.sems[inc] if inc else None, incn))

    def dma(self, eng, fn, stream, waits=()):
        self.op(eng, fn, waits=tuple(waits) + ((stream, self.cnt[stream]),),
                inc=stream, incn=16)

    def emit(self, block):
        def make_runner(eng_name):
            def run(eng):
                for fn, wl, inc, incn in self.q[eng_name]:
                    for h, v in wl:
                        eng.wait_ge(h, v)
                    ins = fn(eng)
                    if inc is not None:
                        if ins is None:
                            ins = eng.nop()
                        ins.then_inc(inc, incn)
            return run
        block.sync(make_runner("sync"))
        block.tensor(make_runner("tensor"))
        block.scalar(make_runner("scalar"))
        block.vector(make_runner("vector"))
        block.gpsimd(make_runner("gpsimd"))

    def flush(self):
        if any(self.q.values()):
            with self.nc.Block() as block:
                self.emit(block)
        self.q = {e: [] for e in ENGINES}


def host_consts():
    dirs = np.array(
        [[(i >> 2 & 1) * 2 - 1, (i >> 1 & 1) * 2 - 1, (i & 1) * 2 - 1] for i in range(8)],
        dtype=np.float32) / np.float32(np.sqrt(3.0, dtype=np.float32))
    c = {}
    c["eye_bf"] = np.eye(128, dtype=np.float32).astype(ml_dtypes.bfloat16)
    c["ones_bf"] = np.ones((128, 128), dtype=ml_dtypes.bfloat16)
    c["ones_f"] = np.ones((128, 128), dtype=np.float32)
    sel = np.zeros((8, 8 * 128), dtype=np.float32)
    for cc in range(8):
        sel[cc, cc * 128:(cc + 1) * 128] = 1.0
    c["sel8"] = sel.astype(ml_dtypes.bfloat16)
    d33 = np.zeros((3, 24), dtype=np.float32)
    for h in range(8):
        for j in range(3):
            d33[:, 3 * h + j] = dirs[h]
    c["dirs33"] = d33.astype(ml_dtypes.bfloat16)
    c["d3s"] = np.ascontiguousarray((dirs.T * np.float32(np.sqrt(128.0))).astype(np.float32))
    m = np.zeros((128, 4 * 512), dtype=np.float32)
    kk = np.arange(128).reshape(128, 1)
    qq = np.arange(512).reshape(1, 512)
    for j in range(4):
        m[:, j * 512:(j + 1) * 512] = (128 * j + kk <= qq).astype(np.float32)
    c["masks"] = m.astype(ml_dtypes.bfloat16)
    sc = np.zeros((128, 8), dtype=np.float32)
    sc[:, 0] = EPS
    sc[:, 1] = 1.0
    c["scal"] = sc
    return c


def build_nc(T, phases="abc", dump=False):
    assert T % 512 == 0
    NT = T // 512
    NQT = NT
    nc = bass.Bass("TRN2", target_bir_lowering=False)

    x_in = nc.dram_tensor("x", [T, C], F32, kind="ExternalInput").ap()
    w_qkv = nc.dram_tensor("w_qkv", [C, 3 * C], F32, kind="ExternalInput").ap()
    w_out = nc.dram_tensor("w_out", [C, C], F32, kind="ExternalInput").ap()
    hs_in = nc.dram_tensor("head_scales", [1, H], F32, kind="ExternalInput").ap()
    w_r = nc.dram_tensor("w_r", [C, C], F32, kind="ExternalInput").ap()
    w_k = nc.dram_tensor("w_k", [C, C], F32, kind="ExternalInput").ap()
    w_v = nc.dram_tensor("w_v", [C, C], F32, kind="ExternalInput").ap()
    mu_r_in = nc.dram_tensor("mu_r", [C], F32, kind="ExternalInput").ap()
    mu_k_in = nc.dram_tensor("mu_k", [C], F32, kind="ExternalInput").ap()
    eye_d = nc.dram_tensor("eye_bf", [128, 128], BF16, kind="ExternalInput").ap()
    onesb_d = nc.dram_tensor("ones_bf", [128, 128], BF16, kind="ExternalInput").ap()
    onesf_d = nc.dram_tensor("ones_f", [128, 128], F32, kind="ExternalInput").ap()
    sel8_d = nc.dram_tensor("sel8", [8, 8 * 128], BF16, kind="ExternalInput").ap()
    dirs33_d = nc.dram_tensor("dirs33", [3, 24], BF16, kind="ExternalInput").ap()
    d3s_d = nc.dram_tensor("d3s", [3, 8], F32, kind="ExternalInput").ap()
    masks_d = nc.dram_tensor("masks", [128, 4 * 512], BF16, kind="ExternalInput").ap()
    scal_d = nc.dram_tensor("scal", [128, 8], F32, kind="ExternalInput").ap()
    out_d = nc.dram_tensor("out", [T, C], F32, kind="ExternalOutput").ap()
    skind = "ExternalOutput" if dump else "Internal"
    dbg = nc.dram_tensor("dbg", [128, 32], F32, kind="ExternalOutput").ap() if dump else None
    q_scr = nc.dram_tensor("q_scr", [H, 128, T], BF16, kind=skind).ap()
    k_scr = nc.dram_tensor("k_scr", [H, 128, T], BF16, kind=skind).ap()
    v_scr = nc.dram_tensor("v_scr", [T, C], BF16, kind=skind).ap()
    o_scr = nc.dram_tensor("o_scr", [C, T], BF16, kind=skind).ap()
    l_scr = nc.dram_tensor("l_scr", [H, T], F32, kind=skind).ap()

    es0 = ExitStack()

    def SB(stk, name, shape, dt):
        return stk.enter_context(nc.sbuf_tensor(name, shape, dt))

    def PS(stk, name, shape, dt):
        return stk.enter_context(nc.psum_tensor(name, shape, dt))

    eye_sb = SB(es0, "eye_sb", [128, 128], BF16)
    onesb_sb = SB(es0, "onesb_sb", [128, 128], BF16)
    onesf_sb = SB(es0, "onesf_sb", [128, 128], F32)
    scal_sb = SB(es0, "scal_sb", [128, 8], F32)
    mu_sb = SB(es0, "mu_sb", [128, 16], F32)
    negB = SB(es0, "negB", [128, H * NQT], F32)
    wout_sb = SB(es0, "wout_sb", [128, 8 * 1024], BF16)
    wrT_sb = SB(es0, "wrT_sb", [128, 8 * 1024], BF16)
    wkT_sb = SB(es0, "wkT_sb", [128, 8 * 1024], BF16)
    wvT_sb = SB(es0, "wvT_sb", [128, 8 * 1024], BF16)
    junk_a = SB(es0, "junk_a", [128, 1024], BF16)
    stat = SB(es0, "stat", [128, 48], F32)

    prog = Prog(nc)
    P = prog.op
    S = lambda n: prog.sem(n, es0)
    CNT = lambda s: prog.cnt[s]

    bar_i = [0]

    def barrier():
        s = S(f"bar{bar_i[0]}")
        bar_i[0] += 1
        for e in ENGINES:
            P(e, lambda eng: eng.nop(), inc=s)
        for e in ENGINES:
            P(e, lambda eng: eng.nop(), waits=((s, len(ENGINES)),))

    es_ia = ExitStack()
    wqkv_sb = SB(es_ia, "wqkv_sb", [128, 8 * 3072], BF16)
    dirs33_sb = SB(es_ia, "dirs33_sb", [3, 24], BF16)
    d3s_sb = SB(es_ia, "d3s_sb", [3, 8], F32)
    c3_sb = SB(es_ia, "c3_sb", [3, 8], F32)
    hsrow = SB(es_ia, "hsrow", [1, 8], F32)
    tp_ps = PS(es_ia, "tp_ps", [128, 2048], BF16)
    proj_ps = PS(es_ia, "proj_ps", [128, 512], F32)

    s_ic = S("ic")
    s_ipe = S("ipe")
    s_ism = S("ism")
    s_wst = [S("wst0"), S("wst1")]

    small = [
        (eye_sb, eye_d), (onesb_sb, onesb_d), (onesf_sb, onesf_d),
        (dirs33_sb, dirs33_d), (d3s_sb, d3s_d), (scal_sb, scal_d), (hsrow, hs_in),
    ]
    for dst, src in small:
        prog.dma("sync", lambda e, d=dst, s=src: e.dma_start(out=d[:], in_=s), s_ism)
    for cc in range(8):
        prog.dma("sync", lambda e, cc=cc: e.dma_start(
            out=mu_sb[:, cc:cc + 1],
            in_=mu_r_in[cc * 128:(cc + 1) * 128].rearrange("(p o) -> p o", o=1)), s_ism)
        prog.dma("sync", lambda e, cc=cc: e.dma_start(
            out=mu_sb[:, 8 + cc:9 + cc],
            in_=mu_k_in[cc * 128:(cc + 1) * 128].rearrange("(p o) -> p o", o=1)), s_ism)
    ism_total = CNT(s_ism)

    es_i = ExitStack()
    wstg = SB(es_i, "wstg", [128, 2 * 3072], F32)
    wtmp_bf = SB(es_i, "wtmp_bf", [128, 2 * 1024], BF16)
    _cut = int(os.environ.get("INIT_CUT", "9"))

    cast_evt = {}
    ci = 0
    for cc in range(8 if _cut >= 2 else 0):
        buf = ci % 2
        prog.dma("sync", lambda e, cc=cc, buf=buf: e.dma_start(
            out=wstg[:, buf * 3072:(buf + 1) * 3072],
            in_=w_qkv[cc * 128:(cc + 1) * 128, :]),
          s_wst[buf], waits=((s_ic, cast_evt.get(ci - 2)),))
        P("vector", lambda e, cc=cc, buf=buf: e.tensor_copy(
            out=wqkv_sb[:, cc * 3072:(cc + 1) * 3072],
            in_=wstg[:, buf * 3072:(buf + 1) * 3072]),
          waits=((s_wst[buf], CNT(s_wst[buf])),), inc=s_ic)
        cast_evt[ci] = CNT(s_ic)
        ci += 1
    for cc in range(8 if _cut >= 3 else 0):
        buf = ci % 2
        prog.dma("sync", lambda e, cc=cc, buf=buf: e.dma_start(
            out=wstg[:, buf * 3072:buf * 3072 + 1024],
            in_=w_out[cc * 128:(cc + 1) * 128, :]),
          s_wst[buf], waits=((s_ic, cast_evt.get(ci - 2)),))
        P("vector", lambda e, cc=cc, buf=buf: e.tensor_copy(
            out=wout_sb[:, cc * 1024:(cc + 1) * 1024],
            in_=wstg[:, buf * 3072:buf * 3072 + 1024]),
          waits=((s_wst[buf], CNT(s_wst[buf])),), inc=s_ic)
        cast_evt[ci] = CNT(s_ic)
        ci += 1
    for (wsrc, wdst) in ([(w_r, wrT_sb), (w_k, wkT_sb), (w_v, wvT_sb)] if _cut >= 4 else []):
        for cc in range(8):
            buf = ci % 2
            prog.dma("sync", lambda e, wsrc=wsrc, cc=cc, buf=buf: e.dma_start(
                out=wstg[:, buf * 3072:buf * 3072 + 1024],
                in_=wsrc[cc * 128:(cc + 1) * 128, :]),
              s_wst[buf], waits=((s_ic, cast_evt.get(ci - 2)),))
            P("vector", lambda e, wdst=wdst, cc=cc, buf=buf: e.tensor_copy(
                out=wdst[:, cc * 1024:(cc + 1) * 1024],
                in_=wstg[:, buf * 3072:buf * 3072 + 1024]),
              waits=((s_wst[buf], CNT(s_wst[buf])),), inc=s_ic)
            cast_evt[ci] = CNT(s_ic)
            ci += 1
    if _cut >= 5:
        P("tensor", lambda e: e.matmul(out=proj_ps[0:3, 0:8], lhsT=onesf_sb[0:1, 0:3],
                                       rhs=hsrow[:, :], start=True, stop=True),
          waits=((s_ism, ism_total),), inc=s_ipe)
        P("vector", lambda e: e.tensor_tensor(out=c3_sb[:, :], in0=proj_ps[0:3, 0:8],
                                              in1=d3s_sb[:, :], op=ALU.mult),
          waits=((s_ipe, CNT(s_ipe)),), inc=s_ic)

    P("sync", lambda e: e.nop(), waits=((s_ism, ism_total),))
    barrier()
    prog.flush()
    es_i.close()

    if "a" not in phases:
        prog.flush()
        es_ia.close()
        es0.close()
        nc.finalize()
        return nc

    s_ax = [S("ax0"), S("ax1")]
    s_dbg = S("dbg0")
    s_aact = S("aact")
    s_adve = S("adve")
    s_ape = S("ape")
    s_apet = S("apet")
    s_atev = S("atev")
    s_qs = [S("qs0"), S("qs1")]
    s_ks = [S("ks0"), S("ks1")]
    s_vs = [S("vs0"), S("vs1")]

    es_a = ExitStack()
    xt = SB(es_a, "xt", [128, 2 * 4096], F32)
    xhat_bf = SB(es_a, "xhat_bf", [128, 2 * 4096], BF16)
    xhatT = SB(es_a, "xhatT", [128, 8 * 512], BF16)
    qk_ps = PS(es_a, "qk_ps", [128, 2048], F32)
    nrm_ps = PS(es_a, "nrm_ps", [128, 512], F32)
    qkst = SB(es_a, "qkst", [128, 4 * 512], BF16)
    qdm = SB(es_a, "qdm", [128, 8 * 512], BF16)
    kdm = SB(es_a, "kdm", [128, 8 * 512], BF16)
    vbuf = SB(es_a, "vbuf", [128, 2 * 1024], BF16)
    sqv = SB(es_a, "sqv", [128, 512], BF16)
    mq2 = SB(es_a, "mq2", [128, 8], F32)
    mk2run = SB(es_a, "mk2run", [128, 8], F32)
    b2t = SB(es_a, "b2t", [128, 8], F32)
    v_ps = qk_ps

    qk_evac2 = {}
    qk_tp_evt = {}
    fix_evt = {}
    v_evac = {}
    ax_done = {}
    dve_var = {}
    act_sq = {}
    act_rstd = {}
    cast_done = {}
    xtp_done = {}
    perhead_done = {}
    qspl = {}
    zfree = {0: 0, 1: 0, 2: 0, 3: 0}
    nblk = [0]

    def emit_xdma(j):
        xb = j % 2
        t0j = j * 512
        prog.dma("sync", lambda e, t0j=t0j, xb=xb: e.dma_start(
            out=xt[:, xb * 4096:(xb + 1) * 4096].rearrange("p (c f) -> p c f", c=4),
            in_=x_in[t0j:t0j + 512, :].rearrange("(c p) f -> p c f", p=128)),
          s_ax[xb], waits=((s_adve, cast_done.get(j - 2)),))
        ax_done[j] = CNT(s_ax[xb])

    def emit_ln_p1(j):
        xb = j % 2
        for tt in range(4):
            P("vector", lambda e, xb=xb, tt=tt: e.reduce_sum(
                out=stat[:, tt:tt + 1],
                in_=xt[:, xb * 4096 + tt * 1024:xb * 4096 + (tt + 1) * 1024], axis=AXX),
              waits=((s_ax[xb], ax_done[j]),), inc=s_adve)

    def emit_ln_p2(j):
        xb = j % 2
        for tt in range(4):
            P("scalar", lambda e, xb=xb, tt=tt: e.activation(
                out=junk_a[:], in_=xt[:, xb * 4096 + tt * 1024:xb * 4096 + (tt + 1) * 1024],
                func=AF.Square, accum_out=stat[:, 8 + tt:9 + tt]),
              waits=((s_ax[xb], ax_done[j]), (s_adve, dve_var.get(j - 1))),
              inc=s_aact)
        act_sq[j] = CNT(s_aact)
        P("vector", lambda e: e.tensor_scalar(
            out=stat[:, 16:20], in0=stat[:, 0:4], scalar1=1.0 / C, scalar2=None, op0=ALU.mult),
          waits=((s_adve, CNT(s_adve)),), inc=s_adve)
        P("vector", lambda e: e.tensor_tensor(
            out=stat[:, 24:28], in0=stat[:, 16:20], in1=stat[:, 16:20], op=ALU.mult),
          waits=((s_adve, CNT(s_adve)),), inc=s_adve)
        P("vector", lambda e: e.scalar_tensor_tensor(
            out=stat[:, 24:28], in0=stat[:, 8:12], scalar=1.0 / C, in1=stat[:, 24:28],
            op0=ALU.mult, op1=ALU.subtract),
          waits=((s_aact, act_sq[j]), (s_adve, CNT(s_adve))), inc=s_adve)
        dve_var[j] = CNT(s_adve)
        P("scalar", lambda e: e.activation(out=stat[:, 24:28], in_=stat[:, 24:28],
                                           func=AF.Ln, bias=scal_sb[:, 0:1]),
          waits=((s_adve, dve_var[j]),))
        P("scalar", lambda e: e.activation(out=stat[:, 24:28], in_=stat[:, 24:28],
                                           func=AF.Exp, scale=-0.5), inc=s_aact)
        act_rstd[j] = CNT(s_aact)

    def emit_ln_p3(j):
        xb = j % 2
        for tt in range(4):
            P("vector", lambda e, xb=xb, tt=tt: e.tensor_scalar(
                out=xhat_bf[:, xb * 4096 + tt * 1024:xb * 4096 + (tt + 1) * 1024],
                in0=xt[:, xb * 4096 + tt * 1024:xb * 4096 + (tt + 1) * 1024],
                scalar1=stat[:, 16 + tt:17 + tt], scalar2=stat[:, 24 + tt:25 + tt],
                op0=ALU.subtract, op1=ALU.mult),
              waits=((s_aact, act_rstd[j]), (s_adve, CNT(s_adve)),
                     (s_apet, xtp_done.get(j - 2))),
              inc=s_adve)
        cast_done[j] = CNT(s_adve)

    emit_xdma(0)
    emit_ln_p1(0)
    emit_ln_p2(0)
    emit_ln_p3(0)

    for i in range(NT):
        t0 = i * 512
        xb = i % 2
        if i + 1 < NT:
            emit_xdma(i + 1)
        for g in range(4):
            gg = i * 4 + g
            tb = (gg % 2) * 1024
            z0, z1 = 2 * (gg % 2), 2 * (gg % 2) + 1
            for s in range(8):
                j = g * 8 + s
                tt, cc = divmod(j, 8)
                P("tensor", lambda e, tt=tt, cc=cc, s=s, tb=tb, xb=xb: e.transpose(
                    out=tp_ps[:, tb + s * 128:tb + (s + 1) * 128],
                    in_=xhat_bf[:, xb * 4096 + tt * 1024 + cc * 128:xb * 4096 + tt * 1024 + (cc + 1) * 128],
                    identity=eye_sb[:]),
                  waits=((s_adve, cast_done[i]), (s_atev, max(zfree[z0], zfree[z1]))),
                  inc=s_apet)
            tr_done = CNT(s_apet)
            for s in range(8):
                j = g * 8 + s
                tt, cc = divmod(j, 8)
                P("vector", lambda e, tt=tt, cc=cc, s=s, tb=tb: e.tensor_copy(
                    out=xhatT[:, cc * 512 + tt * 128:cc * 512 + tt * 128 + 128],
                    in_=tp_ps[:, tb + s * 128:tb + (s + 1) * 128]),
                  waits=((s_apet, tr_done),), inc=s_atev)
            zfree[z0] = zfree[z1] = CNT(s_atev)
        tev_done = CNT(s_atev)
        xtp_done[i] = CNT(s_apet)
        for ci2 in range(16):
            tt, rest = divmod(ci2, 4)
            which, g = divmod(rest, 2)
            bank = ci2 % 4
            sl = ci2 % 4
            wv_extra = (v_evac.get(4 * (i - 1) + 2 + ci2 // 2) if ci2 < 4 else None)
            cbase = which * 1024 + g * 512
            for cc in range(8):
                P("tensor", lambda e, bank=bank, cc=cc, tt=tt, cbase=cbase: e.matmul(
                    out=qk_ps[:, bank * 512:(bank + 1) * 512],
                    lhsT=xhatT[:, cc * 512 + tt * 128:cc * 512 + tt * 128 + 128],
                    rhs=wqkv_sb[:, cc * 3072 + cbase:cc * 3072 + cbase + 512],
                    start=(cc == 0), stop=(cc == 7)),
                  waits=((s_atev, tev_done),
                         (s_aact, qk_evac2.get((i, ci2 - 4))),
                         (s_aact, wv_extra)),
                  inc=(s_ape if cc == 7 else None))
            pe_mm = CNT(s_ape)
            P("scalar", lambda e, sl=sl, bank=bank, which=which: e.activation(
                out=qkst[:, sl * 512:(sl + 1) * 512],
                in_=qk_ps[:, bank * 512:(bank + 1) * 512],
                func=AF.Copy, scale=(SCALE if which == 0 else 1.0)),
              waits=((s_ape, pe_mm), (s_apet, qk_tp_evt.get((i, ci2 - 4)))),
              inc=s_aact)
            qk_evac2[(i, ci2)] = CNT(s_aact)
            zq = ci2 % 4
            tb2 = zq * 512
            for j in range(4):
                P("tensor", lambda e, sl=sl, j=j, tb2=tb2: e.transpose(
                    out=tp_ps[:, tb2 + j * 128:tb2 + (j + 1) * 128],
                    in_=qkst[:, sl * 512 + j * 128:sl * 512 + (j + 1) * 128],
                    identity=eye_sb[:]),
                  waits=((s_aact, qk_evac2[(i, ci2)]), (s_atev, zfree[zq])),
                  inc=s_apet)
            qk_tp_evt[(i, ci2)] = CNT(s_apet)
            for j in range(4):
                h2 = g * 4 + j
                dst = qdm if which == 0 else kdm
                P("vector", lambda e, dst=dst, h2=h2, j=j, tb2=tb2, tt=tt: e.tensor_copy(
                    out=dst[:, h2 * 512 + tt * 128:h2 * 512 + tt * 128 + 128],
                    in_=tp_ps[:, tb2 + j * 128:tb2 + (j + 1) * 128]),
                  waits=((s_apet, qk_tp_evt[(i, ci2)]),
                         (s_aact, perhead_done.get(i - 1)))
                        + qspl.get(i - 1, ()),
                  inc=s_atev)
            zfree[zq] = CNT(s_atev)
        tev_qk = CNT(s_atev)
        prev_nrm = [0]
        for h in range(H):
            if i + 1 < NT:
                if h == 2:
                    emit_ln_p1(i + 1)
                elif h == 4:
                    emit_ln_p2(i + 1)
                elif h == 6:
                    emit_ln_p3(i + 1)
            gh = i * H + h
            hs = h * 512
            P("tensor", lambda e, h=h, hs=hs: e.matmul(
                out=proj_ps[0:3, :], lhsT=dirs33_sb[:, 3 * h:3 * h + 3],
                rhs=qdm[0:3, hs:hs + 512], start=True, stop=True),
              waits=((s_atev, tev_qk), (s_adve, fix_evt.get(gh - 1))),
              inc=s_ape)
            P("vector", lambda e, h=h, hs=hs: e.scalar_tensor_tensor(
                out=qdm[0:3, hs:hs + 512],
                in0=proj_ps[0:3, :], scalar=c3_sb[:, h:h + 1],
                in1=qdm[0:3, hs:hs + 512],
                op0=ALU.mult, op1=ALU.add),
              waits=((s_ape, CNT(s_ape)),), inc=s_adve)
            fix_evt[gh] = CNT(s_adve)
            prog.dma("sync", lambda e, h=h, hs=hs, t0=t0: e.dma_start(
                out=q_scr[h, :, t0:t0 + 512], in_=qdm[:, hs:hs + 512]),
              s_qs[h % 2], waits=((s_adve, fix_evt[gh]),))
            P("scalar", lambda e, hs=hs: e.activation(
                out=sqv[:], in_=qdm[:, hs:hs + 512], func=AF.Square),
              waits=((s_adve, fix_evt[gh]), (s_ape, prev_nrm[0])), inc=s_aact)
            sq_q = CNT(s_aact)
            P("tensor", lambda e: e.matmul(
                out=nrm_ps[:], lhsT=onesb_sb[:], rhs=sqv[:], start=True, stop=True),
              waits=((s_aact, sq_q), (s_adve, CNT(s_adve))), inc=s_ape)
            nrm_q = CNT(s_ape)
            P("vector", lambda e, h=h: e.reduce_max(
                out=mq2[:, h:h + 1], in_=nrm_ps[:], axis=AXX),
              waits=((s_ape, nrm_q),), inc=s_adve)
            prog.dma("sync", lambda e, h=h, hs=hs, t0=t0: e.dma_start(
                out=k_scr[h, :, t0:t0 + 512], in_=kdm[:, hs:hs + 512]),
              s_ks[h % 2], waits=((s_atev, tev_qk),))
            P("scalar", lambda e, hs=hs: e.activation(
                out=sqv[:], in_=kdm[:, hs:hs + 512], func=AF.Square),
              waits=((s_ape, nrm_q),), inc=s_aact)
            sq_k = CNT(s_aact)
            P("tensor", lambda e: e.matmul(
                out=nrm_ps[:], lhsT=onesb_sb[:], rhs=sqv[:], start=True, stop=True),
              waits=((s_aact, sq_k), (s_adve, CNT(s_adve))), inc=s_ape)
            nrm_k = CNT(s_ape)
            prev_nrm[0] = nrm_k
            P("vector", lambda e, h=h: e.reduce_max(
                out=b2t[:, h:h + 1], in_=nrm_ps[:], axis=AXX),
              waits=((s_ape, nrm_k),), inc=s_adve)
            if i == 0:
                P("vector", lambda e, h=h: e.tensor_copy(
                    out=mk2run[:, h:h + 1], in_=b2t[:, h:h + 1]),
                  waits=((s_adve, CNT(s_adve)),), inc=s_adve)
            else:
                P("vector", lambda e, h=h: e.tensor_tensor(
                    out=mk2run[:, h:h + 1], in0=mk2run[:, h:h + 1],
                    in1=b2t[:, h:h + 1], op=ALU.max),
                  waits=((s_adve, CNT(s_adve)),), inc=s_adve)
        perhead_done[i] = CNT(s_aact)
        qspl[i] = ((s_qs[0], CNT(s_qs[0])), (s_qs[1], CNT(s_qs[1])),
                   (s_ks[0], CNT(s_ks[0])), (s_ks[1], CNT(s_ks[1])))
        P("vector", lambda e: e.tensor_tensor(out=b2t[:], in0=mq2[:], in1=mk2run[:],
                                              op=ALU.mult),
          waits=((s_adve, CNT(s_adve)),), inc=s_adve)
        dve_b2 = CNT(s_adve)
        P("scalar", lambda e: e.activation(out=b2t[:], in_=b2t[:], func=AF.Ln),
          waits=((s_adve, dve_b2),))
        P("scalar", lambda e: e.activation(out=b2t[:], in_=b2t[:], func=AF.Exp, scale=0.5),
          inc=s_aact)
        act_b = CNT(s_aact)
        P("vector", lambda e, i=i: e.tensor_scalar(
            out=negB[:, i:i + (H - 1) * NQT + 1:NQT], in0=b2t[:], scalar1=-1.0, scalar2=None,
            op0=ALU.mult),
          waits=((s_aact, act_b),), inc=s_adve)
        for tt in range(4):
            gtt = i * 4 + tt
            vbank = (tt % 2) * 2
            if tt < 2:
                wv = ((s_aact, qk_evac2[(i, 12 + 2 * tt)]),
                      (s_aact, qk_evac2[(i, 13 + 2 * tt)]),
                      (s_atev, tev_done))
            else:
                wv = ((s_aact, v_evac.get(gtt - 2)), (s_atev, tev_done))
            for half in range(2):
                for cc in range(8):
                    P("tensor", lambda e, vbank=vbank, half=half, cc=cc, tt=tt: e.matmul(
                        out=v_ps[:, (vbank + half) * 512:(vbank + half + 1) * 512],
                        lhsT=xhatT[:, cc * 512 + tt * 128:cc * 512 + tt * 128 + 128],
                        rhs=wqkv_sb[:, cc * 3072 + 2048 + half * 512:cc * 3072 + 2048 + (half + 1) * 512],
                        start=(cc == 0), stop=(cc == 7)),
                      waits=wv,
                      inc=(s_ape if cc == 7 else None))
            pe_v = CNT(s_ape)
            buf = tt % 2
            P("scalar", lambda e, buf=buf, vbank=vbank: e.activation(
                out=vbuf[:, buf * 1024:(buf + 1) * 1024],
                in_=v_ps[:, vbank * 512:(vbank + 2) * 512], func=AF.Copy),
              waits=((s_ape, pe_v), (s_vs[tt % 2], CNT(s_vs[tt % 2]))),
              inc=s_aact)
            v_evac[gtt] = CNT(s_aact)
            act_v = CNT(s_aact)
            prog.dma("gpsimd", lambda e, buf=buf, t0=t0, tt=tt: e.dma_start(
                out=v_scr[t0 + tt * 128:t0 + (tt + 1) * 128, :],
                in_=vbuf[:, buf * 1024:(buf + 1) * 1024]),
              s_vs[tt % 2], waits=((s_aact, act_v),))

    P("sync", lambda e: e.nop(),
      waits=tuple((s, CNT(s)) for s in (s_qs[0], s_qs[1], s_ks[0], s_ks[1], s_vs[0], s_vs[1])))
    barrier()
    prog.flush()
    es_a.close()
    es_ia.close()

    if "b" not in phases:
        prog.flush()
        es0.close()
        nc.finalize()
        return nc

    s_kv = [S("kv0"), S("kv1")]
    s_qq = [S("qq0"), S("qq1")]
    s_bs = S("bs")
    s_bexp = S("bexp")
    s_bo = S("bo")
    s_bmsk = S("bmsk")
    s_bla = S("bla")
    s_boe = S("boe")
    s_os = [S("os0"), S("os1")]
    s_ls = [S("ls0"), S("ls1")]
    s_bmld = S("bmld")

    es_b = ExitStack()
    masks_sb = SB(es_b, "masks_sb", [128, 4 * 512], BF16)
    k_sb = SB(es_b, "k_sb", [128, 2 * T], BF16)
    v_sb = SB(es_b, "v_sb", [128, 2 * T], BF16)
    q_sb = SB(es_b, "q_sb", [128, 2 * T], BF16)
    a_sb = SB(es_b, "a_sb", [128, 4 * 512], BF16)
    obf_sb = SB(es_b, "obf_sb", [128, 2 * 512], BF16)
    l_sb = SB(es_b, "l_sb", [1, 2 * 512], F32)
    s_ps = PS(es_b, "s_ps", [128, 2048], F32)
    o_ps = PS(es_b, "o_ps", [128, 1024], F32)
    l_ps = PS(es_b, "l_ps", [128, 2 * 512], F32)

    prog.dma("sync", lambda e: e.dma_start(out=masks_sb[:], in_=masks_d), s_bmld)

    s_end = {}
    o_end = {}
    oe_evt = {}
    l_free = {}
    bank_exp = {0: 0, 1: 0, 2: 0, 3: 0}
    slot_o = {0: 0, 1: 0, 2: 0, 3: 0}

    for h in range(H):
        hb = (h % 2) * T
        w_prev = o_end.get((h - 2) * NQT + NQT - 1) if h >= 2 else None
        prog.dma("gpsimd", lambda e, h=h, hb=hb: e.dma_start(
            out=k_sb[:, hb:hb + T], in_=k_scr[h, :, :]),
          s_kv[h % 2], waits=((s_bo, w_prev),))
        prog.dma("gpsimd", lambda e, h=h, hb=hb: e.dma_start(
            out=v_sb[:, hb:hb + T].rearrange("p (c d) -> p c d", d=128),
            in_=v_scr[:, h * 128:(h + 1) * 128].rearrange("(c p) d -> p c d", p=128)),
          s_kv[h % 2], waits=((s_bo, w_prev),))
        kv_ld = CNT(s_kv[h % 2])
        prog.dma("sync", lambda e, h=h, hb=hb: e.dma_start(
            out=q_sb[:, hb:hb + T], in_=q_scr[h, :, :]),
          s_qq[h % 2], waits=((s_bo, w_prev),))
        q_ld = CNT(s_qq[h % 2])
        for qt in range(NQT):
            pi = h * NQT + qt
            nck = 4 * (qt + 1)
            qb = hb + qt * 512
            col = h * NQT + qt
            pending = []

            def flush_o():
                for c2, exp_v, msk_v in pending:
                    sl = c2 % 4
                    P("tensor", lambda e, c2=c2, pi=pi, hb=hb, sl=sl, nck=nck: e.matmul(
                        out=o_ps[:, (pi % 2) * 512:(pi % 2 + 1) * 512],
                        lhsT=v_sb[:, hb + c2 * 128:hb + (c2 + 1) * 128],
                        rhs=a_sb[:, sl * 512:(sl + 1) * 512],
                        start=(c2 == 0), stop=(c2 == nck - 1)),
                      waits=((s_bexp, exp_v), (s_bmsk, msk_v),
                             (s_boe, oe_evt.get(pi - 2) if c2 == 0 else None)),
                      inc=s_bo)
                    P("tensor", lambda e, c2=c2, pi=pi, sl=sl, nck=nck: e.matmul(
                        out=l_ps[:, (pi % 2) * 512:(pi % 2 + 1) * 512],
                        lhsT=onesb_sb[:],
                        rhs=a_sb[:, sl * 512:(sl + 1) * 512],
                        start=(c2 == 0), stop=(c2 == nck - 1)),
                      waits=((s_bla, l_free.get(pi - 2) if c2 == 0 else None),),
                      inc=s_bo)
                    slot_o[sl] = CNT(s_bo)
                pending.clear()

            for c in range(nck):
                bank = c % 4
                P("tensor", lambda e, c=c, hb=hb, qb=qb, bank=bank: e.matmul(
                    out=s_ps[:, bank * 512:(bank + 1) * 512],
                    lhsT=k_sb[:, hb + c * 128:hb + (c + 1) * 128],
                    rhs=q_sb[:, qb:qb + 512], start=True, stop=True),
                  waits=((s_kv[h % 2], kv_ld), (s_qq[h % 2], q_ld), (s_bexp, bank_exp[bank])),
                  inc=s_bs)
                if c % 2 == 1:
                    flush_o()
                    slot = (c - 1) % 4
                    P("scalar", lambda e, slot=slot, col=col: e.activation(
                        out=a_sb[:, slot * 512:(slot + 2) * 512],
                        in_=s_ps[:, slot * 512:(slot + 2) * 512],
                        func=AF.Exp, bias=negB[:, col:col + 1]),
                      waits=((s_bs, CNT(s_bs)),
                             (s_bo, max(slot_o[slot], slot_o[slot + 1]))),
                      inc=s_bexp)
                    bank_exp[slot] = CNT(s_bexp)
                    bank_exp[slot + 1] = CNT(s_bexp)
                    exp_now = CNT(s_bexp)
                    for c2 in (c - 1, c):
                        sl = c2 % 4
                        diag = c2 >= nck - 4
                        if diag:
                            jm = c2 - (nck - 4)
                            P("vector", lambda e, sl=sl, jm=jm: e.tensor_tensor(
                                out=a_sb[:, sl * 512:(sl + 1) * 512],
                                in0=a_sb[:, sl * 512:(sl + 1) * 512],
                                in1=masks_sb[:, jm * 512:(jm + 1) * 512], op=ALU.mult),
                              waits=((s_bexp, exp_now), (s_bmld, 16)), inc=s_bmsk)
                        pending.append((c2, exp_now, CNT(s_bmsk)))
            flush_o()
            s_end[pi] = CNT(s_bs)
            o_end[pi] = CNT(s_bo)
            P("scalar", lambda e, pi=pi: e.activation(
                out=l_sb[0:1, (pi % 2) * 512:(pi % 2 + 1) * 512],
                in_=l_ps[0:1, (pi % 2) * 512:(pi % 2 + 1) * 512],
                func=AF.Copy),
              waits=((s_bo, o_end[pi]), (s_ls[pi % 2], CNT(s_ls[pi % 2]))),
              inc=s_bla)
            l_free[pi] = CNT(s_bla)
            prog.dma("sync", lambda e, h=h, qt=qt, pi=pi: e.dma_start(
                out=l_scr[h, qt * 512:(qt + 1) * 512].rearrange("(o f) -> o f", o=1),
                in_=l_sb[0:1, (pi % 2) * 512:(pi % 2 + 1) * 512]),
              s_ls[pi % 2], waits=((s_bla, CNT(s_bla)),))
            P("vector", lambda e, pi=pi: e.tensor_copy(
                out=obf_sb[:, (pi % 2) * 512:(pi % 2 + 1) * 512],
                in_=o_ps[:, (pi % 2) * 512:(pi % 2 + 1) * 512]),
              waits=((s_bo, o_end[pi]), (s_os[pi % 2], CNT(s_os[pi % 2]))),
              inc=s_boe)
            oe_evt[pi] = CNT(s_boe)
            prog.dma("sync", lambda e, h=h, qt=qt, pi=pi: e.dma_start(
                out=o_scr[h * 128:(h + 1) * 128, qt * 512:(qt + 1) * 512],
                in_=obf_sb[:, (pi % 2) * 512:(pi % 2 + 1) * 512]),
              s_os[pi % 2], waits=((s_boe, CNT(s_boe)),))

    P("sync", lambda e: e.nop(),
      waits=tuple((s, CNT(s)) for s in (s_os[0], s_os[1], s_ls[0], s_ls[1])))
    barrier()
    prog.flush()
    es_b.close()

    if "c" not in phases:
        prog.flush()
        es0.close()
        nc.finalize()
        return nc

    s_co = [S("co0"), S("co1")]
    s_cx2 = [S("cx20"), S("cx21")]
    s_cl = S("cl")
    s_cpe = S("cpe")
    s_cact = S("cact")
    s_cdve = S("cdve")
    s_cpet = S("cpet")
    s_ctev = S("ctev")
    s_ost = [S("cst0"), S("cst1")]
    s_carry = S("carry0")
    s_csel = S("csel")
    s_cpool = S("cpool")

    _pcut = int(os.environ.get("P2_CUT", "9"))
    _psub = int(os.environ.get("P2_SUB", "9"))
    es_c = ExitStack()
    sel8_sb = SB(es_c, "sel8_sb", [8, 8 * 128], BF16)
    ot_sb = SB(es_c, "ot_sb", [128, 2 * 4096], BF16)
    linv_sb = SB(es_c, "linv_sb", [8, 2 * 512], F32)
    linvh_sb = SB(es_c, "linvh_sb", [8, 2 * 512], BF16)
    linvl_sb = SB(es_c, "linvl_sb", [8, 2 * 512], BF16)
    x2t = SB(es_c, "x2t", [128, 2 * 4096], F32)
    anrm_sb = SB(es_c, "anrm_sb", [128, 1024], F32)
    araw_sb = SB(es_c, "araw_sb", [128, 2 * 1024], F32)
    xh2_bf = SB(es_c, "xh2_bf", [128, 4096], BF16)
    xh2T = SB(es_c, "xh2T", [128, 4096], BF16)
    xrT = SB(es_c, "xrT", [128, 4096], BF16)
    xkT = SB(es_c, "xkT", [128, 4096], BF16)
    carry = SB(es_c, "carry", [128, 8], BF16)
    r_sb = SB(es_c, "r_sb", [128, 1024], F32)
    krel_sb = SB(es_c, "krel_sb", [128, 2 * 512], F32)
    kact_sb = SB(es_c, "kact_sb", [128, 4096], BF16)
    ffn_sb = SB(es_c, "ffn_sb", [128, 4096], F32)
    bc_ps = PS(es_c, "bc_ps", [128, 1024], F32)
    big_ps = PS(es_c, "big_ps", [128, 2048], F32)
    tp2_ps = PS(es_c, "tp2_ps", [128, 2048], BF16)

    P("gpsimd", lambda e: e.memset(carry[:], 0.0), inc=s_carry)
    prog.dma("sync", lambda e: e.dma_start(out=sel8_sb[:], in_=sel8_d), s_csel)

    ot_free = {}
    x_free = {}
    l_free = {}
    anrm_evt = {}
    sig_evt = {}
    ffn_evt = {}
    relu_evt = {}
    sq_evt = {}
    rmul_evt = {}
    linv_bf = {}
    bcast_done = {}
    ffn_pe_done = {}
    ln2_st8 = {}
    ln2_st24 = {}
    ln3_st12 = {}
    ln3_st28 = {}
    ln4_var_free = [None]
    ln4_rstd_free = [None]
    mixp_hist = {}
    store_done = {}
    nblk2 = [0]

    def emit_linv(j):
        lb = (j % 2) * 512
        prog.dma("sync", lambda e, j=j, lb=lb: e.dma_start(
            out=linv_sb[:, lb:lb + 512], in_=l_scr[:, j * 512:(j + 1) * 512]),
          s_cl, waits=((s_cdve, linv_bf.get(j - 2)),))
        cl_ld = CNT(s_cl)
        P("scalar", lambda e, lb=lb: e.activation(
            out=linv_sb[:, lb:lb + 512], in_=linv_sb[0:8, lb:lb + 512], func=AF.Ln),
          waits=((s_cl, cl_ld),))
        P("scalar", lambda e, lb=lb: e.activation(
            out=linv_sb[:, lb:lb + 512], in_=linv_sb[:, lb:lb + 512],
            func=AF.Exp, scale=-1.0), inc=s_cact)
        l_free[j] = CNT(s_cact)
        P("vector", lambda e, lb=lb: e.tensor_copy(
            out=linvh_sb[:, lb:lb + 512], in_=linv_sb[:, lb:lb + 512]),
          waits=((s_cact, l_free[j]), (s_cpe, bcast_done.get(j - 2))), inc=s_cdve)
        P("vector", lambda e, lb=lb: e.tensor_tensor(
            out=linvl_sb[:, lb:lb + 512], in0=linv_sb[:, lb:lb + 512],
            in1=linvh_sb[:, lb:lb + 512], op=ALU.subtract),
          inc=s_cdve)
        linv_bf[j] = CNT(s_cdve)

    for i in range(NT):
        t0 = i * 512
        ob = i % 2
        xb = i % 2
        lb = (i % 2) * 512
        prog.dma("sync", lambda e, t0=t0, ob=ob: e.dma_start(
            out=ot_sb[:, ob * 4096:(ob + 1) * 4096].rearrange("p (c f) -> p c f", c=8),
            in_=o_scr[:, t0:t0 + 512].rearrange("(c p) f -> p c f", p=128)),
          s_co[i % 2], waits=((s_cpe, ot_free.get(i - 2)),))
        co_ld = CNT(s_co[i % 2])
        prog.dma("sync", lambda e, t0=t0, xb=xb: e.dma_start(
            out=x2t[:, xb * 4096:(xb + 1) * 4096].rearrange("p (c f) -> p c f", c=4),
            in_=x_in[t0:t0 + 512, :].rearrange("(c p) f -> p c f", p=128)),
          s_cx2[i % 2], waits=((s_cdve, x_free.get(i - 2)),) + store_done.get(i - 2, ()))
        cx_ld = CNT(s_cx2[i % 2])
        if i == 0:
            emit_linv(0)
        onrm_evts = []
        for cc in range(8):
            P("tensor", lambda e, cc=cc, lb=lb: e.matmul(
                out=bc_ps[:, (cc % 2) * 512:(cc % 2 + 1) * 512],
                lhsT=sel8_sb[:, cc * 128:(cc + 1) * 128], rhs=linvh_sb[0:8, lb:lb + 512],
                start=True, stop=False),
              waits=((s_cdve, linv_bf[i]), (s_csel, 16),
                     (s_cact, relu_evt.get((i - 1, 7))),
                     (s_cdve, onrm_evts[cc - 2] if cc >= 2 else None)))
            P("tensor", lambda e, cc=cc, lb=lb: e.matmul(
                out=bc_ps[:, (cc % 2) * 512:(cc % 2 + 1) * 512],
                lhsT=sel8_sb[:, cc * 128:(cc + 1) * 128], rhs=linvl_sb[0:8, lb:lb + 512],
                start=False, stop=True),
              inc=s_cpe)
            P("vector", lambda e, cc=cc, ob=ob: e.tensor_tensor(
                out=ot_sb[:, ob * 4096 + cc * 512:ob * 4096 + (cc + 1) * 512],
                in0=ot_sb[:, ob * 4096 + cc * 512:ob * 4096 + (cc + 1) * 512],
                in1=bc_ps[:, (cc % 2) * 512:(cc % 2 + 1) * 512], op=ALU.mult),
              waits=((s_cpe, CNT(s_cpe)), (s_co[i % 2], co_ld), (s_cx2[i % 2], cx_ld)),
              inc=s_cdve)
            onrm_evts.append(CNT(s_cdve))
        onrm_done = CNT(s_cdve)
        bcast_done[i] = CNT(s_cpe)
        if i + 1 < NT:
            emit_linv(i + 1)
        x1_evt = {}
        cast_evt2 = {}
        for tt in range(4):
            for half in range(2):
                bank = (tt % 2) * 2 + half
                for cc in range(8):
                    P("tensor", lambda e, bank=bank, cc=cc, tt=tt, half=half, ob=ob: e.matmul(
                        out=big_ps[:, bank * 512:(bank + 1) * 512],
                        lhsT=ot_sb[:, ob * 4096 + cc * 512 + tt * 128:ob * 4096 + cc * 512 + tt * 128 + 128],
                        rhs=wout_sb[:, cc * 1024 + half * 512:cc * 1024 + (half + 1) * 512],
                        start=(cc == 0), stop=(cc == 7)),
                      waits=((s_cdve, onrm_done),
                             (s_cdve, anrm_evt.get((i, tt - 2))),
                             (s_cdve, ffn_evt.get((i - 1, tt + 2)) if tt < 2 else None)),
                      inc=(s_cpe if cc == 7 else None))
            pe_ao = CNT(s_cpe)
            if tt == 3:
                ot_free[i] = pe_ao
            bank0 = (tt % 2) * 2
            ab = tt % 2
            P("vector", lambda e, bank0=bank0, ab=ab: e.tensor_scalar_mul(
                out=araw_sb[:, ab * 1024:(ab + 1) * 1024],
                in0=big_ps[:, bank0 * 512:(bank0 + 2) * 512], scalar1=1.0),
              waits=((s_cpe, pe_ao),), inc=s_cdve)
            anrm_evt[(i, tt)] = CNT(s_cdve)
            ev_done = CNT(s_cdve)
            P("vector", lambda e, ab=ab, tt=tt: e.reduce_sum(
                out=stat[:, tt:tt + 1], in_=araw_sb[:, ab * 1024:(ab + 1) * 1024],
                axis=AXX),
              inc=s_cdve)
            P("scalar", lambda e, ab=ab, tt=tt: e.activation(
                out=junk_a[:], in_=araw_sb[:, ab * 1024:(ab + 1) * 1024],
                func=AF.Square, accum_out=stat[:, 8 + tt:9 + tt]),
              waits=((s_cdve, ev_done), (s_cdve, ln2_st8.get(tt))), inc=s_cact)
            asq = CNT(s_cact)
            P("vector", lambda e, tt=tt: e.tensor_scalar(
                out=stat[:, 16 + tt:17 + tt], in0=stat[:, tt:tt + 1],
                scalar1=1.0 / C, scalar2=None, op0=ALU.mult),
              waits=((s_cdve, CNT(s_cdve)),), inc=s_cdve)
            P("vector", lambda e, tt=tt: e.tensor_tensor(
                out=stat[:, 24 + tt:25 + tt], in0=stat[:, 16 + tt:17 + tt],
                in1=stat[:, 16 + tt:17 + tt], op=ALU.mult),
              waits=((s_cdve, CNT(s_cdve)),), inc=s_cdve)
            P("vector", lambda e, tt=tt: e.scalar_tensor_tensor(
                out=stat[:, 24 + tt:25 + tt], in0=stat[:, 8 + tt:9 + tt], scalar=1.0 / C,
                in1=stat[:, 24 + tt:25 + tt], op0=ALU.mult, op1=ALU.subtract),
              waits=((s_cact, asq), (s_cdve, CNT(s_cdve))), inc=s_cdve)
            dv = CNT(s_cdve)
            ln2_st8[tt] = dv
            P("scalar", lambda e, tt=tt: e.activation(
                out=stat[:, 24 + tt:25 + tt], in_=stat[:, 24 + tt:25 + tt],
                func=AF.Ln, bias=scal_sb[:, 0:1]),
              waits=((s_cdve, dv), (s_cdve, ln2_st24.get(tt))))
            P("scalar", lambda e, tt=tt: e.activation(
                out=stat[:, 24 + tt:25 + tt], in_=stat[:, 24 + tt:25 + tt],
                func=AF.Exp, scale=-0.5), inc=s_cact)
            ar = CNT(s_cact)
            P("vector", lambda e, ab=ab, tt=tt: e.tensor_scalar(
                out=anrm_sb[:], in0=araw_sb[:, ab * 1024:(ab + 1) * 1024],
                scalar1=stat[:, 16 + tt:17 + tt], scalar2=stat[:, 24 + tt:25 + tt],
                op0=ALU.subtract, op1=ALU.mult),
              waits=((s_cact, ar),), inc=s_cdve)
            ln2_st24[tt] = CNT(s_cdve)
            P("vector", lambda e, tt=tt, xb=xb: e.tensor_tensor(
                out=x2t[:, xb * 4096 + tt * 1024:xb * 4096 + (tt + 1) * 1024],
                in0=anrm_sb[:],
                in1=x2t[:, xb * 4096 + tt * 1024:xb * 4096 + (tt + 1) * 1024], op=ALU.add),
              inc=s_cdve)
            x1_evt[tt] = CNT(s_cdve)
            P("vector", lambda e, tt=tt, xb=xb: e.reduce_sum(
                out=stat[:, 4 + tt:5 + tt],
                in_=x2t[:, xb * 4096 + tt * 1024:xb * 4096 + (tt + 1) * 1024], axis=AXX),
              inc=s_cdve)
            P("scalar", lambda e, tt=tt, xb=xb: e.activation(
                out=junk_a[:], in_=x2t[:, xb * 4096 + tt * 1024:xb * 4096 + (tt + 1) * 1024],
                func=AF.Square, accum_out=stat[:, 12 + tt:13 + tt]),
              waits=((s_cdve, x1_evt[tt]), (s_cdve, ln3_st12.get(tt))), inc=s_cact)
            asq3 = CNT(s_cact)
            P("vector", lambda e, tt=tt: e.tensor_scalar(
                out=stat[:, 20 + tt:21 + tt], in0=stat[:, 4 + tt:5 + tt],
                scalar1=1.0 / C, scalar2=None, op0=ALU.mult),
              waits=((s_cdve, CNT(s_cdve)),), inc=s_cdve)
            P("vector", lambda e, tt=tt: e.tensor_tensor(
                out=stat[:, 28 + tt:29 + tt], in0=stat[:, 20 + tt:21 + tt],
                in1=stat[:, 20 + tt:21 + tt], op=ALU.mult),
              waits=((s_cdve, CNT(s_cdve)),), inc=s_cdve)
            P("vector", lambda e, tt=tt: e.scalar_tensor_tensor(
                out=stat[:, 28 + tt:29 + tt], in0=stat[:, 12 + tt:13 + tt], scalar=1.0 / C,
                in1=stat[:, 28 + tt:29 + tt], op0=ALU.mult, op1=ALU.subtract),
              waits=((s_cact, asq3), (s_cdve, CNT(s_cdve))), inc=s_cdve)
            dv3 = CNT(s_cdve)
            ln3_st12[tt] = dv3
            P("scalar", lambda e, tt=tt: e.activation(
                out=stat[:, 28 + tt:29 + tt], in_=stat[:, 28 + tt:29 + tt],
                func=AF.Ln, bias=scal_sb[:, 0:1]),
              waits=((s_cdve, dv3), (s_cdve, ln3_st28.get(tt))))
            P("scalar", lambda e, tt=tt: e.activation(
                out=stat[:, 28 + tt:29 + tt], in_=stat[:, 28 + tt:29 + tt],
                func=AF.Exp, scale=-0.5), inc=s_cact)
            ar3 = CNT(s_cact)
            P("vector", lambda e, tt=tt, xb=xb: e.tensor_scalar(
                out=xh2_bf[:, tt * 1024:(tt + 1) * 1024],
                in0=x2t[:, xb * 4096 + tt * 1024:xb * 4096 + (tt + 1) * 1024],
                scalar1=stat[:, 20 + tt:21 + tt], scalar2=stat[:, 28 + tt:29 + tt],
                op0=ALU.subtract, op1=ALU.mult),
              waits=((s_cact, ar3),), inc=s_cdve)
            ln3_st28[tt] = CNT(s_cdve)
            cast_evt2[tt] = CNT(s_cdve)
        for tt in range(4):
            gg = i * 4 + tt
            tb = (gg % 2) * 1024
            for s in range(8):
                cc = s
                P("tensor", lambda e, tt=tt, cc=cc, s=s, tb=tb: e.transpose(
                    out=tp2_ps[:, tb + s * 128:tb + (s + 1) * 128],
                    in_=xh2_bf[:, tt * 1024 + cc * 128:tt * 1024 + (cc + 1) * 128],
                    identity=eye_sb[:]),
                  waits=((s_cdve, cast_evt2[tt]), (s_ctev, 8 * (gg - 1))),
                  inc=s_cpet)
            tr_done2 = CNT(s_cpet)
            for s in range(8):
                cc = s
                P("vector", lambda e, tt=tt, cc=cc, s=s, tb=tb: e.tensor_copy(
                    out=xh2T[:, cc * 512 + tt * 128:cc * 512 + tt * 128 + 128],
                    in_=tp2_ps[:, tb + s * 128:tb + (s + 1) * 128]),
                  waits=((s_cpet, tr_done2), (s_cpool, mixp_hist.get(i - 1))),
                  inc=s_ctev)
        if _pcut < 4:
            continue
        tev2_done = CNT(s_ctev)
        for cc in range(8):
            s0 = cc * 512
            w0 = ((s_carry, 1),) if i == 0 and cc == 0 else ()
            P("vector", lambda e, s0=s0: e.tensor_tensor(
                out=xrT[:, s0 + 1:s0 + 512], in0=xh2T[:, s0 + 1:s0 + 512],
                in1=xh2T[:, s0:s0 + 511], op=ALU.subtract), waits=w0)
            P("vector", lambda e, s0=s0, cc=cc: e.tensor_tensor(
                out=xrT[:, s0:s0 + 1], in0=xh2T[:, s0:s0 + 1],
                in1=carry[:, cc:cc + 1], op=ALU.subtract))
            P("vector", lambda e, s0=s0, cc=cc: e.scalar_tensor_tensor(
                out=xkT[:, s0 + 1:s0 + 512], in0=xrT[:, s0 + 1:s0 + 512],
                scalar=mu_sb[:, 8 + cc:9 + cc], in1=xh2T[:, s0:s0 + 511],
                op0=ALU.mult, op1=ALU.add))
            P("vector", lambda e, s0=s0, cc=cc: e.scalar_tensor_tensor(
                out=xkT[:, s0:s0 + 1], in0=xrT[:, s0:s0 + 1],
                scalar=mu_sb[:, 8 + cc:9 + cc], in1=carry[:, cc:cc + 1],
                op0=ALU.mult, op1=ALU.add),
              inc=s_cdve)
        mixk_done = CNT(s_cdve)
        for cc in range(8):
            s0 = cc * 512
            P("vector", lambda e, s0=s0, cc=cc: e.scalar_tensor_tensor(
                out=xrT[:, s0 + 1:s0 + 512], in0=xrT[:, s0 + 1:s0 + 512],
                scalar=mu_sb[:, cc:cc + 1], in1=xh2T[:, s0:s0 + 511],
                op0=ALU.mult, op1=ALU.add))
            P("vector", lambda e, s0=s0, cc=cc: e.scalar_tensor_tensor(
                out=xrT[:, s0:s0 + 1], in0=xrT[:, s0:s0 + 1],
                scalar=mu_sb[:, cc:cc + 1], in1=carry[:, cc:cc + 1],
                op0=ALU.mult, op1=ALU.add))
            P("vector", lambda e, s0=s0, cc=cc: e.tensor_copy(
                out=carry[:, cc:cc + 1], in_=xh2T[:, s0 + 511:s0 + 512]),
              inc=s_cdve)
        mix_done = CNT(s_cdve)
        mixp_done = CNT(s_cpool)
        mixp_hist[i] = mixp_done
        if _pcut < 5:
            continue
        for dd in range(8):
            bank = dd % 2
            for cc in range(8):
                P("tensor", lambda e, bank=bank, cc=cc, dd=dd: e.matmul(
                    out=bc_ps[:, bank * 512:(bank + 1) * 512],
                    lhsT=wkT_sb[:, cc * 1024 + dd * 128:cc * 1024 + (dd + 1) * 128],
                    rhs=xkT[:, cc * 512:(cc + 1) * 512],
                    start=(cc == 0), stop=(cc == 7)),
                  waits=((s_cdve, mixk_done),
                         (s_cact, relu_evt.get((i, dd - 2)))),
                  inc=(s_cpe if cc == 7 else None))
            pe_k = CNT(s_cpe)
            P("scalar", lambda e, bank=bank, dd=dd: e.activation(
                out=krel_sb[:, (dd % 2) * 512:(dd % 2 + 1) * 512],
                in_=bc_ps[:, bank * 512:(bank + 1) * 512], func=AF.Relu),
              waits=((s_cpe, pe_k), (s_cdve, sq_evt.get((i, dd - 2)))),
              inc=s_cact)
            relu_evt[(i, dd)] = CNT(s_cact)
            P("vector", lambda e, dd=dd: e.tensor_tensor(
                out=kact_sb[:, dd * 512:(dd + 1) * 512],
                in0=krel_sb[:, (dd % 2) * 512:(dd % 2 + 1) * 512],
                in1=krel_sb[:, (dd % 2) * 512:(dd % 2 + 1) * 512], op=ALU.mult),
              waits=((s_cact, relu_evt[(i, dd)]),), inc=s_cdve)
            sq_evt[(i, dd)] = CNT(s_cdve)
        kact_done = CNT(s_cdve)
        if _pcut < 6:
            continue
        pe_r_evt = {}
        pe_v_evt = {}

        def emit_rmm(tt, i=i, mix_done=mix_done):
            for half in range(2):
                bank = (tt % 2) * 2 + half
                for cc in range(8):
                    P("tensor", lambda e, bank=bank, cc=cc, tt=tt, half=half: e.matmul(
                        out=big_ps[:, bank * 512:(bank + 1) * 512],
                        lhsT=xrT[:, cc * 512 + tt * 128:cc * 512 + tt * 128 + 128],
                        rhs=wrT_sb[:, cc * 1024 + half * 512:cc * 1024 + (half + 1) * 512],
                        start=(cc == 0), stop=(cc == 7)),
                      waits=((s_cdve, mix_done),
                             (s_cdve, anrm_evt.get((i, tt + 2)) if tt < 2 else None),
                             (s_cdve, ffn_evt.get((i, tt - 2)) if tt >= 2 else None)),
                      inc=(s_cpe if cc == 7 else None))
            pe_r_evt[tt] = CNT(s_cpe)

        def emit_sig(tt, i=i, kact_done=kact_done):
            bank0 = (tt % 2) * 2
            P("scalar", lambda e, bank0=bank0: e.activation(
                out=r_sb[:], in_=big_ps[:, bank0 * 512:(bank0 + 2) * 512],
                func=AF.Exp, scale=-1.0),
              waits=((s_cpe, pe_r_evt[tt]),
                     (s_cdve, rmul_evt.get((i, tt - 1)))),
              inc=s_cact)
            sig_evt[(i, tt)] = CNT(s_cact)
            P("scalar", lambda e: e.activation(
                out=r_sb[:], in_=r_sb[:], func=AF.Ln, bias=scal_sb[:, 1:2]))
            P("scalar", lambda e: e.activation(
                out=r_sb[:], in_=r_sb[:], func=AF.Exp, scale=-1.0), inc=s_cact)
            sig_evt[(i, tt, "r")] = CNT(s_cact)

        def emit_vmm(tt, i=i, kact_done=kact_done):
            for half in range(2):
                bank = (tt % 2) * 2 + half
                for cc in range(8):
                    P("tensor", lambda e, bank=bank, cc=cc, tt=tt, half=half: e.matmul(
                        out=big_ps[:, bank * 512:(bank + 1) * 512],
                        lhsT=kact_sb[:, cc * 512 + tt * 128:cc * 512 + tt * 128 + 128],
                        rhs=wvT_sb[:, cc * 1024 + half * 512:cc * 1024 + (half + 1) * 512],
                        start=(cc == 0), stop=(cc == 7)),
                      waits=((s_cdve, kact_done), (s_cact, sig_evt[(i, tt)]),),
                      inc=(s_cpe if cc == 7 else None))
            pe_v_evt[tt] = CNT(s_cpe)

        def emit_ffnmul(tt, i=i):
            bank0 = (tt % 2) * 2
            P("vector", lambda e, tt=tt, bank0=bank0: e.tensor_tensor(
                out=ffn_sb[:, tt * 1024:(tt + 1) * 1024], in0=r_sb[:],
                in1=big_ps[:, bank0 * 512:(bank0 + 2) * 512], op=ALU.mult),
              waits=((s_cpe, pe_v_evt[tt]), (s_cact, sig_evt[(i, tt, "r")])),
              inc=s_cdve)
            ffn_evt[(i, tt)] = CNT(s_cdve)
            rmul_evt[(i, tt)] = ffn_evt[(i, tt)]

        emit_rmm(0)
        emit_sig(0)
        emit_rmm(1)
        emit_vmm(0)
        emit_ffnmul(0)
        emit_sig(1)
        emit_vmm(1)
        emit_ffnmul(1)
        emit_rmm(2)
        emit_sig(2)
        emit_rmm(3)
        emit_vmm(2)
        emit_ffnmul(2)
        emit_sig(3)
        emit_vmm(3)
        emit_ffnmul(3)
        ffn_pe_done[i] = CNT(s_cpe)
        if _pcut < 7:
            continue
        for tt in range(4):
            P("vector", lambda e, tt=tt: e.reduce_sum(
                out=stat[:, 32 + tt:33 + tt], in_=ffn_sb[:, tt * 1024:(tt + 1) * 1024],
                axis=AXX),
              waits=((s_cdve, ffn_evt[(i, tt)]),), inc=s_cdve)
            P("scalar", lambda e, tt=tt: e.activation(
                out=junk_a[:], in_=ffn_sb[:, tt * 1024:(tt + 1) * 1024],
                func=AF.Square, accum_out=stat[:, 36 + tt:37 + tt]),
              waits=((s_cdve, ffn_evt[(i, tt)]), (s_cdve, ln4_var_free[0])), inc=s_cact)
        asq3 = CNT(s_cact)
        P("vector", lambda e: e.tensor_scalar(
            out=stat[:, 40:44], in0=stat[:, 32:36], scalar1=1.0 / C, scalar2=None,
            op0=ALU.mult),
          waits=((s_cdve, CNT(s_cdve)),), inc=s_cdve)
        P("vector", lambda e: e.tensor_tensor(
            out=stat[:, 44:48], in0=stat[:, 40:44], in1=stat[:, 40:44], op=ALU.mult),
          waits=((s_cdve, CNT(s_cdve)),), inc=s_cdve)
        P("vector", lambda e: e.scalar_tensor_tensor(
            out=stat[:, 44:48], in0=stat[:, 36:40], scalar=1.0 / C, in1=stat[:, 44:48],
            op0=ALU.mult, op1=ALU.subtract),
          waits=((s_cact, asq3), (s_cdve, CNT(s_cdve))), inc=s_cdve)
        dv3 = CNT(s_cdve)
        ln4_var_free[0] = dv3
        P("scalar", lambda e: e.activation(out=stat[:, 44:48], in_=stat[:, 44:48],
                                           func=AF.Ln, bias=scal_sb[:, 0:1]),
          waits=((s_cdve, dv3), (s_cdve, ln4_rstd_free[0])))
        P("scalar", lambda e: e.activation(out=stat[:, 44:48], in_=stat[:, 44:48],
                                           func=AF.Exp, scale=-0.5), inc=s_cact)
        ar3 = CNT(s_cact)
        for tt in range(4):
            ost = (i * 4 + tt) % 2
            P("vector", lambda e, tt=tt: e.tensor_scalar(
                out=anrm_sb[:], in0=ffn_sb[:, tt * 1024:(tt + 1) * 1024],
                scalar1=stat[:, 40 + tt:41 + tt], scalar2=stat[:, 44 + tt:45 + tt],
                op0=ALU.subtract, op1=ALU.mult),
              waits=((s_cact, ar3),))
            P("vector", lambda e, tt=tt, xb=xb: e.tensor_tensor(
                out=x2t[:, xb * 4096 + tt * 1024:xb * 4096 + (tt + 1) * 1024],
                in0=anrm_sb[:],
                in1=x2t[:, xb * 4096 + tt * 1024:xb * 4096 + (tt + 1) * 1024], op=ALU.add),
              inc=s_cdve)
            od = CNT(s_cdve)
            prog.dma("gpsimd", lambda e, t0=t0, tt=tt, xb=xb: e.dma_start(
                out=out_d[t0 + tt * 128:t0 + (tt + 1) * 128, :],
                in_=x2t[:, xb * 4096 + tt * 1024:xb * 4096 + (tt + 1) * 1024]),
              s_ost[ost], waits=((s_cdve, od),))
        x_free[i] = CNT(s_cdve)
        ln4_rstd_free[0] = CNT(s_cdve)
        store_done[i] = ((s_ost[0], CNT(s_ost[0])), (s_ost[1], CNT(s_ost[1])))

    if _pcut >= 7:
        P("sync", lambda e: e.nop(),
          waits=((s_ost[0], CNT(s_ost[0])), (s_ost[1], CNT(s_ost[1]))))
    else:
        P("sync", lambda e: e.nop(), waits=((s_cdve, CNT(s_cdve)),))

    prog.flush()
    es_c.close()
    es0.close()
    nc.finalize()
    return nc


_NC_CACHE = {}


def get_nc(T):
    if T not in _NC_CACHE:
        _NC_CACHE[T] = build_nc(T)
    return _NC_CACHE[T]


def kernel(L0_attn_x, L0_attn_w_qkv, L0_attn_w_out, L0_attn_head_scales,
           L0_ffn_W_r, L0_ffn_W_k, L0_ffn_W_v, L0_ffn_mu_r, L0_ffn_mu_k):
    x = np.ascontiguousarray(np.asarray(L0_attn_x, dtype=np.float32))
    B, T, _ = x.shape
    nc = get_nc(T)
    consts = host_consts()
    base = {
        "w_qkv": np.ascontiguousarray(np.asarray(L0_attn_w_qkv, np.float32)),
        "w_out": np.ascontiguousarray(np.asarray(L0_attn_w_out, np.float32)),
        "head_scales": np.ascontiguousarray(
            np.asarray(L0_attn_head_scales, np.float32).reshape(1, 8)),
        "w_r": np.ascontiguousarray(np.asarray(L0_ffn_W_r, np.float32).T),
        "w_k": np.ascontiguousarray(np.asarray(L0_ffn_W_k, np.float32).T),
        "w_v": np.ascontiguousarray(np.asarray(L0_ffn_W_v, np.float32).T),
        "mu_r": np.ascontiguousarray(np.asarray(L0_ffn_mu_r, np.float32)),
        "mu_k": np.ascontiguousarray(np.asarray(L0_ffn_mu_k, np.float32)),
        "eye_bf": consts["eye_bf"], "ones_bf": consts["ones_bf"],
        "ones_f": consts["ones_f"], "sel8": consts["sel8"],
        "dirs33": consts["dirs33"], "d3s": consts["d3s"],
        "masks": consts["masks"], "scal": consts["scal"],
    }
    in_maps = [dict(base, x=np.ascontiguousarray(x[b])) for b in range(B)]
    res = run_bass_kernel_spmd(nc, in_maps, core_ids=list(range(B)))
    return np.stack([np.asarray(r["out"], np.float32) for r in res.results], axis=0)


# version 17
# speedup vs baseline: 1.0084x; 1.0084x over previous
import math
import os
import numpy as np
import ml_dtypes
from contextlib import ExitStack

import concourse.bass as bass
import concourse.mybir as mybir
from concourse.bass_utils import run_bass_kernel_spmd

F32 = mybir.dt.float32
BF16 = mybir.dt.bfloat16
AF = mybir.ActivationFunctionType
ALU = mybir.AluOpType
AXX = mybir.AxisListType.X

C = 1024
H = 8
SCALE = 1.0 / math.sqrt(128.0)
EPS = 1e-5

ENGINES = ("sync", "tensor", "scalar", "vector", "gpsimd")


class Prog:
    def __init__(self, nc):
        self.nc = nc
        self.q = {e: [] for e in ENGINES}
        self.sems = {}
        self.cnt = {}
        self.last_wait = {}

    def sem(self, name, es):
        h = es.enter_context(self.nc.semaphore(name))
        self.sems[name] = h
        self.cnt[name] = 0
        return name

    def op(self, eng, fn, waits=(), inc=None, incn=1):
        wl = []
        for s, v in waits:
            if v is None or v <= 0:
                continue
            k = (eng, s)
            if self.last_wait.get(k, -1) >= v:
                continue
            self.last_wait[k] = v
            wl.append((self.sems[s], v))
        if inc is not None:
            self.cnt[inc] += incn
        self.q[eng].append((fn, wl, self.sems[inc] if inc else None, incn))

    def dma(self, eng, fn, stream, waits=()):
        self.op(eng, fn, waits=tuple(waits) + ((stream, self.cnt[stream]),),
                inc=stream, incn=16)

    def emit(self, block):
        def make_runner(eng_name):
            def run(eng):
                for fn, wl, inc, incn in self.q[eng_name]:
                    for h, v in wl:
                        eng.wait_ge(h, v)
                    ins = fn(eng)
                    if inc is not None:
                        if ins is None:
                            ins = eng.nop()
                        ins.then_inc(inc, incn)
            return run
        block.sync(make_runner("sync"))
        block.tensor(make_runner("tensor"))
        block.scalar(make_runner("scalar"))
        block.vector(make_runner("vector"))
        block.gpsimd(make_runner("gpsimd"))

    def flush(self):
        if any(self.q.values()):
            with self.nc.Block() as block:
                self.emit(block)
        self.q = {e: [] for e in ENGINES}


def host_consts():
    dirs = np.array(
        [[(i >> 2 & 1) * 2 - 1, (i >> 1 & 1) * 2 - 1, (i & 1) * 2 - 1] for i in range(8)],
        dtype=np.float32) / np.float32(np.sqrt(3.0, dtype=np.float32))
    c = {}
    c["eye_bf"] = np.eye(128, dtype=np.float32).astype(ml_dtypes.bfloat16)
    c["ones_bf"] = np.ones((128, 128), dtype=ml_dtypes.bfloat16)
    c["ones_f"] = np.ones((128, 128), dtype=np.float32)
    sel = np.zeros((8, 8 * 128), dtype=np.float32)
    for cc in range(8):
        sel[cc, cc * 128:(cc + 1) * 128] = 1.0
    c["sel8"] = sel.astype(ml_dtypes.bfloat16)
    d33 = np.zeros((3, 24), dtype=np.float32)
    for h in range(8):
        for j in range(3):
            d33[:, 3 * h + j] = dirs[h]
    c["dirs33"] = d33.astype(ml_dtypes.bfloat16)
    c["d3s"] = np.ascontiguousarray((dirs.T * np.float32(np.sqrt(128.0))).astype(np.float32))
    m = np.zeros((128, 4 * 512), dtype=np.float32)
    kk = np.arange(128).reshape(128, 1)
    qq = np.arange(512).reshape(1, 512)
    for j in range(4):
        m[:, j * 512:(j + 1) * 512] = (128 * j + kk <= qq).astype(np.float32)
    c["masks"] = m.astype(ml_dtypes.bfloat16)
    sc = np.zeros((128, 8), dtype=np.float32)
    sc[:, 0] = EPS
    sc[:, 1] = 1.0
    c["scal"] = sc
    return c


def build_nc(T, phases="abc", dump=False):
    assert T % 512 == 0
    NT = T // 512
    NQT = NT
    nc = bass.Bass("TRN2", target_bir_lowering=False)

    x_in = nc.dram_tensor("x", [T, C], F32, kind="ExternalInput").ap()
    w_qkv = nc.dram_tensor("w_qkv", [C, 3 * C], F32, kind="ExternalInput").ap()
    w_out = nc.dram_tensor("w_out", [C, C], F32, kind="ExternalInput").ap()
    hs_in = nc.dram_tensor("head_scales", [1, H], F32, kind="ExternalInput").ap()
    w_r = nc.dram_tensor("w_r", [C, C], F32, kind="ExternalInput").ap()
    w_k = nc.dram_tensor("w_k", [C, C], F32, kind="ExternalInput").ap()
    w_v = nc.dram_tensor("w_v", [C, C], F32, kind="ExternalInput").ap()
    mu_r_in = nc.dram_tensor("mu_r", [C], F32, kind="ExternalInput").ap()
    mu_k_in = nc.dram_tensor("mu_k", [C], F32, kind="ExternalInput").ap()
    eye_d = nc.dram_tensor("eye_bf", [128, 128], BF16, kind="ExternalInput").ap()
    onesb_d = nc.dram_tensor("ones_bf", [128, 128], BF16, kind="ExternalInput").ap()
    onesf_d = nc.dram_tensor("ones_f", [128, 128], F32, kind="ExternalInput").ap()
    sel8_d = nc.dram_tensor("sel8", [8, 8 * 128], BF16, kind="ExternalInput").ap()
    dirs33_d = nc.dram_tensor("dirs33", [3, 24], BF16, kind="ExternalInput").ap()
    d3s_d = nc.dram_tensor("d3s", [3, 8], F32, kind="ExternalInput").ap()
    masks_d = nc.dram_tensor("masks", [128, 4 * 512], BF16, kind="ExternalInput").ap()
    scal_d = nc.dram_tensor("scal", [128, 8], F32, kind="ExternalInput").ap()
    out_d = nc.dram_tensor("out", [T, C], F32, kind="ExternalOutput").ap()
    skind = "ExternalOutput" if dump else "Internal"
    dbg = nc.dram_tensor("dbg", [128, 32], F32, kind="ExternalOutput").ap() if dump else None
    q_scr = nc.dram_tensor("q_scr", [H, 128, T], BF16, kind=skind).ap()
    k_scr = nc.dram_tensor("k_scr", [H, 128, T], BF16, kind=skind).ap()
    v_scr = nc.dram_tensor("v_scr", [T, C], BF16, kind=skind).ap()
    o_scr = nc.dram_tensor("o_scr", [C, T], BF16, kind=skind).ap()
    l_scr = nc.dram_tensor("l_scr", [H, T], F32, kind=skind).ap()

    es0 = ExitStack()

    def SB(stk, name, shape, dt):
        return stk.enter_context(nc.sbuf_tensor(name, shape, dt))

    def PS(stk, name, shape, dt):
        return stk.enter_context(nc.psum_tensor(name, shape, dt))

    eye_sb = SB(es0, "eye_sb", [128, 128], BF16)
    onesb_sb = SB(es0, "onesb_sb", [128, 128], BF16)
    onesf_sb = SB(es0, "onesf_sb", [128, 128], F32)
    scal_sb = SB(es0, "scal_sb", [128, 8], F32)
    mu_sb = SB(es0, "mu_sb", [128, 16], F32)
    negB = SB(es0, "negB", [128, H * NQT], F32)
    wout_sb = SB(es0, "wout_sb", [128, 8 * 1024], BF16)
    wrT_sb = SB(es0, "wrT_sb", [128, 8 * 1024], BF16)
    wkT_sb = SB(es0, "wkT_sb", [128, 8 * 1024], BF16)
    wvT_sb = SB(es0, "wvT_sb", [128, 8 * 1024], BF16)
    junk_a = SB(es0, "junk_a", [128, 1024], BF16)
    stat = SB(es0, "stat", [128, 48], F32)

    prog = Prog(nc)
    P = prog.op
    S = lambda n: prog.sem(n, es0)
    CNT = lambda s: prog.cnt[s]

    bar_i = [0]

    def barrier():
        s = S(f"bar{bar_i[0]}")
        bar_i[0] += 1
        for e in ENGINES:
            P(e, lambda eng: eng.nop(), inc=s)
        for e in ENGINES:
            P(e, lambda eng: eng.nop(), waits=((s, len(ENGINES)),))

    es_ia = ExitStack()
    wqkv_sb = SB(es_ia, "wqkv_sb", [128, 8 * 3072], BF16)
    dirs33_sb = SB(es_ia, "dirs33_sb", [3, 24], BF16)
    d3s_sb = SB(es_ia, "d3s_sb", [3, 8], F32)
    c3_sb = SB(es_ia, "c3_sb", [3, 8], F32)
    hsrow = SB(es_ia, "hsrow", [1, 8], F32)
    tp_ps = PS(es_ia, "tp_ps", [128, 2048], BF16)
    proj_ps = PS(es_ia, "proj_ps", [128, 512], F32)

    s_ic = S("ic")
    s_ipe = S("ipe")
    s_ism = S("ism")
    s_wst = [S("wst0"), S("wst1")]

    small = [
        (eye_sb, eye_d), (onesb_sb, onesb_d), (onesf_sb, onesf_d),
        (dirs33_sb, dirs33_d), (d3s_sb, d3s_d), (scal_sb, scal_d), (hsrow, hs_in),
    ]
    for dst, src in small:
        prog.dma("sync", lambda e, d=dst, s=src: e.dma_start(out=d[:], in_=s), s_ism)
    for cc in range(8):
        prog.dma("sync", lambda e, cc=cc: e.dma_start(
            out=mu_sb[:, cc:cc + 1],
            in_=mu_r_in[cc * 128:(cc + 1) * 128].rearrange("(p o) -> p o", o=1)), s_ism)
        prog.dma("sync", lambda e, cc=cc: e.dma_start(
            out=mu_sb[:, 8 + cc:9 + cc],
            in_=mu_k_in[cc * 128:(cc + 1) * 128].rearrange("(p o) -> p o", o=1)), s_ism)
    ism_total = CNT(s_ism)

    es_i = ExitStack()
    wstg = SB(es_i, "wstg", [128, 2 * 3072], F32)
    wtmp_bf = SB(es_i, "wtmp_bf", [128, 2 * 1024], BF16)
    _cut = int(os.environ.get("INIT_CUT", "9"))

    cast_evt = {}
    ci = 0
    for cc in range(8 if _cut >= 2 else 0):
        buf = ci % 2
        prog.dma("sync", lambda e, cc=cc, buf=buf: e.dma_start(
            out=wstg[:, buf * 3072:(buf + 1) * 3072],
            in_=w_qkv[cc * 128:(cc + 1) * 128, :]),
          s_wst[buf], waits=((s_ic, cast_evt.get(ci - 2)),))
        P("vector", lambda e, cc=cc, buf=buf: e.tensor_copy(
            out=wqkv_sb[:, cc * 3072:(cc + 1) * 3072],
            in_=wstg[:, buf * 3072:(buf + 1) * 3072]),
          waits=((s_wst[buf], CNT(s_wst[buf])),), inc=s_ic)
        cast_evt[ci] = CNT(s_ic)
        ci += 1
    for cc in range(8 if _cut >= 3 else 0):
        buf = ci % 2
        prog.dma("sync", lambda e, cc=cc, buf=buf: e.dma_start(
            out=wstg[:, buf * 3072:buf * 3072 + 1024],
            in_=w_out[cc * 128:(cc + 1) * 128, :]),
          s_wst[buf], waits=((s_ic, cast_evt.get(ci - 2)),))
        P("vector", lambda e, cc=cc, buf=buf: e.tensor_copy(
            out=wout_sb[:, cc * 1024:(cc + 1) * 1024],
            in_=wstg[:, buf * 3072:buf * 3072 + 1024]),
          waits=((s_wst[buf], CNT(s_wst[buf])),), inc=s_ic)
        cast_evt[ci] = CNT(s_ic)
        ci += 1
    for (wsrc, wdst) in ([(w_r, wrT_sb), (w_k, wkT_sb), (w_v, wvT_sb)] if _cut >= 4 else []):
        for cc in range(8):
            buf = ci % 2
            prog.dma("sync", lambda e, wsrc=wsrc, cc=cc, buf=buf: e.dma_start(
                out=wstg[:, buf * 3072:buf * 3072 + 1024],
                in_=wsrc[cc * 128:(cc + 1) * 128, :]),
              s_wst[buf], waits=((s_ic, cast_evt.get(ci - 2)),))
            P("vector", lambda e, wdst=wdst, cc=cc, buf=buf: e.tensor_copy(
                out=wdst[:, cc * 1024:(cc + 1) * 1024],
                in_=wstg[:, buf * 3072:buf * 3072 + 1024]),
              waits=((s_wst[buf], CNT(s_wst[buf])),), inc=s_ic)
            cast_evt[ci] = CNT(s_ic)
            ci += 1
    if _cut >= 5:
        P("tensor", lambda e: e.matmul(out=proj_ps[0:3, 0:8], lhsT=onesf_sb[0:1, 0:3],
                                       rhs=hsrow[:, :], start=True, stop=True),
          waits=((s_ism, ism_total),), inc=s_ipe)
        P("vector", lambda e: e.tensor_tensor(out=c3_sb[:, :], in0=proj_ps[0:3, 0:8],
                                              in1=d3s_sb[:, :], op=ALU.mult),
          waits=((s_ipe, CNT(s_ipe)),), inc=s_ic)

    P("sync", lambda e: e.nop(), waits=((s_ism, ism_total),))
    barrier()
    prog.flush()
    es_i.close()

    if "a" not in phases:
        prog.flush()
        es_ia.close()
        es0.close()
        nc.finalize()
        return nc

    s_ax = [S("ax0"), S("ax1")]
    s_dbg = S("dbg0")
    s_aact = S("aact")
    s_adve = S("adve")
    s_ape = S("ape")
    s_apet = S("apet")
    s_atev = S("atev")
    s_qs = [S("qs0"), S("qs1")]
    s_ks = [S("ks0"), S("ks1")]
    s_vs = [S("vs0"), S("vs1")]

    es_a = ExitStack()
    xt = SB(es_a, "xt", [128, 2 * 4096], F32)
    xhat_bf = SB(es_a, "xhat_bf", [128, 2 * 4096], BF16)
    xhatT = SB(es_a, "xhatT", [128, 8 * 512], BF16)
    qk_ps = PS(es_a, "qk_ps", [128, 2048], F32)
    nrm_ps = PS(es_a, "nrm_ps", [128, 512], F32)
    qkst = SB(es_a, "qkst", [128, 4 * 512], BF16)
    qdm = SB(es_a, "qdm", [128, 8 * 512], BF16)
    kdm = SB(es_a, "kdm", [128, 8 * 512], BF16)
    vbuf = SB(es_a, "vbuf", [128, 2 * 1024], BF16)
    sqv = SB(es_a, "sqv", [128, 4 * 512], BF16)
    mq2 = SB(es_a, "mq2", [128, 8], F32)
    mk2run = SB(es_a, "mk2run", [128, 8], F32)
    b2t = SB(es_a, "b2t", [128, 8], F32)
    v_ps = qk_ps

    qk_evac2 = {}
    qk_tp_evt = {}
    fix_evt = {}
    v_evac = {}
    ax_done = {}
    dve_var = {}
    act_sq = {}
    act_rstd = {}
    cast_done = {}
    xtp_done = {}
    perhead_done = {}
    qspl = {}
    zfree = {0: 0, 1: 0, 2: 0, 3: 0}
    nblk = [0]

    def emit_xdma(j):
        xb = j % 2
        t0j = j * 512
        prog.dma("sync", lambda e, t0j=t0j, xb=xb: e.dma_start(
            out=xt[:, xb * 4096:(xb + 1) * 4096].rearrange("p (c f) -> p c f", c=4),
            in_=x_in[t0j:t0j + 512, :].rearrange("(c p) f -> p c f", p=128)),
          s_ax[xb], waits=((s_adve, cast_done.get(j - 2)),))
        ax_done[j] = CNT(s_ax[xb])

    def emit_ln_p1(j):
        xb = j % 2
        for tt in range(4):
            P("vector", lambda e, xb=xb, tt=tt: e.reduce_sum(
                out=stat[:, tt:tt + 1],
                in_=xt[:, xb * 4096 + tt * 1024:xb * 4096 + (tt + 1) * 1024], axis=AXX),
              waits=((s_ax[xb], ax_done[j]),), inc=s_adve)

    def emit_ln_p2(j):
        xb = j % 2
        for tt in range(4):
            P("scalar", lambda e, xb=xb, tt=tt: e.activation(
                out=junk_a[:], in_=xt[:, xb * 4096 + tt * 1024:xb * 4096 + (tt + 1) * 1024],
                func=AF.Square, accum_out=stat[:, 8 + tt:9 + tt]),
              waits=((s_ax[xb], ax_done[j]), (s_adve, dve_var.get(j - 1))),
              inc=s_aact)
        act_sq[j] = CNT(s_aact)
        P("vector", lambda e: e.tensor_scalar(
            out=stat[:, 16:20], in0=stat[:, 0:4], scalar1=1.0 / C, scalar2=None, op0=ALU.mult),
          waits=((s_adve, CNT(s_adve)),), inc=s_adve)
        P("vector", lambda e: e.tensor_tensor(
            out=stat[:, 24:28], in0=stat[:, 16:20], in1=stat[:, 16:20], op=ALU.mult),
          waits=((s_adve, CNT(s_adve)),), inc=s_adve)
        P("vector", lambda e: e.scalar_tensor_tensor(
            out=stat[:, 24:28], in0=stat[:, 8:12], scalar=1.0 / C, in1=stat[:, 24:28],
            op0=ALU.mult, op1=ALU.subtract),
          waits=((s_aact, act_sq[j]), (s_adve, CNT(s_adve))), inc=s_adve)
        dve_var[j] = CNT(s_adve)
        P("scalar", lambda e: e.activation(out=stat[:, 24:28], in_=stat[:, 24:28],
                                           func=AF.Ln, bias=scal_sb[:, 0:1]),
          waits=((s_adve, dve_var[j]),))
        P("scalar", lambda e: e.activation(out=stat[:, 24:28], in_=stat[:, 24:28],
                                           func=AF.Exp, scale=-0.5), inc=s_aact)
        act_rstd[j] = CNT(s_aact)

    def emit_ln_p3(j):
        xb = j % 2
        for tt in range(4):
            P("vector", lambda e, xb=xb, tt=tt: e.tensor_scalar(
                out=xhat_bf[:, xb * 4096 + tt * 1024:xb * 4096 + (tt + 1) * 1024],
                in0=xt[:, xb * 4096 + tt * 1024:xb * 4096 + (tt + 1) * 1024],
                scalar1=stat[:, 16 + tt:17 + tt], scalar2=stat[:, 24 + tt:25 + tt],
                op0=ALU.subtract, op1=ALU.mult),
              waits=((s_aact, act_rstd[j]), (s_adve, CNT(s_adve)),
                     (s_apet, xtp_done.get(j - 2))),
              inc=s_adve)
        cast_done[j] = CNT(s_adve)

    emit_xdma(0)
    emit_ln_p1(0)
    emit_ln_p2(0)
    emit_ln_p3(0)

    for i in range(NT):
        t0 = i * 512
        xb = i % 2
        if i + 1 < NT:
            emit_xdma(i + 1)
        for g in range(4):
            gg = i * 4 + g
            tb = (gg % 2) * 1024
            z0, z1 = 2 * (gg % 2), 2 * (gg % 2) + 1
            for s in range(8):
                j = g * 8 + s
                tt, cc = divmod(j, 8)
                P("tensor", lambda e, tt=tt, cc=cc, s=s, tb=tb, xb=xb: e.transpose(
                    out=tp_ps[:, tb + s * 128:tb + (s + 1) * 128],
                    in_=xhat_bf[:, xb * 4096 + tt * 1024 + cc * 128:xb * 4096 + tt * 1024 + (cc + 1) * 128],
                    identity=eye_sb[:]),
                  waits=((s_adve, cast_done[i]), (s_atev, max(zfree[z0], zfree[z1]))),
                  inc=s_apet)
            tr_done = CNT(s_apet)
            for s in range(8):
                j = g * 8 + s
                tt, cc = divmod(j, 8)
                P("vector", lambda e, tt=tt, cc=cc, s=s, tb=tb: e.tensor_copy(
                    out=xhatT[:, cc * 512 + tt * 128:cc * 512 + tt * 128 + 128],
                    in_=tp_ps[:, tb + s * 128:tb + (s + 1) * 128]),
                  waits=((s_apet, tr_done),), inc=s_atev)
            zfree[z0] = zfree[z1] = CNT(s_atev)
        tev_done = CNT(s_atev)
        xtp_done[i] = CNT(s_apet)
        def emit_qktp(ci2, i=i):
            tt, rest = divmod(ci2, 4)
            which, g = divmod(rest, 2)
            sl = ci2 % 4
            zq = ci2 % 4
            tb2 = zq * 512
            for j in range(4):
                P("tensor", lambda e, sl=sl, j=j, tb2=tb2: e.transpose(
                    out=tp_ps[:, tb2 + j * 128:tb2 + (j + 1) * 128],
                    in_=qkst[:, sl * 512 + j * 128:sl * 512 + (j + 1) * 128],
                    identity=eye_sb[:]),
                  waits=((s_aact, qk_evac2[(i, ci2)]), (s_atev, zfree[zq])),
                  inc=s_apet)
            qk_tp_evt[(i, ci2)] = CNT(s_apet)
            for j in range(4):
                h2 = g * 4 + j
                dst = qdm if which == 0 else kdm
                P("vector", lambda e, dst=dst, h2=h2, j=j, tb2=tb2, tt=tt: e.tensor_copy(
                    out=dst[:, h2 * 512 + tt * 128:h2 * 512 + tt * 128 + 128],
                    in_=tp_ps[:, tb2 + j * 128:tb2 + (j + 1) * 128]),
                  waits=((s_apet, qk_tp_evt[(i, ci2)]),
                         (s_aact, perhead_done.get(i - 1)))
                        + qspl.get(i - 1, ()),
                  inc=s_atev)
            zfree[zq] = CNT(s_atev)

        for ci2 in range(16):
            tt, rest = divmod(ci2, 4)
            which, g = divmod(rest, 2)
            bank = ci2 % 4
            sl = ci2 % 4
            wv_extra = (v_evac.get(4 * (i - 1) + 2 + ci2 // 2) if ci2 < 4 else None)
            cbase = which * 1024 + g * 512
            for cc in range(8):
                P("tensor", lambda e, bank=bank, cc=cc, tt=tt, cbase=cbase: e.matmul(
                    out=qk_ps[:, bank * 512:(bank + 1) * 512],
                    lhsT=xhatT[:, cc * 512 + tt * 128:cc * 512 + tt * 128 + 128],
                    rhs=wqkv_sb[:, cc * 3072 + cbase:cc * 3072 + cbase + 512],
                    start=(cc == 0), stop=(cc == 7)),
                  waits=((s_atev, tev_done),
                         (s_aact, qk_evac2.get((i, ci2 - 4))),
                         (s_aact, wv_extra)),
                  inc=(s_ape if cc == 7 else None))
            pe_mm = CNT(s_ape)
            P("scalar", lambda e, sl=sl, bank=bank, which=which: e.activation(
                out=qkst[:, sl * 512:(sl + 1) * 512],
                in_=qk_ps[:, bank * 512:(bank + 1) * 512],
                func=AF.Copy, scale=(SCALE if which == 0 else 1.0)),
              waits=((s_ape, pe_mm), (s_apet, qk_tp_evt.get((i, ci2 - 4)))),
              inc=s_aact)
            qk_evac2[(i, ci2)] = CNT(s_aact)
            if ci2 >= 1:
                emit_qktp(ci2 - 1)
        emit_qktp(15)
        tev_qk = CNT(s_atev)
        nrm_evt = {}
        sq_evt2 = {}
        for h in range(H):
            if i + 1 < NT:
                if h == 2:
                    emit_ln_p1(i + 1)
                elif h == 4:
                    emit_ln_p2(i + 1)
                elif h == 6:
                    emit_ln_p3(i + 1)
            gh = i * H + h
            hs = h * 512
            P("tensor", lambda e, h=h, hs=hs: e.matmul(
                out=proj_ps[0:3, :], lhsT=dirs33_sb[:, 3 * h:3 * h + 3],
                rhs=qdm[0:3, hs:hs + 512], start=True, stop=True),
              waits=((s_atev, tev_qk), (s_adve, fix_evt.get(gh - 1))),
              inc=s_ape)
            P("vector", lambda e, h=h, hs=hs: e.scalar_tensor_tensor(
                out=qdm[0:3, hs:hs + 512],
                in0=proj_ps[0:3, :], scalar=c3_sb[:, h:h + 1],
                in1=qdm[0:3, hs:hs + 512],
                op0=ALU.mult, op1=ALU.add),
              waits=((s_ape, CNT(s_ape)),), inc=s_adve)
            fix_evt[gh] = CNT(s_adve)
            prog.dma("sync", lambda e, h=h, hs=hs, t0=t0: e.dma_start(
                out=q_scr[h, :, t0:t0 + 512], in_=qdm[:, hs:hs + 512]),
              s_qs[h % 2], waits=((s_adve, fix_evt[gh]),))
            prog.dma("sync", lambda e, h=h, hs=hs, t0=t0: e.dma_start(
                out=k_scr[h, :, t0:t0 + 512], in_=kdm[:, hs:hs + 512]),
              s_ks[h % 2], waits=((s_atev, tev_qk),))
        for idx in range(16):
            h = idx // 2
            which = idx % 2
            sl4 = idx % 4
            src = qdm if which == 0 else kdm
            hs = h * 512
            P("scalar", lambda e, src=src, hs=hs, sl4=sl4: e.activation(
                out=sqv[:, sl4 * 512:(sl4 + 1) * 512], in_=src[:, hs:hs + 512],
                func=AF.Square),
              waits=(((s_adve, fix_evt[i * H + h]) if which == 0 else (s_atev, tev_qk)),
                     (s_ape, nrm_evt.get(idx - 4))),
              inc=s_aact)
            sq_evt2[idx] = CNT(s_aact)
            P("tensor", lambda e, sl4=sl4: e.matmul(
                out=nrm_ps[:], lhsT=onesb_sb[:], rhs=sqv[:, sl4 * 512:(sl4 + 1) * 512],
                start=True, stop=True),
              waits=((s_aact, sq_evt2[idx]), (s_adve, CNT(s_adve))), inc=s_ape)
            nrm_evt[idx] = CNT(s_ape)
            if which == 0:
                P("vector", lambda e, h=h: e.reduce_max(
                    out=mq2[:, h:h + 1], in_=nrm_ps[:], axis=AXX),
                  waits=((s_ape, nrm_evt[idx]),), inc=s_adve)
            else:
                P("vector", lambda e, h=h: e.reduce_max(
                    out=b2t[:, h:h + 1], in_=nrm_ps[:], axis=AXX),
                  waits=((s_ape, nrm_evt[idx]),), inc=s_adve)
                if i == 0:
                    P("vector", lambda e, h=h: e.tensor_copy(
                        out=mk2run[:, h:h + 1], in_=b2t[:, h:h + 1]),
                      waits=((s_adve, CNT(s_adve)),), inc=s_adve)
                else:
                    P("vector", lambda e, h=h: e.tensor_tensor(
                        out=mk2run[:, h:h + 1], in0=mk2run[:, h:h + 1],
                        in1=b2t[:, h:h + 1], op=ALU.max),
                      waits=((s_adve, CNT(s_adve)),), inc=s_adve)
        perhead_done[i] = CNT(s_aact)
        qspl[i] = ((s_qs[0], CNT(s_qs[0])), (s_qs[1], CNT(s_qs[1])),
                   (s_ks[0], CNT(s_ks[0])), (s_ks[1], CNT(s_ks[1])))
        P("vector", lambda e: e.tensor_tensor(out=b2t[:], in0=mq2[:], in1=mk2run[:],
                                              op=ALU.mult),
          waits=((s_adve, CNT(s_adve)),), inc=s_adve)
        dve_b2 = CNT(s_adve)
        P("scalar", lambda e: e.activation(out=b2t[:], in_=b2t[:], func=AF.Ln),
          waits=((s_adve, dve_b2),))
        P("scalar", lambda e: e.activation(out=b2t[:], in_=b2t[:], func=AF.Exp, scale=0.5),
          inc=s_aact)
        act_b = CNT(s_aact)
        P("vector", lambda e, i=i: e.tensor_scalar(
            out=negB[:, i:i + (H - 1) * NQT + 1:NQT], in0=b2t[:], scalar1=-1.0, scalar2=None,
            op0=ALU.mult),
          waits=((s_aact, act_b),), inc=s_adve)
        for tt in range(4):
            gtt = i * 4 + tt
            vbank = (tt % 2) * 2
            if tt < 2:
                wv = ((s_aact, qk_evac2[(i, 12 + 2 * tt)]),
                      (s_aact, qk_evac2[(i, 13 + 2 * tt)]),
                      (s_atev, tev_done))
            else:
                wv = ((s_aact, v_evac.get(gtt - 2)), (s_atev, tev_done))
            for half in range(2):
                for cc in range(8):
                    P("tensor", lambda e, vbank=vbank, half=half, cc=cc, tt=tt: e.matmul(
                        out=v_ps[:, (vbank + half) * 512:(vbank + half + 1) * 512],
                        lhsT=xhatT[:, cc * 512 + tt * 128:cc * 512 + tt * 128 + 128],
                        rhs=wqkv_sb[:, cc * 3072 + 2048 + half * 512:cc * 3072 + 2048 + (half + 1) * 512],
                        start=(cc == 0), stop=(cc == 7)),
                      waits=wv,
                      inc=(s_ape if cc == 7 else None))
            pe_v = CNT(s_ape)
            buf = tt % 2
            P("scalar", lambda e, buf=buf, vbank=vbank: e.activation(
                out=vbuf[:, buf * 1024:(buf + 1) * 1024],
                in_=v_ps[:, vbank * 512:(vbank + 2) * 512], func=AF.Copy),
              waits=((s_ape, pe_v), (s_vs[tt % 2], CNT(s_vs[tt % 2]))),
              inc=s_aact)
            v_evac[gtt] = CNT(s_aact)
            act_v = CNT(s_aact)
            prog.dma("gpsimd", lambda e, buf=buf, t0=t0, tt=tt: e.dma_start(
                out=v_scr[t0 + tt * 128:t0 + (tt + 1) * 128, :],
                in_=vbuf[:, buf * 1024:(buf + 1) * 1024]),
              s_vs[tt % 2], waits=((s_aact, act_v),))

    P("sync", lambda e: e.nop(),
      waits=tuple((s, CNT(s)) for s in (s_qs[0], s_qs[1], s_ks[0], s_ks[1], s_vs[0], s_vs[1])))
    barrier()
    prog.flush()
    es_a.close()
    es_ia.close()

    if "b" not in phases:
        prog.flush()
        es0.close()
        nc.finalize()
        return nc

    s_kv = [S("kv0"), S("kv1")]
    s_qq = [S("qq0"), S("qq1")]
    s_bs = S("bs")
    s_bexp = S("bexp")
    s_bo = S("bo")
    s_bmsk = S("bmsk")
    s_bla = S("bla")
    s_boe = S("boe")
    s_os = [S("os0"), S("os1")]
    s_ls = [S("ls0"), S("ls1")]
    s_bmld = S("bmld")

    es_b = ExitStack()
    masks_sb = SB(es_b, "masks_sb", [128, 4 * 512], BF16)
    k_sb = SB(es_b, "k_sb", [128, 2 * T], BF16)
    v_sb = SB(es_b, "v_sb", [128, 2 * T], BF16)
    q_sb = SB(es_b, "q_sb", [128, 2 * T], BF16)
    a_sb = SB(es_b, "a_sb", [128, 4 * 512], BF16)
    obf_sb = SB(es_b, "obf_sb", [128, 2 * 512], BF16)
    l_sb = SB(es_b, "l_sb", [1, 2 * 512], F32)
    s_ps = PS(es_b, "s_ps", [128, 2048], F32)
    o_ps = PS(es_b, "o_ps", [128, 1024], F32)
    l_ps = PS(es_b, "l_ps", [128, 2 * 512], F32)

    prog.dma("sync", lambda e: e.dma_start(out=masks_sb[:], in_=masks_d), s_bmld)

    s_end = {}
    o_end = {}
    oe_evt = {}
    l_free = {}
    bank_exp = {0: 0, 1: 0, 2: 0, 3: 0}
    slot_o = {0: 0, 1: 0, 2: 0, 3: 0}

    for h in range(H):
        hb = (h % 2) * T
        w_prev = o_end.get((h - 2) * NQT + NQT - 1) if h >= 2 else None
        prog.dma("gpsimd", lambda e, h=h, hb=hb: e.dma_start(
            out=k_sb[:, hb:hb + T], in_=k_scr[h, :, :]),
          s_kv[h % 2], waits=((s_bo, w_prev),))
        prog.dma("gpsimd", lambda e, h=h, hb=hb: e.dma_start(
            out=v_sb[:, hb:hb + T].rearrange("p (c d) -> p c d", d=128),
            in_=v_scr[:, h * 128:(h + 1) * 128].rearrange("(c p) d -> p c d", p=128)),
          s_kv[h % 2], waits=((s_bo, w_prev),))
        kv_ld = CNT(s_kv[h % 2])
        prog.dma("sync", lambda e, h=h, hb=hb: e.dma_start(
            out=q_sb[:, hb:hb + T], in_=q_scr[h, :, :]),
          s_qq[h % 2], waits=((s_bo, w_prev),))
        q_ld = CNT(s_qq[h % 2])
        for qt in range(NQT):
            pi = h * NQT + qt
            nck = 4 * (qt + 1)
            qb = hb + qt * 512
            col = h * NQT + qt
            pending = []

            def flush_o():
                for c2, exp_v, msk_v in pending:
                    sl = c2 % 4
                    P("tensor", lambda e, c2=c2, pi=pi, hb=hb, sl=sl, nck=nck: e.matmul(
                        out=o_ps[:, (pi % 2) * 512:(pi % 2 + 1) * 512],
                        lhsT=v_sb[:, hb + c2 * 128:hb + (c2 + 1) * 128],
                        rhs=a_sb[:, sl * 512:(sl + 1) * 512],
                        start=(c2 == 0), stop=(c2 == nck - 1)),
                      waits=((s_bexp, exp_v), (s_bmsk, msk_v),
                             (s_boe, oe_evt.get(pi - 2) if c2 == 0 else None)),
                      inc=s_bo)
                    P("tensor", lambda e, c2=c2, pi=pi, sl=sl, nck=nck: e.matmul(
                        out=l_ps[:, (pi % 2) * 512:(pi % 2 + 1) * 512],
                        lhsT=onesb_sb[:],
                        rhs=a_sb[:, sl * 512:(sl + 1) * 512],
                        start=(c2 == 0), stop=(c2 == nck - 1)),
                      waits=((s_bla, l_free.get(pi - 2) if c2 == 0 else None),),
                      inc=s_bo)
                    slot_o[sl] = CNT(s_bo)
                pending.clear()

            for c in range(nck):
                bank = c % 4
                P("tensor", lambda e, c=c, hb=hb, qb=qb, bank=bank: e.matmul(
                    out=s_ps[:, bank * 512:(bank + 1) * 512],
                    lhsT=k_sb[:, hb + c * 128:hb + (c + 1) * 128],
                    rhs=q_sb[:, qb:qb + 512], start=True, stop=True),
                  waits=((s_kv[h % 2], kv_ld), (s_qq[h % 2], q_ld), (s_bexp, bank_exp[bank])),
                  inc=s_bs)
                if c % 2 == 1:
                    flush_o()
                    slot = (c - 1) % 4
                    P("scalar", lambda e, slot=slot, col=col: e.activation(
                        out=a_sb[:, slot * 512:(slot + 2) * 512],
                        in_=s_ps[:, slot * 512:(slot + 2) * 512],
                        func=AF.Exp, bias=negB[:, col:col + 1]),
                      waits=((s_bs, CNT(s_bs)),
                             (s_bo, max(slot_o[slot], slot_o[slot + 1]))),
                      inc=s_bexp)
                    bank_exp[slot] = CNT(s_bexp)
                    bank_exp[slot + 1] = CNT(s_bexp)
                    exp_now = CNT(s_bexp)
                    for c2 in (c - 1, c):
                        sl = c2 % 4
                        diag = c2 >= nck - 4
                        if diag:
                            jm = c2 - (nck - 4)
                            P("vector", lambda e, sl=sl, jm=jm: e.tensor_tensor(
                                out=a_sb[:, sl * 512:(sl + 1) * 512],
                                in0=a_sb[:, sl * 512:(sl + 1) * 512],
                                in1=masks_sb[:, jm * 512:(jm + 1) * 512], op=ALU.mult),
                              waits=((s_bexp, exp_now), (s_bmld, 16)), inc=s_bmsk)
                        pending.append((c2, exp_now, CNT(s_bmsk)))
            flush_o()
            s_end[pi] = CNT(s_bs)
            o_end[pi] = CNT(s_bo)
            P("scalar", lambda e, pi=pi: e.activation(
                out=l_sb[0:1, (pi % 2) * 512:(pi % 2 + 1) * 512],
                in_=l_ps[0:1, (pi % 2) * 512:(pi % 2 + 1) * 512],
                func=AF.Copy),
              waits=((s_bo, o_end[pi]), (s_ls[pi % 2], CNT(s_ls[pi % 2]))),
              inc=s_bla)
            l_free[pi] = CNT(s_bla)
            prog.dma("sync", lambda e, h=h, qt=qt, pi=pi: e.dma_start(
                out=l_scr[h, qt * 512:(qt + 1) * 512].rearrange("(o f) -> o f", o=1),
                in_=l_sb[0:1, (pi % 2) * 512:(pi % 2 + 1) * 512]),
              s_ls[pi % 2], waits=((s_bla, CNT(s_bla)),))
            P("vector", lambda e, pi=pi: e.tensor_copy(
                out=obf_sb[:, (pi % 2) * 512:(pi % 2 + 1) * 512],
                in_=o_ps[:, (pi % 2) * 512:(pi % 2 + 1) * 512]),
              waits=((s_bo, o_end[pi]), (s_os[pi % 2], CNT(s_os[pi % 2]))),
              inc=s_boe)
            oe_evt[pi] = CNT(s_boe)
            prog.dma("sync", lambda e, h=h, qt=qt, pi=pi: e.dma_start(
                out=o_scr[h * 128:(h + 1) * 128, qt * 512:(qt + 1) * 512],
                in_=obf_sb[:, (pi % 2) * 512:(pi % 2 + 1) * 512]),
              s_os[pi % 2], waits=((s_boe, CNT(s_boe)),))

    P("sync", lambda e: e.nop(),
      waits=tuple((s, CNT(s)) for s in (s_os[0], s_os[1], s_ls[0], s_ls[1])))
    barrier()
    prog.flush()
    es_b.close()

    if "c" not in phases:
        prog.flush()
        es0.close()
        nc.finalize()
        return nc

    s_co = [S("co0"), S("co1")]
    s_cx2 = [S("cx20"), S("cx21")]
    s_cl = S("cl")
    s_cpe = S("cpe")
    s_cact = S("cact")
    s_cdve = S("cdve")
    s_cpet = S("cpet")
    s_ctev = S("ctev")
    s_ost = [S("cst0"), S("cst1")]
    s_carry = S("carry0")
    s_csel = S("csel")
    s_cpool = S("cpool")

    _pcut = int(os.environ.get("P2_CUT", "9"))
    _psub = int(os.environ.get("P2_SUB", "9"))
    es_c = ExitStack()
    sel8_sb = SB(es_c, "sel8_sb", [8, 8 * 128], BF16)
    ot_sb = SB(es_c, "ot_sb", [128, 2 * 4096], BF16)
    linv_sb = SB(es_c, "linv_sb", [8, 2 * 512], F32)
    linvh_sb = SB(es_c, "linvh_sb", [8, 2 * 512], BF16)
    linvl_sb = SB(es_c, "linvl_sb", [8, 2 * 512], BF16)
    x2t = SB(es_c, "x2t", [128, 2 * 4096], F32)
    anrm_sb = SB(es_c, "anrm_sb", [128, 1024], F32)
    araw_sb = SB(es_c, "araw_sb", [128, 2 * 1024], F32)
    xh2_bf = SB(es_c, "xh2_bf", [128, 4096], BF16)
    xh2T = SB(es_c, "xh2T", [128, 4096], BF16)
    xrT = SB(es_c, "xrT", [128, 4096], BF16)
    xkT = SB(es_c, "xkT", [128, 4096], BF16)
    carry = SB(es_c, "carry", [128, 8], BF16)
    r_sb = SB(es_c, "r_sb", [128, 1024], F32)
    krel_sb = SB(es_c, "krel_sb", [128, 2 * 512], F32)
    kact_sb = SB(es_c, "kact_sb", [128, 4096], BF16)
    ffn_sb = SB(es_c, "ffn_sb", [128, 4096], F32)
    bc_ps = PS(es_c, "bc_ps", [128, 1024], F32)
    big_ps = PS(es_c, "big_ps", [128, 2048], F32)
    tp2_ps = PS(es_c, "tp2_ps", [128, 2048], BF16)

    P("gpsimd", lambda e: e.memset(carry[:], 0.0), inc=s_carry)
    prog.dma("sync", lambda e: e.dma_start(out=sel8_sb[:], in_=sel8_d), s_csel)

    ot_free = {}
    x_free = {}
    l_free = {}
    anrm_evt = {}
    sig_evt = {}
    ffn_evt = {}
    relu_evt = {}
    sq_evt = {}
    rmul_evt = {}
    linv_bf = {}
    bcast_done = {}
    ffn_pe_done = {}
    ln2_st8 = {}
    ln2_st24 = {}
    ln3_st12 = {}
    ln3_st28 = {}
    ln4_var_free = [None]
    ln4_rstd_free = [None]
    mixp_hist = {}
    store_done = {}
    nblk2 = [0]

    def emit_linv(j):
        lb = (j % 2) * 512
        prog.dma("sync", lambda e, j=j, lb=lb: e.dma_start(
            out=linv_sb[:, lb:lb + 512], in_=l_scr[:, j * 512:(j + 1) * 512]),
          s_cl, waits=((s_cdve, linv_bf.get(j - 2)),))
        cl_ld = CNT(s_cl)
        P("scalar", lambda e, lb=lb: e.activation(
            out=linv_sb[:, lb:lb + 512], in_=linv_sb[0:8, lb:lb + 512], func=AF.Ln),
          waits=((s_cl, cl_ld),))
        P("scalar", lambda e, lb=lb: e.activation(
            out=linv_sb[:, lb:lb + 512], in_=linv_sb[:, lb:lb + 512],
            func=AF.Exp, scale=-1.0), inc=s_cact)
        l_free[j] = CNT(s_cact)
        P("vector", lambda e, lb=lb: e.tensor_copy(
            out=linvh_sb[:, lb:lb + 512], in_=linv_sb[:, lb:lb + 512]),
          waits=((s_cact, l_free[j]), (s_cpe, bcast_done.get(j - 2))), inc=s_cdve)
        P("vector", lambda e, lb=lb: e.tensor_tensor(
            out=linvl_sb[:, lb:lb + 512], in0=linv_sb[:, lb:lb + 512],
            in1=linvh_sb[:, lb:lb + 512], op=ALU.subtract),
          inc=s_cdve)
        linv_bf[j] = CNT(s_cdve)

    for i in range(NT):
        t0 = i * 512
        ob = i % 2
        xb = i % 2
        lb = (i % 2) * 512
        prog.dma("sync", lambda e, t0=t0, ob=ob: e.dma_start(
            out=ot_sb[:, ob * 4096:(ob + 1) * 4096].rearrange("p (c f) -> p c f", c=8),
            in_=o_scr[:, t0:t0 + 512].rearrange("(c p) f -> p c f", p=128)),
          s_co[i % 2], waits=((s_cpe, ot_free.get(i - 2)),))
        co_ld = CNT(s_co[i % 2])
        prog.dma("sync", lambda e, t0=t0, xb=xb: e.dma_start(
            out=x2t[:, xb * 4096:(xb + 1) * 4096].rearrange("p (c f) -> p c f", c=4),
            in_=x_in[t0:t0 + 512, :].rearrange("(c p) f -> p c f", p=128)),
          s_cx2[i % 2], waits=((s_cdve, x_free.get(i - 2)),) + store_done.get(i - 2, ()))
        cx_ld = CNT(s_cx2[i % 2])
        if i == 0:
            emit_linv(0)
        onrm_evts = []
        for cc in range(8):
            P("tensor", lambda e, cc=cc, lb=lb: e.matmul(
                out=bc_ps[:, (cc % 2) * 512:(cc % 2 + 1) * 512],
                lhsT=sel8_sb[:, cc * 128:(cc + 1) * 128], rhs=linvh_sb[0:8, lb:lb + 512],
                start=True, stop=False),
              waits=((s_cdve, linv_bf[i]), (s_csel, 16),
                     (s_cact, relu_evt.get((i - 1, 7))),
                     (s_cdve, onrm_evts[cc - 2] if cc >= 2 else None)))
            P("tensor", lambda e, cc=cc, lb=lb: e.matmul(
                out=bc_ps[:, (cc % 2) * 512:(cc % 2 + 1) * 512],
                lhsT=sel8_sb[:, cc * 128:(cc + 1) * 128], rhs=linvl_sb[0:8, lb:lb + 512],
                start=False, stop=True),
              inc=s_cpe)
            P("vector", lambda e, cc=cc, ob=ob: e.tensor_tensor(
                out=ot_sb[:, ob * 4096 + cc * 512:ob * 4096 + (cc + 1) * 512],
                in0=ot_sb[:, ob * 4096 + cc * 512:ob * 4096 + (cc + 1) * 512],
                in1=bc_ps[:, (cc % 2) * 512:(cc % 2 + 1) * 512], op=ALU.mult),
              waits=((s_cpe, CNT(s_cpe)), (s_co[i % 2], co_ld), (s_cx2[i % 2], cx_ld)),
              inc=s_cdve)
            onrm_evts.append(CNT(s_cdve))
        onrm_done = CNT(s_cdve)
        bcast_done[i] = CNT(s_cpe)
        if i + 1 < NT:
            emit_linv(i + 1)
        x1_evt = {}
        cast_evt2 = {}
        for tt in range(4):
            for half in range(2):
                bank = (tt % 2) * 2 + half
                for cc in range(8):
                    P("tensor", lambda e, bank=bank, cc=cc, tt=tt, half=half, ob=ob: e.matmul(
                        out=big_ps[:, bank * 512:(bank + 1) * 512],
                        lhsT=ot_sb[:, ob * 4096 + cc * 512 + tt * 128:ob * 4096 + cc * 512 + tt * 128 + 128],
                        rhs=wout_sb[:, cc * 1024 + half * 512:cc * 1024 + (half + 1) * 512],
                        start=(cc == 0), stop=(cc == 7)),
                      waits=((s_cdve, onrm_done),
                             (s_cdve, anrm_evt.get((i, tt - 2))),
                             (s_cdve, ffn_evt.get((i - 1, tt + 2)) if tt < 2 else None)),
                      inc=(s_cpe if cc == 7 else None))
            pe_ao = CNT(s_cpe)
            if tt == 3:
                ot_free[i] = pe_ao
            bank0 = (tt % 2) * 2
            ab = tt % 2
            P("vector", lambda e, bank0=bank0, ab=ab: e.tensor_scalar_mul(
                out=araw_sb[:, ab * 1024:(ab + 1) * 1024],
                in0=big_ps[:, bank0 * 512:(bank0 + 2) * 512], scalar1=1.0),
              waits=((s_cpe, pe_ao),), inc=s_cdve)
            anrm_evt[(i, tt)] = CNT(s_cdve)
            ev_done = CNT(s_cdve)
            P("vector", lambda e, ab=ab, tt=tt: e.reduce_sum(
                out=stat[:, tt:tt + 1], in_=araw_sb[:, ab * 1024:(ab + 1) * 1024],
                axis=AXX),
              inc=s_cdve)
            P("scalar", lambda e, ab=ab, tt=tt: e.activation(
                out=junk_a[:], in_=araw_sb[:, ab * 1024:(ab + 1) * 1024],
                func=AF.Square, accum_out=stat[:, 8 + tt:9 + tt]),
              waits=((s_cdve, ev_done), (s_cdve, ln2_st8.get(tt))), inc=s_cact)
            asq = CNT(s_cact)
            P("vector", lambda e, tt=tt: e.tensor_scalar(
                out=stat[:, 16 + tt:17 + tt], in0=stat[:, tt:tt + 1],
                scalar1=1.0 / C, scalar2=None, op0=ALU.mult),
              waits=((s_cdve, CNT(s_cdve)),), inc=s_cdve)
            P("vector", lambda e, tt=tt: e.tensor_tensor(
                out=stat[:, 24 + tt:25 + tt], in0=stat[:, 16 + tt:17 + tt],
                in1=stat[:, 16 + tt:17 + tt], op=ALU.mult),
              waits=((s_cdve, CNT(s_cdve)),), inc=s_cdve)
            P("vector", lambda e, tt=tt: e.scalar_tensor_tensor(
                out=stat[:, 24 + tt:25 + tt], in0=stat[:, 8 + tt:9 + tt], scalar=1.0 / C,
                in1=stat[:, 24 + tt:25 + tt], op0=ALU.mult, op1=ALU.subtract),
              waits=((s_cact, asq), (s_cdve, CNT(s_cdve))), inc=s_cdve)
            dv = CNT(s_cdve)
            ln2_st8[tt] = dv
            P("scalar", lambda e, tt=tt: e.activation(
                out=stat[:, 24 + tt:25 + tt], in_=stat[:, 24 + tt:25 + tt],
                func=AF.Ln, bias=scal_sb[:, 0:1]),
              waits=((s_cdve, dv), (s_cdve, ln2_st24.get(tt))))
            P("scalar", lambda e, tt=tt: e.activation(
                out=stat[:, 24 + tt:25 + tt], in_=stat[:, 24 + tt:25 + tt],
                func=AF.Exp, scale=-0.5), inc=s_cact)
            ar = CNT(s_cact)
            P("vector", lambda e, ab=ab, tt=tt: e.tensor_scalar(
                out=anrm_sb[:], in0=araw_sb[:, ab * 1024:(ab + 1) * 1024],
                scalar1=stat[:, 16 + tt:17 + tt], scalar2=stat[:, 24 + tt:25 + tt],
                op0=ALU.subtract, op1=ALU.mult),
              waits=((s_cact, ar),), inc=s_cdve)
            ln2_st24[tt] = CNT(s_cdve)
            P("vector", lambda e, tt=tt, xb=xb: e.tensor_tensor(
                out=x2t[:, xb * 4096 + tt * 1024:xb * 4096 + (tt + 1) * 1024],
                in0=anrm_sb[:],
                in1=x2t[:, xb * 4096 + tt * 1024:xb * 4096 + (tt + 1) * 1024], op=ALU.add),
              inc=s_cdve)
            x1_evt[tt] = CNT(s_cdve)
            P("vector", lambda e, tt=tt, xb=xb: e.reduce_sum(
                out=stat[:, 4 + tt:5 + tt],
                in_=x2t[:, xb * 4096 + tt * 1024:xb * 4096 + (tt + 1) * 1024], axis=AXX),
              inc=s_cdve)
            P("scalar", lambda e, tt=tt, xb=xb: e.activation(
                out=junk_a[:], in_=x2t[:, xb * 4096 + tt * 1024:xb * 4096 + (tt + 1) * 1024],
                func=AF.Square, accum_out=stat[:, 12 + tt:13 + tt]),
              waits=((s_cdve, x1_evt[tt]), (s_cdve, ln3_st12.get(tt))), inc=s_cact)
            asq3 = CNT(s_cact)
            P("vector", lambda e, tt=tt: e.tensor_scalar(
                out=stat[:, 20 + tt:21 + tt], in0=stat[:, 4 + tt:5 + tt],
                scalar1=1.0 / C, scalar2=None, op0=ALU.mult),
              waits=((s_cdve, CNT(s_cdve)),), inc=s_cdve)
            P("vector", lambda e, tt=tt: e.tensor_tensor(
                out=stat[:, 28 + tt:29 + tt], in0=stat[:, 20 + tt:21 + tt],
                in1=stat[:, 20 + tt:21 + tt], op=ALU.mult),
              waits=((s_cdve, CNT(s_cdve)),), inc=s_cdve)
            P("vector", lambda e, tt=tt: e.scalar_tensor_tensor(
                out=stat[:, 28 + tt:29 + tt], in0=stat[:, 12 + tt:13 + tt], scalar=1.0 / C,
                in1=stat[:, 28 + tt:29 + tt], op0=ALU.mult, op1=ALU.subtract),
              waits=((s_cact, asq3), (s_cdve, CNT(s_cdve))), inc=s_cdve)
            dv3 = CNT(s_cdve)
            ln3_st12[tt] = dv3
            P("scalar", lambda e, tt=tt: e.activation(
                out=stat[:, 28 + tt:29 + tt], in_=stat[:, 28 + tt:29 + tt],
                func=AF.Ln, bias=scal_sb[:, 0:1]),
              waits=((s_cdve, dv3), (s_cdve, ln3_st28.get(tt))))
            P("scalar", lambda e, tt=tt: e.activation(
                out=stat[:, 28 + tt:29 + tt], in_=stat[:, 28 + tt:29 + tt],
                func=AF.Exp, scale=-0.5), inc=s_cact)
            ar3 = CNT(s_cact)
            P("vector", lambda e, tt=tt, xb=xb: e.tensor_scalar(
                out=xh2_bf[:, tt * 1024:(tt + 1) * 1024],
                in0=x2t[:, xb * 4096 + tt * 1024:xb * 4096 + (tt + 1) * 1024],
                scalar1=stat[:, 20 + tt:21 + tt], scalar2=stat[:, 28 + tt:29 + tt],
                op0=ALU.subtract, op1=ALU.mult),
              waits=((s_cact, ar3),), inc=s_cdve)
            ln3_st28[tt] = CNT(s_cdve)
            cast_evt2[tt] = CNT(s_cdve)
        for tt in range(4):
            gg = i * 4 + tt
            tb = (gg % 2) * 1024
            for s in range(8):
                cc = s
                P("tensor", lambda e, tt=tt, cc=cc, s=s, tb=tb: e.transpose(
                    out=tp2_ps[:, tb + s * 128:tb + (s + 1) * 128],
                    in_=xh2_bf[:, tt * 1024 + cc * 128:tt * 1024 + (cc + 1) * 128],
                    identity=eye_sb[:]),
                  waits=((s_cdve, cast_evt2[tt]), (s_ctev, 8 * (gg - 1))),
                  inc=s_cpet)
            tr_done2 = CNT(s_cpet)
            for s in range(8):
                cc = s
                P("vector", lambda e, tt=tt, cc=cc, s=s, tb=tb: e.tensor_copy(
                    out=xh2T[:, cc * 512 + tt * 128:cc * 512 + tt * 128 + 128],
                    in_=tp2_ps[:, tb + s * 128:tb + (s + 1) * 128]),
                  waits=((s_cpet, tr_done2), (s_cpool, mixp_hist.get(i - 1))),
                  inc=s_ctev)
        if _pcut < 4:
            continue
        tev2_done = CNT(s_ctev)
        for cc in range(8):
            s0 = cc * 512
            w0 = ((s_carry, 1),) if i == 0 and cc == 0 else ()
            P("vector", lambda e, s0=s0: e.tensor_tensor(
                out=xrT[:, s0 + 1:s0 + 512], in0=xh2T[:, s0 + 1:s0 + 512],
                in1=xh2T[:, s0:s0 + 511], op=ALU.subtract), waits=w0)
            P("vector", lambda e, s0=s0, cc=cc: e.tensor_tensor(
                out=xrT[:, s0:s0 + 1], in0=xh2T[:, s0:s0 + 1],
                in1=carry[:, cc:cc + 1], op=ALU.subtract))
            P("vector", lambda e, s0=s0, cc=cc: e.scalar_tensor_tensor(
                out=xkT[:, s0 + 1:s0 + 512], in0=xrT[:, s0 + 1:s0 + 512],
                scalar=mu_sb[:, 8 + cc:9 + cc], in1=xh2T[:, s0:s0 + 511],
                op0=ALU.mult, op1=ALU.add))
            P("vector", lambda e, s0=s0, cc=cc: e.scalar_tensor_tensor(
                out=xkT[:, s0:s0 + 1], in0=xrT[:, s0:s0 + 1],
                scalar=mu_sb[:, 8 + cc:9 + cc], in1=carry[:, cc:cc + 1],
                op0=ALU.mult, op1=ALU.add),
              inc=s_cdve)
        mixk_done = CNT(s_cdve)
        for cc in range(8):
            s0 = cc * 512
            P("vector", lambda e, s0=s0, cc=cc: e.scalar_tensor_tensor(
                out=xrT[:, s0 + 1:s0 + 512], in0=xrT[:, s0 + 1:s0 + 512],
                scalar=mu_sb[:, cc:cc + 1], in1=xh2T[:, s0:s0 + 511],
                op0=ALU.mult, op1=ALU.add))
            P("vector", lambda e, s0=s0, cc=cc: e.scalar_tensor_tensor(
                out=xrT[:, s0:s0 + 1], in0=xrT[:, s0:s0 + 1],
                scalar=mu_sb[:, cc:cc + 1], in1=carry[:, cc:cc + 1],
                op0=ALU.mult, op1=ALU.add))
            P("vector", lambda e, s0=s0, cc=cc: e.tensor_copy(
                out=carry[:, cc:cc + 1], in_=xh2T[:, s0 + 511:s0 + 512]),
              inc=s_cdve)
        mix_done = CNT(s_cdve)
        mixp_done = CNT(s_cpool)
        mixp_hist[i] = mixp_done
        if _pcut < 5:
            continue
        for dd in range(8):
            bank = dd % 2
            for cc in range(8):
                P("tensor", lambda e, bank=bank, cc=cc, dd=dd: e.matmul(
                    out=bc_ps[:, bank * 512:(bank + 1) * 512],
                    lhsT=wkT_sb[:, cc * 1024 + dd * 128:cc * 1024 + (dd + 1) * 128],
                    rhs=xkT[:, cc * 512:(cc + 1) * 512],
                    start=(cc == 0), stop=(cc == 7)),
                  waits=((s_cdve, mixk_done),
                         (s_cact, relu_evt.get((i, dd - 2)))),
                  inc=(s_cpe if cc == 7 else None))
            pe_k = CNT(s_cpe)
            P("scalar", lambda e, bank=bank, dd=dd: e.activation(
                out=krel_sb[:, (dd % 2) * 512:(dd % 2 + 1) * 512],
                in_=bc_ps[:, bank * 512:(bank + 1) * 512], func=AF.Relu),
              waits=((s_cpe, pe_k), (s_cdve, sq_evt.get((i, dd - 2)))),
              inc=s_cact)
            relu_evt[(i, dd)] = CNT(s_cact)
            P("vector", lambda e, dd=dd: e.tensor_tensor(
                out=kact_sb[:, dd * 512:(dd + 1) * 512],
                in0=krel_sb[:, (dd % 2) * 512:(dd % 2 + 1) * 512],
                in1=krel_sb[:, (dd % 2) * 512:(dd % 2 + 1) * 512], op=ALU.mult),
              waits=((s_cact, relu_evt[(i, dd)]),), inc=s_cdve)
            sq_evt[(i, dd)] = CNT(s_cdve)
        kact_done = CNT(s_cdve)
        if _pcut < 6:
            continue
        pe_r_evt = {}
        pe_v_evt = {}

        def emit_rmm(tt, i=i, mix_done=mix_done):
            for half in range(2):
                bank = (tt % 2) * 2 + half
                for cc in range(8):
                    P("tensor", lambda e, bank=bank, cc=cc, tt=tt, half=half: e.matmul(
                        out=big_ps[:, bank * 512:(bank + 1) * 512],
                        lhsT=xrT[:, cc * 512 + tt * 128:cc * 512 + tt * 128 + 128],
                        rhs=wrT_sb[:, cc * 1024 + half * 512:cc * 1024 + (half + 1) * 512],
                        start=(cc == 0), stop=(cc == 7)),
                      waits=((s_cdve, mix_done),
                             (s_cdve, anrm_evt.get((i, tt + 2)) if tt < 2 else None),
                             (s_cdve, ffn_evt.get((i, tt - 2)) if tt >= 2 else None)),
                      inc=(s_cpe if cc == 7 else None))
            pe_r_evt[tt] = CNT(s_cpe)

        def emit_sig(tt, i=i, kact_done=kact_done):
            bank0 = (tt % 2) * 2
            P("scalar", lambda e, bank0=bank0: e.activation(
                out=r_sb[:], in_=big_ps[:, bank0 * 512:(bank0 + 2) * 512],
                func=AF.Exp, scale=-1.0),
              waits=((s_cpe, pe_r_evt[tt]),
                     (s_cdve, rmul_evt.get((i, tt - 1)))),
              inc=s_cact)
            sig_evt[(i, tt)] = CNT(s_cact)
            P("scalar", lambda e: e.activation(
                out=r_sb[:], in_=r_sb[:], func=AF.Ln, bias=scal_sb[:, 1:2]))
            P("scalar", lambda e: e.activation(
                out=r_sb[:], in_=r_sb[:], func=AF.Exp, scale=-1.0), inc=s_cact)
            sig_evt[(i, tt, "r")] = CNT(s_cact)

        def emit_vmm(tt, i=i, kact_done=kact_done):
            for half in range(2):
                bank = (tt % 2) * 2 + half
                for cc in range(8):
                    P("tensor", lambda e, bank=bank, cc=cc, tt=tt, half=half: e.matmul(
                        out=big_ps[:, bank * 512:(bank + 1) * 512],
                        lhsT=kact_sb[:, cc * 512 + tt * 128:cc * 512 + tt * 128 + 128],
                        rhs=wvT_sb[:, cc * 1024 + half * 512:cc * 1024 + (half + 1) * 512],
                        start=(cc == 0), stop=(cc == 7)),
                      waits=((s_cdve, kact_done), (s_cact, sig_evt[(i, tt)]),),
                      inc=(s_cpe if cc == 7 else None))
            pe_v_evt[tt] = CNT(s_cpe)

        def emit_ffnmul(tt, i=i):
            bank0 = (tt % 2) * 2
            P("vector", lambda e, tt=tt, bank0=bank0: e.tensor_tensor(
                out=ffn_sb[:, tt * 1024:(tt + 1) * 1024], in0=r_sb[:],
                in1=big_ps[:, bank0 * 512:(bank0 + 2) * 512], op=ALU.mult),
              waits=((s_cpe, pe_v_evt[tt]), (s_cact, sig_evt[(i, tt, "r")])),
              inc=s_cdve)
            ffn_evt[(i, tt)] = CNT(s_cdve)
            rmul_evt[(i, tt)] = ffn_evt[(i, tt)]

        emit_rmm(0)
        emit_sig(0)
        emit_rmm(1)
        emit_vmm(0)
        emit_ffnmul(0)
        emit_sig(1)
        emit_vmm(1)
        emit_ffnmul(1)
        emit_rmm(2)
        emit_sig(2)
        emit_rmm(3)
        emit_vmm(2)
        emit_ffnmul(2)
        emit_sig(3)
        emit_vmm(3)
        emit_ffnmul(3)
        ffn_pe_done[i] = CNT(s_cpe)
        if _pcut < 7:
            continue
        for tt in range(4):
            P("vector", lambda e, tt=tt: e.reduce_sum(
                out=stat[:, 32 + tt:33 + tt], in_=ffn_sb[:, tt * 1024:(tt + 1) * 1024],
                axis=AXX),
              waits=((s_cdve, ffn_evt[(i, tt)]),), inc=s_cdve)
            P("scalar", lambda e, tt=tt: e.activation(
                out=junk_a[:], in_=ffn_sb[:, tt * 1024:(tt + 1) * 1024],
                func=AF.Square, accum_out=stat[:, 36 + tt:37 + tt]),
              waits=((s_cdve, ffn_evt[(i, tt)]), (s_cdve, ln4_var_free[0])), inc=s_cact)
        asq3 = CNT(s_cact)
        P("vector", lambda e: e.tensor_scalar(
            out=stat[:, 40:44], in0=stat[:, 32:36], scalar1=1.0 / C, scalar2=None,
            op0=ALU.mult),
          waits=((s_cdve, CNT(s_cdve)),), inc=s_cdve)
        P("vector", lambda e: e.tensor_tensor(
            out=stat[:, 44:48], in0=stat[:, 40:44], in1=stat[:, 40:44], op=ALU.mult),
          waits=((s_cdve, CNT(s_cdve)),), inc=s_cdve)
        P("vector", lambda e: e.scalar_tensor_tensor(
            out=stat[:, 44:48], in0=stat[:, 36:40], scalar=1.0 / C, in1=stat[:, 44:48],
            op0=ALU.mult, op1=ALU.subtract),
          waits=((s_cact, asq3), (s_cdve, CNT(s_cdve))), inc=s_cdve)
        dv3 = CNT(s_cdve)
        ln4_var_free[0] = dv3
        P("scalar", lambda e: e.activation(out=stat[:, 44:48], in_=stat[:, 44:48],
                                           func=AF.Ln, bias=scal_sb[:, 0:1]),
          waits=((s_cdve, dv3), (s_cdve, ln4_rstd_free[0])))
        P("scalar", lambda e: e.activation(out=stat[:, 44:48], in_=stat[:, 44:48],
                                           func=AF.Exp, scale=-0.5), inc=s_cact)
        ar3 = CNT(s_cact)
        for tt in range(4):
            ost = (i * 4 + tt) % 2
            P("vector", lambda e, tt=tt: e.tensor_scalar(
                out=anrm_sb[:], in0=ffn_sb[:, tt * 1024:(tt + 1) * 1024],
                scalar1=stat[:, 40 + tt:41 + tt], scalar2=stat[:, 44 + tt:45 + tt],
                op0=ALU.subtract, op1=ALU.mult),
              waits=((s_cact, ar3),))
            P("vector", lambda e, tt=tt, xb=xb: e.tensor_tensor(
                out=x2t[:, xb * 4096 + tt * 1024:xb * 4096 + (tt + 1) * 1024],
                in0=anrm_sb[:],
                in1=x2t[:, xb * 4096 + tt * 1024:xb * 4096 + (tt + 1) * 1024], op=ALU.add),
              inc=s_cdve)
            od = CNT(s_cdve)
            prog.dma("gpsimd", lambda e, t0=t0, tt=tt, xb=xb: e.dma_start(
                out=out_d[t0 + tt * 128:t0 + (tt + 1) * 128, :],
                in_=x2t[:, xb * 4096 + tt * 1024:xb * 4096 + (tt + 1) * 1024]),
              s_ost[ost], waits=((s_cdve, od),))
        x_free[i] = CNT(s_cdve)
        ln4_rstd_free[0] = CNT(s_cdve)
        store_done[i] = ((s_ost[0], CNT(s_ost[0])), (s_ost[1], CNT(s_ost[1])))

    if _pcut >= 7:
        P("sync", lambda e: e.nop(),
          waits=((s_ost[0], CNT(s_ost[0])), (s_ost[1], CNT(s_ost[1]))))
    else:
        P("sync", lambda e: e.nop(), waits=((s_cdve, CNT(s_cdve)),))

    prog.flush()
    es_c.close()
    es0.close()
    nc.finalize()
    return nc


_NC_CACHE = {}


def get_nc(T):
    if T not in _NC_CACHE:
        _NC_CACHE[T] = build_nc(T)
    return _NC_CACHE[T]


def kernel(L0_attn_x, L0_attn_w_qkv, L0_attn_w_out, L0_attn_head_scales,
           L0_ffn_W_r, L0_ffn_W_k, L0_ffn_W_v, L0_ffn_mu_r, L0_ffn_mu_k):
    x = np.ascontiguousarray(np.asarray(L0_attn_x, dtype=np.float32))
    B, T, _ = x.shape
    nc = get_nc(T)
    consts = host_consts()
    base = {
        "w_qkv": np.ascontiguousarray(np.asarray(L0_attn_w_qkv, np.float32)),
        "w_out": np.ascontiguousarray(np.asarray(L0_attn_w_out, np.float32)),
        "head_scales": np.ascontiguousarray(
            np.asarray(L0_attn_head_scales, np.float32).reshape(1, 8)),
        "w_r": np.ascontiguousarray(np.asarray(L0_ffn_W_r, np.float32).T),
        "w_k": np.ascontiguousarray(np.asarray(L0_ffn_W_k, np.float32).T),
        "w_v": np.ascontiguousarray(np.asarray(L0_ffn_W_v, np.float32).T),
        "mu_r": np.ascontiguousarray(np.asarray(L0_ffn_mu_r, np.float32)),
        "mu_k": np.ascontiguousarray(np.asarray(L0_ffn_mu_k, np.float32)),
        "eye_bf": consts["eye_bf"], "ones_bf": consts["ones_bf"],
        "ones_f": consts["ones_f"], "sel8": consts["sel8"],
        "dirs33": consts["dirs33"], "d3s": consts["d3s"],
        "masks": consts["masks"], "scal": consts["scal"],
    }
    in_maps = [dict(base, x=np.ascontiguousarray(x[b])) for b in range(B)]
    res = run_bass_kernel_spmd(nc, in_maps, core_ids=list(range(B)))
    return np.stack([np.asarray(r["out"], np.float32) for r in res.results], axis=0)


# version 18 (final)
# speedup vs baseline: 1.0695x; 1.0606x over previous
import math
import os
import numpy as np
import ml_dtypes
from contextlib import ExitStack

import concourse.bass as bass
import concourse.mybir as mybir
from concourse.bass_utils import run_bass_kernel_spmd

F32 = mybir.dt.float32
BF16 = mybir.dt.bfloat16
AF = mybir.ActivationFunctionType
ALU = mybir.AluOpType
AXX = mybir.AxisListType.X

C = 1024
H = 8
SCALE = 1.0 / math.sqrt(128.0)
EPS = 1e-5

ENGINES = ("sync", "tensor", "scalar", "vector", "gpsimd")


class Prog:
    def __init__(self, nc):
        self.nc = nc
        self.q = {e: [] for e in ENGINES}
        self.sems = {}
        self.cnt = {}
        self.last_wait = {}

    def sem(self, name, es):
        h = es.enter_context(self.nc.semaphore(name))
        self.sems[name] = h
        self.cnt[name] = 0
        return name

    def op(self, eng, fn, waits=(), inc=None, incn=1):
        wl = []
        for s, v in waits:
            if v is None or v <= 0:
                continue
            k = (eng, s)
            if self.last_wait.get(k, -1) >= v:
                continue
            self.last_wait[k] = v
            wl.append((self.sems[s], v))
        if inc is not None:
            self.cnt[inc] += incn
        self.q[eng].append((fn, wl, self.sems[inc] if inc else None, incn))

    def dma(self, eng, fn, stream, waits=()):
        self.op(eng, fn, waits=tuple(waits) + ((stream, self.cnt[stream]),),
                inc=stream, incn=16)

    def emit(self, block):
        def make_runner(eng_name):
            def run(eng):
                for fn, wl, inc, incn in self.q[eng_name]:
                    for h, v in wl:
                        eng.wait_ge(h, v)
                    ins = fn(eng)
                    if inc is not None:
                        if ins is None:
                            ins = eng.nop()
                        ins.then_inc(inc, incn)
            return run
        block.sync(make_runner("sync"))
        block.tensor(make_runner("tensor"))
        block.scalar(make_runner("scalar"))
        block.vector(make_runner("vector"))
        block.gpsimd(make_runner("gpsimd"))

    def flush(self):
        if any(self.q.values()):
            with self.nc.Block() as block:
                self.emit(block)
        self.q = {e: [] for e in ENGINES}


def host_consts():
    dirs = np.array(
        [[(i >> 2 & 1) * 2 - 1, (i >> 1 & 1) * 2 - 1, (i & 1) * 2 - 1] for i in range(8)],
        dtype=np.float32) / np.float32(np.sqrt(3.0, dtype=np.float32))
    c = {}
    c["eye_bf"] = np.eye(128, dtype=np.float32).astype(ml_dtypes.bfloat16)
    c["ones_bf"] = np.ones((128, 128), dtype=ml_dtypes.bfloat16)
    c["ones_f"] = np.ones((128, 128), dtype=np.float32)
    sel = np.zeros((8, 8 * 128), dtype=np.float32)
    for cc in range(8):
        sel[cc, cc * 128:(cc + 1) * 128] = 1.0
    c["sel8"] = sel.astype(ml_dtypes.bfloat16)
    d33 = np.zeros((3, 24), dtype=np.float32)
    for h in range(8):
        for j in range(3):
            d33[:, 3 * h + j] = dirs[h]
    c["dirs33"] = d33.astype(ml_dtypes.bfloat16)
    c["d3s"] = np.ascontiguousarray((dirs.T * np.float32(np.sqrt(128.0))).astype(np.float32))
    m = np.zeros((128, 4 * 512), dtype=np.float32)
    kk = np.arange(128).reshape(128, 1)
    qq = np.arange(512).reshape(1, 512)
    for j in range(4):
        m[:, j * 512:(j + 1) * 512] = (128 * j + kk <= qq).astype(np.float32)
    c["masks"] = m.astype(ml_dtypes.bfloat16)
    sc = np.zeros((128, 8), dtype=np.float32)
    sc[:, 0] = EPS
    sc[:, 1] = 1.0
    c["scal"] = sc
    return c


def build_nc(T, phases="abc", dump=False):
    assert T % 512 == 0
    NT = T // 512
    NQT = NT
    nc = bass.Bass("TRN2", target_bir_lowering=False)

    x_in = nc.dram_tensor("x", [T, C], F32, kind="ExternalInput").ap()
    w_qkv = nc.dram_tensor("w_qkv", [C, 3 * C], F32, kind="ExternalInput").ap()
    w_out = nc.dram_tensor("w_out", [C, C], F32, kind="ExternalInput").ap()
    hs_in = nc.dram_tensor("head_scales", [1, H], F32, kind="ExternalInput").ap()
    w_r = nc.dram_tensor("w_r", [C, C], F32, kind="ExternalInput").ap()
    w_k = nc.dram_tensor("w_k", [C, C], F32, kind="ExternalInput").ap()
    w_v = nc.dram_tensor("w_v", [C, C], F32, kind="ExternalInput").ap()
    mu_r_in = nc.dram_tensor("mu_r", [C], F32, kind="ExternalInput").ap()
    mu_k_in = nc.dram_tensor("mu_k", [C], F32, kind="ExternalInput").ap()
    eye_d = nc.dram_tensor("eye_bf", [128, 128], BF16, kind="ExternalInput").ap()
    onesb_d = nc.dram_tensor("ones_bf", [128, 128], BF16, kind="ExternalInput").ap()
    onesf_d = nc.dram_tensor("ones_f", [128, 128], F32, kind="ExternalInput").ap()
    sel8_d = nc.dram_tensor("sel8", [8, 8 * 128], BF16, kind="ExternalInput").ap()
    dirs33_d = nc.dram_tensor("dirs33", [3, 24], BF16, kind="ExternalInput").ap()
    d3s_d = nc.dram_tensor("d3s", [3, 8], F32, kind="ExternalInput").ap()
    masks_d = nc.dram_tensor("masks", [128, 4 * 512], BF16, kind="ExternalInput").ap()
    scal_d = nc.dram_tensor("scal", [128, 8], F32, kind="ExternalInput").ap()
    out_d = nc.dram_tensor("out", [T, C], F32, kind="ExternalOutput").ap()
    skind = "ExternalOutput" if dump else "Internal"
    dbg = nc.dram_tensor("dbg", [128, 32], F32, kind="ExternalOutput").ap() if dump else None
    q_scr = nc.dram_tensor("q_scr", [H, 128, T], BF16, kind=skind).ap()
    k_scr = nc.dram_tensor("k_scr", [H, 128, T], BF16, kind=skind).ap()
    v_scr = nc.dram_tensor("v_scr", [T, C], BF16, kind=skind).ap()
    o_scr = nc.dram_tensor("o_scr", [C, T], BF16, kind=skind).ap()
    l_scr = nc.dram_tensor("l_scr", [H, T], F32, kind=skind).ap()

    es0 = ExitStack()

    def SB(stk, name, shape, dt):
        return stk.enter_context(nc.sbuf_tensor(name, shape, dt))

    def PS(stk, name, shape, dt):
        return stk.enter_context(nc.psum_tensor(name, shape, dt))

    eye_sb = SB(es0, "eye_sb", [128, 128], BF16)
    onesb_sb = SB(es0, "onesb_sb", [128, 128], BF16)
    onesf_sb = SB(es0, "onesf_sb", [128, 128], F32)
    scal_sb = SB(es0, "scal_sb", [128, 8], F32)
    mu_sb = SB(es0, "mu_sb", [128, 16], F32)
    negB = SB(es0, "negB", [128, H * NQT], F32)
    wout_sb = SB(es0, "wout_sb", [128, 8 * 1024], BF16)
    wrT_sb = SB(es0, "wrT_sb", [128, 8 * 1024], BF16)
    wkT_sb = SB(es0, "wkT_sb", [128, 8 * 1024], BF16)
    wvT_sb = SB(es0, "wvT_sb", [128, 8 * 1024], BF16)
    junk_a = SB(es0, "junk_a", [128, 1024], BF16)
    stat = SB(es0, "stat", [128, 48], F32)

    prog = Prog(nc)
    P = prog.op
    S = lambda n: prog.sem(n, es0)
    CNT = lambda s: prog.cnt[s]

    bar_i = [0]

    def barrier():
        s = S(f"bar{bar_i[0]}")
        bar_i[0] += 1
        for e in ENGINES:
            P(e, lambda eng: eng.nop(), inc=s)
        for e in ENGINES:
            P(e, lambda eng: eng.nop(), waits=((s, len(ENGINES)),))

    es_ia = ExitStack()
    wqkv_sb = SB(es_ia, "wqkv_sb", [128, 8 * 3072], BF16)
    dirs33_sb = SB(es_ia, "dirs33_sb", [3, 24], BF16)
    d3s_sb = SB(es_ia, "d3s_sb", [3, 8], F32)
    c3_sb = SB(es_ia, "c3_sb", [3, 8], F32)
    hsrow = SB(es_ia, "hsrow", [1, 8], F32)
    tp_ps = PS(es_ia, "tp_ps", [128, 2048], BF16)
    proj_ps = PS(es_ia, "proj_ps", [128, 512], F32)

    s_ic = S("ic")
    s_ipe = S("ipe")
    s_ism = S("ism")
    s_wst = [S("wst0"), S("wst1")]

    small = [
        (eye_sb, eye_d), (onesb_sb, onesb_d), (onesf_sb, onesf_d),
        (dirs33_sb, dirs33_d), (d3s_sb, d3s_d), (scal_sb, scal_d), (hsrow, hs_in),
    ]
    for dst, src in small:
        prog.dma("sync", lambda e, d=dst, s=src: e.dma_start(out=d[:], in_=s), s_ism)
    for cc in range(8):
        prog.dma("sync", lambda e, cc=cc: e.dma_start(
            out=mu_sb[:, cc:cc + 1],
            in_=mu_r_in[cc * 128:(cc + 1) * 128].rearrange("(p o) -> p o", o=1)), s_ism)
        prog.dma("sync", lambda e, cc=cc: e.dma_start(
            out=mu_sb[:, 8 + cc:9 + cc],
            in_=mu_k_in[cc * 128:(cc + 1) * 128].rearrange("(p o) -> p o", o=1)), s_ism)
    ism_total = CNT(s_ism)

    es_i = ExitStack()
    wstg = SB(es_i, "wstg", [128, 2 * 3072], F32)
    wtmp_bf = SB(es_i, "wtmp_bf", [128, 2 * 1024], BF16)
    _cut = int(os.environ.get("INIT_CUT", "9"))

    cast_evt = {}
    ci = 0
    for cc in range(8 if _cut >= 2 else 0):
        buf = ci % 2
        prog.dma("sync", lambda e, cc=cc, buf=buf: e.dma_start(
            out=wstg[:, buf * 3072:(buf + 1) * 3072],
            in_=w_qkv[cc * 128:(cc + 1) * 128, :]),
          s_wst[buf], waits=((s_ic, cast_evt.get(ci - 2)),))
        P("vector", lambda e, cc=cc, buf=buf: e.tensor_copy(
            out=wqkv_sb[:, cc * 3072:(cc + 1) * 3072],
            in_=wstg[:, buf * 3072:(buf + 1) * 3072]),
          waits=((s_wst[buf], CNT(s_wst[buf])),), inc=s_ic)
        cast_evt[ci] = CNT(s_ic)
        ci += 1
    for cc in range(8 if _cut >= 3 else 0):
        buf = ci % 2
        prog.dma("sync", lambda e, cc=cc, buf=buf: e.dma_start(
            out=wstg[:, buf * 3072:buf * 3072 + 1024],
            in_=w_out[cc * 128:(cc + 1) * 128, :]),
          s_wst[buf], waits=((s_ic, cast_evt.get(ci - 2)),))
        P("vector", lambda e, cc=cc, buf=buf: e.tensor_copy(
            out=wout_sb[:, cc * 1024:(cc + 1) * 1024],
            in_=wstg[:, buf * 3072:buf * 3072 + 1024]),
          waits=((s_wst[buf], CNT(s_wst[buf])),), inc=s_ic)
        cast_evt[ci] = CNT(s_ic)
        ci += 1
    for (wsrc, wdst) in ([(w_r, wrT_sb), (w_k, wkT_sb), (w_v, wvT_sb)] if _cut >= 4 else []):
        for cc in range(8):
            buf = ci % 2
            prog.dma("sync", lambda e, wsrc=wsrc, cc=cc, buf=buf: e.dma_start(
                out=wstg[:, buf * 3072:buf * 3072 + 1024],
                in_=wsrc[cc * 128:(cc + 1) * 128, :]),
              s_wst[buf], waits=((s_ic, cast_evt.get(ci - 2)),))
            P("vector", lambda e, wdst=wdst, cc=cc, buf=buf: e.tensor_copy(
                out=wdst[:, cc * 1024:(cc + 1) * 1024],
                in_=wstg[:, buf * 3072:buf * 3072 + 1024]),
              waits=((s_wst[buf], CNT(s_wst[buf])),), inc=s_ic)
            cast_evt[ci] = CNT(s_ic)
            ci += 1
    if _cut >= 5:
        P("tensor", lambda e: e.matmul(out=proj_ps[0:3, 0:8], lhsT=onesf_sb[0:1, 0:3],
                                       rhs=hsrow[:, :], start=True, stop=True),
          waits=((s_ism, ism_total),), inc=s_ipe)
        P("vector", lambda e: e.tensor_tensor(out=c3_sb[:, :], in0=proj_ps[0:3, 0:8],
                                              in1=d3s_sb[:, :], op=ALU.mult),
          waits=((s_ipe, CNT(s_ipe)),), inc=s_ic)

    P("sync", lambda e: e.nop(), waits=((s_ism, ism_total),))
    barrier()
    prog.flush()
    es_i.close()

    if "a" not in phases:
        prog.flush()
        es_ia.close()
        es0.close()
        nc.finalize()
        return nc

    s_ax = [S("ax0"), S("ax1")]
    s_dbg = S("dbg0")
    s_aact = S("aact")
    s_adve = S("adve")
    s_ape = S("ape")
    s_apet = S("apet")
    s_atev = S("atev")
    s_qs = [S("qs0"), S("qs1")]
    s_ks = [S("ks0"), S("ks1")]
    s_vs = [S("vs0"), S("vs1")]

    es_a = ExitStack()
    xt = SB(es_a, "xt", [128, 2 * 4096], F32)
    xhat_bf = SB(es_a, "xhat_bf", [128, 2 * 4096], BF16)
    xhatT = SB(es_a, "xhatT", [128, 8 * 512], BF16)
    qk_ps = PS(es_a, "qk_ps", [128, 2048], F32)
    nrm_ps = PS(es_a, "nrm_ps", [128, 512], F32)
    qkst = SB(es_a, "qkst", [128, 4 * 512], BF16)
    qdm = SB(es_a, "qdm", [128, 8 * 512], BF16)
    kdm = SB(es_a, "kdm", [128, 8 * 512], BF16)
    vbuf = SB(es_a, "vbuf", [128, 2 * 1024], BF16)
    sqv = SB(es_a, "sqv", [128, 4 * 512], BF16)
    mq2 = SB(es_a, "mq2", [128, 8], F32)
    mk2run = SB(es_a, "mk2run", [128, 8], F32)
    b2t = SB(es_a, "b2t", [128, 8], F32)
    v_ps = qk_ps

    qk_evac2 = {}
    qk_tp_evt = {}
    fix_evt = {}
    v_evac = {}
    ax_done = {}
    dve_var = {}
    act_sq = {}
    act_rstd = {}
    cast_done = {}
    xtp_done = {}
    perhead_done = {}
    qspl = {}
    zfree = {0: 0, 1: 0, 2: 0, 3: 0}
    nblk = [0]

    def emit_xdma(j):
        xb = j % 2
        t0j = j * 512
        prog.dma("sync", lambda e, t0j=t0j, xb=xb: e.dma_start(
            out=xt[:, xb * 4096:(xb + 1) * 4096].rearrange("p (c f) -> p c f", c=4),
            in_=x_in[t0j:t0j + 512, :].rearrange("(c p) f -> p c f", p=128)),
          s_ax[xb], waits=((s_adve, cast_done.get(j - 2)),))
        ax_done[j] = CNT(s_ax[xb])

    def emit_ln_p1(j):
        xb = j % 2
        for tt in range(4):
            P("vector", lambda e, xb=xb, tt=tt: e.reduce_sum(
                out=stat[:, tt:tt + 1],
                in_=xt[:, xb * 4096 + tt * 1024:xb * 4096 + (tt + 1) * 1024], axis=AXX),
              waits=((s_ax[xb], ax_done[j]),), inc=s_adve)

    def emit_ln_p2(j):
        xb = j % 2
        for tt in range(4):
            P("scalar", lambda e, xb=xb, tt=tt: e.activation(
                out=junk_a[:], in_=xt[:, xb * 4096 + tt * 1024:xb * 4096 + (tt + 1) * 1024],
                func=AF.Square, accum_out=stat[:, 8 + tt:9 + tt]),
              waits=((s_ax[xb], ax_done[j]), (s_adve, dve_var.get(j - 1))),
              inc=s_aact)
        act_sq[j] = CNT(s_aact)
        P("vector", lambda e: e.tensor_scalar(
            out=stat[:, 16:20], in0=stat[:, 0:4], scalar1=1.0 / C, scalar2=None, op0=ALU.mult),
          waits=((s_adve, CNT(s_adve)),), inc=s_adve)
        P("vector", lambda e: e.tensor_tensor(
            out=stat[:, 24:28], in0=stat[:, 16:20], in1=stat[:, 16:20], op=ALU.mult),
          waits=((s_adve, CNT(s_adve)),), inc=s_adve)
        P("vector", lambda e: e.scalar_tensor_tensor(
            out=stat[:, 24:28], in0=stat[:, 8:12], scalar=1.0 / C, in1=stat[:, 24:28],
            op0=ALU.mult, op1=ALU.subtract),
          waits=((s_aact, act_sq[j]), (s_adve, CNT(s_adve))), inc=s_adve)
        dve_var[j] = CNT(s_adve)
        P("scalar", lambda e: e.activation(out=stat[:, 24:28], in_=stat[:, 24:28],
                                           func=AF.Ln, bias=scal_sb[:, 0:1]),
          waits=((s_adve, dve_var[j]),))
        P("scalar", lambda e: e.activation(out=stat[:, 24:28], in_=stat[:, 24:28],
                                           func=AF.Exp, scale=-0.5), inc=s_aact)
        act_rstd[j] = CNT(s_aact)

    def emit_ln_p3(j):
        xb = j % 2
        for tt in range(4):
            P("vector", lambda e, xb=xb, tt=tt: e.tensor_scalar(
                out=xhat_bf[:, xb * 4096 + tt * 1024:xb * 4096 + (tt + 1) * 1024],
                in0=xt[:, xb * 4096 + tt * 1024:xb * 4096 + (tt + 1) * 1024],
                scalar1=stat[:, 16 + tt:17 + tt], scalar2=stat[:, 24 + tt:25 + tt],
                op0=ALU.subtract, op1=ALU.mult),
              waits=((s_aact, act_rstd[j]), (s_adve, CNT(s_adve)),
                     (s_apet, xtp_done.get(j - 2))),
              inc=s_adve)
        cast_done[j] = CNT(s_adve)

    emit_xdma(0)
    emit_ln_p1(0)
    emit_ln_p2(0)
    emit_ln_p3(0)

    for i in range(NT):
        t0 = i * 512
        xb = i % 2
        if i + 1 < NT:
            emit_xdma(i + 1)
        for g in range(4):
            gg = i * 4 + g
            tb = (gg % 2) * 1024
            z0, z1 = 2 * (gg % 2), 2 * (gg % 2) + 1
            for s in range(8):
                j = g * 8 + s
                tt, cc = divmod(j, 8)
                P("tensor", lambda e, tt=tt, cc=cc, s=s, tb=tb, xb=xb: e.transpose(
                    out=tp_ps[:, tb + s * 128:tb + (s + 1) * 128],
                    in_=xhat_bf[:, xb * 4096 + tt * 1024 + cc * 128:xb * 4096 + tt * 1024 + (cc + 1) * 128],
                    identity=eye_sb[:]),
                  waits=((s_adve, cast_done[i]), (s_atev, max(zfree[z0], zfree[z1]))),
                  inc=s_apet)
            tr_done = CNT(s_apet)
            for s in range(8):
                j = g * 8 + s
                tt, cc = divmod(j, 8)
                P("vector", lambda e, tt=tt, cc=cc, s=s, tb=tb: e.tensor_copy(
                    out=xhatT[:, cc * 512 + tt * 128:cc * 512 + tt * 128 + 128],
                    in_=tp_ps[:, tb + s * 128:tb + (s + 1) * 128]),
                  waits=((s_apet, tr_done),), inc=s_atev)
            zfree[z0] = zfree[z1] = CNT(s_atev)
        tev_done = CNT(s_atev)
        xtp_done[i] = CNT(s_apet)
        def emit_qktp(ci2, i=i):
            tt, rest = divmod(ci2, 4)
            which, g = divmod(rest, 2)
            sl = ci2 % 4
            zq = ci2 % 4
            tb2 = zq * 512
            for j in range(4):
                P("tensor", lambda e, sl=sl, j=j, tb2=tb2: e.transpose(
                    out=tp_ps[:, tb2 + j * 128:tb2 + (j + 1) * 128],
                    in_=qkst[:, sl * 512 + j * 128:sl * 512 + (j + 1) * 128],
                    identity=eye_sb[:]),
                  waits=((s_aact, qk_evac2[(i, ci2)]), (s_atev, zfree[zq])),
                  inc=s_apet)
            qk_tp_evt[(i, ci2)] = CNT(s_apet)
            for j in range(4):
                h2 = g * 4 + j
                dst = qdm if which == 0 else kdm
                P("vector", lambda e, dst=dst, h2=h2, j=j, tb2=tb2, tt=tt: e.tensor_copy(
                    out=dst[:, h2 * 512 + tt * 128:h2 * 512 + tt * 128 + 128],
                    in_=tp_ps[:, tb2 + j * 128:tb2 + (j + 1) * 128]),
                  waits=((s_apet, qk_tp_evt[(i, ci2)]),
                         (s_aact, perhead_done.get(i - 1)))
                        + qspl.get(i - 1, ()),
                  inc=s_atev)
            zfree[zq] = CNT(s_atev)

        for ci2 in range(16):
            tt, rest = divmod(ci2, 4)
            which, g = divmod(rest, 2)
            bank = ci2 % 4
            sl = ci2 % 4
            wv_extra = (v_evac.get(4 * (i - 1) + 2 + ci2 // 2) if ci2 < 4 else None)
            cbase = which * 1024 + g * 512
            for cc in range(8):
                P("tensor", lambda e, bank=bank, cc=cc, tt=tt, cbase=cbase: e.matmul(
                    out=qk_ps[:, bank * 512:(bank + 1) * 512],
                    lhsT=xhatT[:, cc * 512 + tt * 128:cc * 512 + tt * 128 + 128],
                    rhs=wqkv_sb[:, cc * 3072 + cbase:cc * 3072 + cbase + 512],
                    start=(cc == 0), stop=(cc == 7)),
                  waits=((s_atev, tev_done),
                         (s_aact, qk_evac2.get((i, ci2 - 4))),
                         (s_aact, wv_extra)),
                  inc=(s_ape if cc == 7 else None))
            pe_mm = CNT(s_ape)
            P("scalar", lambda e, sl=sl, bank=bank, which=which: e.activation(
                out=qkst[:, sl * 512:(sl + 1) * 512],
                in_=qk_ps[:, bank * 512:(bank + 1) * 512],
                func=AF.Copy, scale=(SCALE if which == 0 else 1.0)),
              waits=((s_ape, pe_mm), (s_apet, qk_tp_evt.get((i, ci2 - 4)))),
              inc=s_aact)
            qk_evac2[(i, ci2)] = CNT(s_aact)
            if ci2 >= 1:
                emit_qktp(ci2 - 1)
        emit_qktp(15)
        tev_qk = CNT(s_atev)
        nrm_evt = {}
        sq_evt2 = {}
        for h in range(H):
            if i + 1 < NT:
                if h == 2:
                    emit_ln_p1(i + 1)
                elif h == 4:
                    emit_ln_p2(i + 1)
                elif h == 6:
                    emit_ln_p3(i + 1)
            gh = i * H + h
            hs = h * 512
            P("tensor", lambda e, h=h, hs=hs: e.matmul(
                out=proj_ps[0:3, :], lhsT=dirs33_sb[:, 3 * h:3 * h + 3],
                rhs=qdm[0:3, hs:hs + 512], start=True, stop=True),
              waits=((s_atev, tev_qk), (s_adve, fix_evt.get(gh - 1))),
              inc=s_ape)
            P("vector", lambda e, h=h, hs=hs: e.scalar_tensor_tensor(
                out=qdm[0:3, hs:hs + 512],
                in0=proj_ps[0:3, :], scalar=c3_sb[:, h:h + 1],
                in1=qdm[0:3, hs:hs + 512],
                op0=ALU.mult, op1=ALU.add),
              waits=((s_ape, CNT(s_ape)),), inc=s_adve)
            fix_evt[gh] = CNT(s_adve)
            prog.dma("sync", lambda e, h=h, hs=hs, t0=t0: e.dma_start(
                out=q_scr[h, :, t0:t0 + 512], in_=qdm[:, hs:hs + 512]),
              s_qs[h % 2], waits=((s_adve, fix_evt[gh]),))
            prog.dma("sync", lambda e, h=h, hs=hs, t0=t0: e.dma_start(
                out=k_scr[h, :, t0:t0 + 512], in_=kdm[:, hs:hs + 512]),
              s_ks[h % 2], waits=((s_atev, tev_qk),))
        for idx in range(16):
            h = idx // 2
            which = idx % 2
            sl4 = idx % 4
            src = qdm if which == 0 else kdm
            hs = h * 512
            P("scalar", lambda e, src=src, hs=hs, sl4=sl4: e.activation(
                out=sqv[:, sl4 * 512:(sl4 + 1) * 512], in_=src[:, hs:hs + 512],
                func=AF.Square),
              waits=(((s_adve, fix_evt[i * H + h]) if which == 0 else (s_atev, tev_qk)),
                     (s_ape, nrm_evt.get(idx - 4))),
              inc=s_aact)
            sq_evt2[idx] = CNT(s_aact)
            P("tensor", lambda e, sl4=sl4: e.matmul(
                out=nrm_ps[:], lhsT=onesb_sb[:], rhs=sqv[:, sl4 * 512:(sl4 + 1) * 512],
                start=True, stop=True),
              waits=((s_aact, sq_evt2[idx]), (s_adve, CNT(s_adve))), inc=s_ape)
            nrm_evt[idx] = CNT(s_ape)
            if which == 0:
                P("vector", lambda e, h=h: e.reduce_max(
                    out=mq2[:, h:h + 1], in_=nrm_ps[:], axis=AXX),
                  waits=((s_ape, nrm_evt[idx]),), inc=s_adve)
            else:
                P("vector", lambda e, h=h: e.reduce_max(
                    out=b2t[:, h:h + 1], in_=nrm_ps[:], axis=AXX),
                  waits=((s_ape, nrm_evt[idx]),), inc=s_adve)
                if i == 0:
                    P("vector", lambda e, h=h: e.tensor_copy(
                        out=mk2run[:, h:h + 1], in_=b2t[:, h:h + 1]),
                      waits=((s_adve, CNT(s_adve)),), inc=s_adve)
                else:
                    P("vector", lambda e, h=h: e.tensor_tensor(
                        out=mk2run[:, h:h + 1], in0=mk2run[:, h:h + 1],
                        in1=b2t[:, h:h + 1], op=ALU.max),
                      waits=((s_adve, CNT(s_adve)),), inc=s_adve)
        perhead_done[i] = CNT(s_aact)
        qspl[i] = ((s_qs[0], CNT(s_qs[0])), (s_qs[1], CNT(s_qs[1])),
                   (s_ks[0], CNT(s_ks[0])), (s_ks[1], CNT(s_ks[1])))
        P("vector", lambda e: e.tensor_tensor(out=b2t[:], in0=mq2[:], in1=mk2run[:],
                                              op=ALU.mult),
          waits=((s_adve, CNT(s_adve)),), inc=s_adve)
        dve_b2 = CNT(s_adve)
        P("scalar", lambda e: e.activation(out=b2t[:], in_=b2t[:], func=AF.Ln),
          waits=((s_adve, dve_b2),))
        P("scalar", lambda e: e.activation(out=b2t[:], in_=b2t[:], func=AF.Exp, scale=0.5),
          inc=s_aact)
        act_b = CNT(s_aact)
        P("vector", lambda e, i=i: e.tensor_scalar(
            out=negB[:, i:i + (H - 1) * NQT + 1:NQT], in0=b2t[:], scalar1=-1.0, scalar2=None,
            op0=ALU.mult),
          waits=((s_aact, act_b),), inc=s_adve)
        for tt in range(4):
            gtt = i * 4 + tt
            vbank = (tt % 2) * 2
            if tt < 2:
                wv = ((s_aact, qk_evac2[(i, 12 + 2 * tt)]),
                      (s_aact, qk_evac2[(i, 13 + 2 * tt)]),
                      (s_atev, tev_done))
            else:
                wv = ((s_aact, v_evac.get(gtt - 2)), (s_atev, tev_done))
            for half in range(2):
                for cc in range(8):
                    P("tensor", lambda e, vbank=vbank, half=half, cc=cc, tt=tt: e.matmul(
                        out=v_ps[:, (vbank + half) * 512:(vbank + half + 1) * 512],
                        lhsT=xhatT[:, cc * 512 + tt * 128:cc * 512 + tt * 128 + 128],
                        rhs=wqkv_sb[:, cc * 3072 + 2048 + half * 512:cc * 3072 + 2048 + (half + 1) * 512],
                        start=(cc == 0), stop=(cc == 7)),
                      waits=wv,
                      inc=(s_ape if cc == 7 else None))
            pe_v = CNT(s_ape)
            buf = tt % 2
            P("scalar", lambda e, buf=buf, vbank=vbank: e.activation(
                out=vbuf[:, buf * 1024:(buf + 1) * 1024],
                in_=v_ps[:, vbank * 512:(vbank + 2) * 512], func=AF.Copy),
              waits=((s_ape, pe_v), (s_vs[tt % 2], CNT(s_vs[tt % 2]))),
              inc=s_aact)
            v_evac[gtt] = CNT(s_aact)
            act_v = CNT(s_aact)
            prog.dma("gpsimd", lambda e, buf=buf, t0=t0, tt=tt: e.dma_start(
                out=v_scr[t0 + tt * 128:t0 + (tt + 1) * 128, :],
                in_=vbuf[:, buf * 1024:(buf + 1) * 1024]),
              s_vs[tt % 2], waits=((s_aact, act_v),))

    P("sync", lambda e: e.nop(),
      waits=tuple((s, CNT(s)) for s in (s_qs[0], s_qs[1], s_ks[0], s_ks[1], s_vs[0], s_vs[1])))
    barrier()
    prog.flush()
    es_a.close()
    es_ia.close()

    if "b" not in phases:
        prog.flush()
        es0.close()
        nc.finalize()
        return nc

    s_kv = [S("kv0"), S("kv1")]
    s_qq = [S("qq0"), S("qq1")]
    s_bs = S("bs")
    s_bexp = S("bexp")
    s_bo = S("bo")
    s_bmsk = S("bmsk")
    s_bla = S("bla")
    s_boe = S("boe")
    s_os = [S("os0"), S("os1")]
    s_ls = [S("ls0"), S("ls1")]
    s_bmld = S("bmld")

    es_b = ExitStack()
    masks_sb = SB(es_b, "masks_sb", [128, 4 * 512], BF16)
    k_sb = SB(es_b, "k_sb", [128, 2 * T], BF16)
    v_sb = SB(es_b, "v_sb", [128, 2 * T], BF16)
    q_sb = SB(es_b, "q_sb", [128, 2 * T], BF16)
    a_sb = SB(es_b, "a_sb", [128, 4 * 512], BF16)
    obf_sb = SB(es_b, "obf_sb", [128, 2 * 512], BF16)
    l_sb = SB(es_b, "l_sb", [1, 2 * 512], F32)
    s_ps = PS(es_b, "s_ps", [128, 2048], F32)
    o_ps = PS(es_b, "o_ps", [128, 1024], F32)
    l_ps = PS(es_b, "l_ps", [128, 2 * 512], F32)

    prog.dma("sync", lambda e: e.dma_start(out=masks_sb[:], in_=masks_d), s_bmld)

    s_end = {}
    o_end = {}
    oe_evt = {}
    l_free = {}
    bank_exp = {0: 0, 1: 0, 2: 0, 3: 0}
    slot_o = {0: 0, 1: 0, 2: 0, 3: 0}

    for h in range(H):
        hb = (h % 2) * T
        w_prev = o_end.get((h - 2) * NQT + NQT - 1) if h >= 2 else None
        prog.dma("gpsimd", lambda e, h=h, hb=hb: e.dma_start(
            out=k_sb[:, hb:hb + T], in_=k_scr[h, :, :]),
          s_kv[h % 2], waits=((s_bo, w_prev),))
        prog.dma("gpsimd", lambda e, h=h, hb=hb: e.dma_start(
            out=v_sb[:, hb:hb + T].rearrange("p (c d) -> p c d", d=128),
            in_=v_scr[:, h * 128:(h + 1) * 128].rearrange("(c p) d -> p c d", p=128)),
          s_kv[h % 2], waits=((s_bo, w_prev),))
        kv_ld = CNT(s_kv[h % 2])
        prog.dma("sync", lambda e, h=h, hb=hb: e.dma_start(
            out=q_sb[:, hb:hb + T], in_=q_scr[h, :, :]),
          s_qq[h % 2], waits=((s_bo, w_prev),))
        q_ld = CNT(s_qq[h % 2])
        for qt in range(NQT):
            pi = h * NQT + qt
            nck = 4 * (qt + 1)
            qb = hb + qt * 512
            col = h * NQT + qt
            pending = []

            def flush_o():
                for c2, exp_v, msk_v in pending:
                    sl = c2 % 4
                    P("tensor", lambda e, c2=c2, pi=pi, hb=hb, sl=sl, nck=nck: e.matmul(
                        out=o_ps[:, (pi % 2) * 512:(pi % 2 + 1) * 512],
                        lhsT=v_sb[:, hb + c2 * 128:hb + (c2 + 1) * 128],
                        rhs=a_sb[:, sl * 512:(sl + 1) * 512],
                        start=(c2 == 0), stop=(c2 == nck - 1)),
                      waits=((s_bexp, exp_v), (s_bmsk, msk_v),
                             (s_boe, oe_evt.get(pi - 2) if c2 == 0 else None)),
                      inc=s_bo)
                    P("tensor", lambda e, c2=c2, pi=pi, sl=sl, nck=nck: e.matmul(
                        out=l_ps[:, (pi % 2) * 512:(pi % 2 + 1) * 512],
                        lhsT=onesb_sb[:],
                        rhs=a_sb[:, sl * 512:(sl + 1) * 512],
                        start=(c2 == 0), stop=(c2 == nck - 1)),
                      waits=((s_bla, l_free.get(pi - 2) if c2 == 0 else None),),
                      inc=s_bo)
                    slot_o[sl] = CNT(s_bo)
                pending.clear()

            for c in range(nck):
                bank = c % 4
                P("tensor", lambda e, c=c, hb=hb, qb=qb, bank=bank: e.matmul(
                    out=s_ps[:, bank * 512:(bank + 1) * 512],
                    lhsT=k_sb[:, hb + c * 128:hb + (c + 1) * 128],
                    rhs=q_sb[:, qb:qb + 512], start=True, stop=True),
                  waits=((s_kv[h % 2], kv_ld), (s_qq[h % 2], q_ld), (s_bexp, bank_exp[bank])),
                  inc=s_bs)
                if c % 2 == 1:
                    flush_o()
                    slot = (c - 1) % 4
                    P("scalar", lambda e, slot=slot, col=col: e.activation(
                        out=a_sb[:, slot * 512:(slot + 2) * 512],
                        in_=s_ps[:, slot * 512:(slot + 2) * 512],
                        func=AF.Exp, bias=negB[:, col:col + 1]),
                      waits=((s_bs, CNT(s_bs)),
                             (s_bo, max(slot_o[slot], slot_o[slot + 1]))),
                      inc=s_bexp)
                    bank_exp[slot] = CNT(s_bexp)
                    bank_exp[slot + 1] = CNT(s_bexp)
                    exp_now = CNT(s_bexp)
                    for c2 in (c - 1, c):
                        sl = c2 % 4
                        diag = c2 >= nck - 4
                        if diag:
                            jm = c2 - (nck - 4)
                            P("vector", lambda e, sl=sl, jm=jm: e.tensor_tensor(
                                out=a_sb[:, sl * 512:(sl + 1) * 512],
                                in0=a_sb[:, sl * 512:(sl + 1) * 512],
                                in1=masks_sb[:, jm * 512:(jm + 1) * 512], op=ALU.mult),
                              waits=((s_bexp, exp_now), (s_bmld, 16)), inc=s_bmsk)
                        pending.append((c2, exp_now, CNT(s_bmsk)))
            flush_o()
            s_end[pi] = CNT(s_bs)
            o_end[pi] = CNT(s_bo)
            P("scalar", lambda e, pi=pi: e.activation(
                out=l_sb[0:1, (pi % 2) * 512:(pi % 2 + 1) * 512],
                in_=l_ps[0:1, (pi % 2) * 512:(pi % 2 + 1) * 512],
                func=AF.Copy),
              waits=((s_bo, o_end[pi]), (s_ls[pi % 2], CNT(s_ls[pi % 2]))),
              inc=s_bla)
            l_free[pi] = CNT(s_bla)
            prog.dma("sync", lambda e, h=h, qt=qt, pi=pi: e.dma_start(
                out=l_scr[h, qt * 512:(qt + 1) * 512].rearrange("(o f) -> o f", o=1),
                in_=l_sb[0:1, (pi % 2) * 512:(pi % 2 + 1) * 512]),
              s_ls[pi % 2], waits=((s_bla, CNT(s_bla)),))
            P("vector", lambda e, pi=pi: e.tensor_copy(
                out=obf_sb[:, (pi % 2) * 512:(pi % 2 + 1) * 512],
                in_=o_ps[:, (pi % 2) * 512:(pi % 2 + 1) * 512]),
              waits=((s_bo, o_end[pi]), (s_os[pi % 2], CNT(s_os[pi % 2]))),
              inc=s_boe)
            oe_evt[pi] = CNT(s_boe)
            prog.dma("sync", lambda e, h=h, qt=qt, pi=pi: e.dma_start(
                out=o_scr[h * 128:(h + 1) * 128, qt * 512:(qt + 1) * 512],
                in_=obf_sb[:, (pi % 2) * 512:(pi % 2 + 1) * 512]),
              s_os[pi % 2], waits=((s_boe, CNT(s_boe)),))

    P("sync", lambda e: e.nop(),
      waits=tuple((s, CNT(s)) for s in (s_os[0], s_os[1], s_ls[0], s_ls[1])))
    barrier()
    prog.flush()
    es_b.close()

    if "c" not in phases:
        prog.flush()
        es0.close()
        nc.finalize()
        return nc

    s_co = [S("co0"), S("co1")]
    s_cx2 = [S("cx20"), S("cx21")]
    s_cl = S("cl")
    s_cpe = S("cpe")
    s_cact = S("cact")
    s_cdve = S("cdve")
    s_cpet = S("cpet")
    s_ctev = S("ctev")
    s_ost = [S("cst0"), S("cst1")]
    s_carry = S("carry0")
    s_csel = S("csel")
    s_cpool = S("cpool")

    _pcut = int(os.environ.get("P2_CUT", "9"))
    _psub = int(os.environ.get("P2_SUB", "9"))
    es_c = ExitStack()
    sel8_sb = SB(es_c, "sel8_sb", [8, 8 * 128], BF16)
    ot_sb = SB(es_c, "ot_sb", [128, 2 * 4096], BF16)
    linv_sb = SB(es_c, "linv_sb", [8, 2 * 512], F32)
    linvh_sb = SB(es_c, "linvh_sb", [8, 2 * 512], BF16)
    linvl_sb = SB(es_c, "linvl_sb", [8, 2 * 512], BF16)
    x2t = SB(es_c, "x2t", [128, 2 * 4096], F32)
    anrm_sb = SB(es_c, "anrm_sb", [128, 1024], F32)
    araw_sb = SB(es_c, "araw_sb", [128, 2 * 1024], F32)
    xh2_bf = SB(es_c, "xh2_bf", [128, 4096], BF16)
    xh2T = SB(es_c, "xh2T", [128, 4096], BF16)
    xrT = SB(es_c, "xrT", [128, 4096], BF16)
    xkT = SB(es_c, "xkT", [128, 4096], BF16)
    carry = SB(es_c, "carry", [128, 8], BF16)
    r_sb = SB(es_c, "r_sb", [128, 1024], F32)
    krel_sb = SB(es_c, "krel_sb", [128, 2 * 512], F32)
    kact_sb = SB(es_c, "kact_sb", [128, 4096], BF16)
    ffn_sb = SB(es_c, "ffn_sb", [128, 4096], F32)
    bc_ps = PS(es_c, "bc_ps", [128, 1024], F32)
    big_ps = PS(es_c, "big_ps", [128, 2048], F32)
    tp2_ps = PS(es_c, "tp2_ps", [128, 2048], BF16)

    P("gpsimd", lambda e: e.memset(carry[:], 0.0), inc=s_carry)
    prog.dma("sync", lambda e: e.dma_start(out=sel8_sb[:], in_=sel8_d), s_csel)

    ot_free = {}
    x_free = {}
    l_free = {}
    anrm_evt = {}
    sig_evt = {}
    ffn_evt = {}
    relu_evt = {}
    sq_evt = {}
    rmul_evt = {}
    linv_bf = {}
    bcast_done = {}
    ffn_pe_done = {}
    ln2_st8 = {}
    ln2_st24 = {}
    ln3_st12 = {}
    ln3_st28 = {}
    ln4_var_free = [None]
    ln4_rstd_free = [None]
    mixp_hist = {}
    store_done = {}
    nblk2 = [0]

    def emit_linv(j):
        lb = (j % 2) * 512
        prog.dma("sync", lambda e, j=j, lb=lb: e.dma_start(
            out=linv_sb[:, lb:lb + 512], in_=l_scr[:, j * 512:(j + 1) * 512]),
          s_cl, waits=((s_cdve, linv_bf.get(j - 2)),))
        cl_ld = CNT(s_cl)
        P("scalar", lambda e, lb=lb: e.activation(
            out=linv_sb[:, lb:lb + 512], in_=linv_sb[0:8, lb:lb + 512], func=AF.Ln),
          waits=((s_cl, cl_ld),))
        P("scalar", lambda e, lb=lb: e.activation(
            out=linv_sb[:, lb:lb + 512], in_=linv_sb[:, lb:lb + 512],
            func=AF.Exp, scale=-1.0), inc=s_cact)
        l_free[j] = CNT(s_cact)
        P("vector", lambda e, lb=lb: e.tensor_copy(
            out=linvh_sb[:, lb:lb + 512], in_=linv_sb[:, lb:lb + 512]),
          waits=((s_cact, l_free[j]), (s_cpe, bcast_done.get(j - 2))), inc=s_cdve)
        P("vector", lambda e, lb=lb: e.tensor_tensor(
            out=linvl_sb[:, lb:lb + 512], in0=linv_sb[:, lb:lb + 512],
            in1=linvh_sb[:, lb:lb + 512], op=ALU.subtract),
          inc=s_cdve)
        linv_bf[j] = CNT(s_cdve)

    for i in range(NT):
        t0 = i * 512
        ob = i % 2
        xb = i % 2
        lb = (i % 2) * 512
        prog.dma("sync", lambda e, t0=t0, ob=ob: e.dma_start(
            out=ot_sb[:, ob * 4096:(ob + 1) * 4096].rearrange("p (c f) -> p c f", c=8),
            in_=o_scr[:, t0:t0 + 512].rearrange("(c p) f -> p c f", p=128)),
          s_co[i % 2], waits=((s_cpe, ot_free.get(i - 2)),))
        co_ld = CNT(s_co[i % 2])
        prog.dma("sync", lambda e, t0=t0, xb=xb: e.dma_start(
            out=x2t[:, xb * 4096:(xb + 1) * 4096].rearrange("p (c f) -> p c f", c=4),
            in_=x_in[t0:t0 + 512, :].rearrange("(c p) f -> p c f", p=128)),
          s_cx2[i % 2], waits=((s_cdve, x_free.get(i - 2)),) + store_done.get(i - 2, ()))
        cx_ld = CNT(s_cx2[i % 2])
        if i == 0:
            emit_linv(0)
        onrm_evts = []
        for cc in range(8):
            P("tensor", lambda e, cc=cc, lb=lb: e.matmul(
                out=bc_ps[:, (cc % 2) * 512:(cc % 2 + 1) * 512],
                lhsT=sel8_sb[:, cc * 128:(cc + 1) * 128], rhs=linvh_sb[0:8, lb:lb + 512],
                start=True, stop=False),
              waits=((s_cdve, linv_bf[i]), (s_csel, 16),
                     (s_cact, relu_evt.get((i - 1, 7))),
                     (s_cdve, onrm_evts[cc - 2] if cc >= 2 else None)))
            P("tensor", lambda e, cc=cc, lb=lb: e.matmul(
                out=bc_ps[:, (cc % 2) * 512:(cc % 2 + 1) * 512],
                lhsT=sel8_sb[:, cc * 128:(cc + 1) * 128], rhs=linvl_sb[0:8, lb:lb + 512],
                start=False, stop=True),
              inc=s_cpe)
            P("vector", lambda e, cc=cc, ob=ob: e.tensor_tensor(
                out=ot_sb[:, ob * 4096 + cc * 512:ob * 4096 + (cc + 1) * 512],
                in0=ot_sb[:, ob * 4096 + cc * 512:ob * 4096 + (cc + 1) * 512],
                in1=bc_ps[:, (cc % 2) * 512:(cc % 2 + 1) * 512], op=ALU.mult),
              waits=((s_cpe, CNT(s_cpe)), (s_co[i % 2], co_ld), (s_cx2[i % 2], cx_ld)),
              inc=s_cdve)
            onrm_evts.append(CNT(s_cdve))
        onrm_done = CNT(s_cdve)
        bcast_done[i] = CNT(s_cpe)
        if i + 1 < NT:
            emit_linv(i + 1)
        x1_evt = {}
        cast_evt2 = {}
        for tt in range(4):
            for half in range(2):
                bank = (tt % 2) * 2 + half
                for cc in range(8):
                    P("tensor", lambda e, bank=bank, cc=cc, tt=tt, half=half, ob=ob: e.matmul(
                        out=big_ps[:, bank * 512:(bank + 1) * 512],
                        lhsT=ot_sb[:, ob * 4096 + cc * 512 + tt * 128:ob * 4096 + cc * 512 + tt * 128 + 128],
                        rhs=wout_sb[:, cc * 1024 + half * 512:cc * 1024 + (half + 1) * 512],
                        start=(cc == 0), stop=(cc == 7)),
                      waits=((s_cdve, onrm_done),
                             (s_cdve, anrm_evt.get((i, tt - 2))),
                             (s_cdve, ffn_evt.get((i - 1, tt + 2)) if tt < 2 else None)),
                      inc=(s_cpe if cc == 7 else None))
            pe_ao = CNT(s_cpe)
            if tt == 3:
                ot_free[i] = pe_ao
            bank0 = (tt % 2) * 2
            ab = tt % 2
            P("scalar", lambda e, bank0=bank0, ab=ab: e.activation(
                out=araw_sb[:, ab * 1024:(ab + 1) * 1024],
                in_=big_ps[:, bank0 * 512:(bank0 + 2) * 512], func=AF.Copy),
              waits=((s_cpe, pe_ao),
                     (s_cdve, ln2_st24.get(tt - 2 if tt >= 2 else tt + 2))),
              inc=s_cdve)
            anrm_evt[(i, tt)] = CNT(s_cdve)
            ev_done = CNT(s_cdve)
            P("vector", lambda e, ab=ab, tt=tt: e.reduce_sum(
                out=stat[:, tt:tt + 1], in_=araw_sb[:, ab * 1024:(ab + 1) * 1024],
                axis=AXX),
              waits=((s_cdve, ev_done),), inc=s_cdve)
            P("scalar", lambda e, ab=ab, tt=tt: e.activation(
                out=junk_a[:], in_=araw_sb[:, ab * 1024:(ab + 1) * 1024],
                func=AF.Square, accum_out=stat[:, 8 + tt:9 + tt]),
              waits=((s_cdve, ev_done), (s_cdve, ln2_st8.get(tt))), inc=s_cact)
            asq = CNT(s_cact)
            P("vector", lambda e, tt=tt: e.tensor_scalar(
                out=stat[:, 16 + tt:17 + tt], in0=stat[:, tt:tt + 1],
                scalar1=1.0 / C, scalar2=None, op0=ALU.mult),
              waits=((s_cdve, CNT(s_cdve)),), inc=s_cdve)
            P("vector", lambda e, tt=tt: e.tensor_tensor(
                out=stat[:, 24 + tt:25 + tt], in0=stat[:, 16 + tt:17 + tt],
                in1=stat[:, 16 + tt:17 + tt], op=ALU.mult),
              waits=((s_cdve, CNT(s_cdve)),), inc=s_cdve)
            P("vector", lambda e, tt=tt: e.scalar_tensor_tensor(
                out=stat[:, 24 + tt:25 + tt], in0=stat[:, 8 + tt:9 + tt], scalar=1.0 / C,
                in1=stat[:, 24 + tt:25 + tt], op0=ALU.mult, op1=ALU.subtract),
              waits=((s_cact, asq), (s_cdve, CNT(s_cdve))), inc=s_cdve)
            dv = CNT(s_cdve)
            ln2_st8[tt] = dv
            P("scalar", lambda e, tt=tt: e.activation(
                out=stat[:, 24 + tt:25 + tt], in_=stat[:, 24 + tt:25 + tt],
                func=AF.Ln, bias=scal_sb[:, 0:1]),
              waits=((s_cdve, dv), (s_cdve, ln2_st24.get(tt))))
            P("scalar", lambda e, tt=tt: e.activation(
                out=stat[:, 24 + tt:25 + tt], in_=stat[:, 24 + tt:25 + tt],
                func=AF.Exp, scale=-0.5), inc=s_cact)
            ar = CNT(s_cact)
            P("vector", lambda e, ab=ab, tt=tt: e.tensor_scalar(
                out=anrm_sb[:], in0=araw_sb[:, ab * 1024:(ab + 1) * 1024],
                scalar1=stat[:, 16 + tt:17 + tt], scalar2=stat[:, 24 + tt:25 + tt],
                op0=ALU.subtract, op1=ALU.mult),
              waits=((s_cact, ar),), inc=s_cdve)
            ln2_st24[tt] = CNT(s_cdve)
            P("vector", lambda e, tt=tt, xb=xb: e.tensor_tensor(
                out=x2t[:, xb * 4096 + tt * 1024:xb * 4096 + (tt + 1) * 1024],
                in0=anrm_sb[:],
                in1=x2t[:, xb * 4096 + tt * 1024:xb * 4096 + (tt + 1) * 1024], op=ALU.add),
              inc=s_cdve)
            x1_evt[tt] = CNT(s_cdve)
            P("vector", lambda e, tt=tt, xb=xb: e.reduce_sum(
                out=stat[:, 4 + tt:5 + tt],
                in_=x2t[:, xb * 4096 + tt * 1024:xb * 4096 + (tt + 1) * 1024], axis=AXX),
              inc=s_cdve)
            P("scalar", lambda e, tt=tt, xb=xb: e.activation(
                out=junk_a[:], in_=x2t[:, xb * 4096 + tt * 1024:xb * 4096 + (tt + 1) * 1024],
                func=AF.Square, accum_out=stat[:, 12 + tt:13 + tt]),
              waits=((s_cdve, x1_evt[tt]), (s_cdve, ln3_st12.get(tt))), inc=s_cact)
            asq3 = CNT(s_cact)
            P("vector", lambda e, tt=tt: e.tensor_scalar(
                out=stat[:, 20 + tt:21 + tt], in0=stat[:, 4 + tt:5 + tt],
                scalar1=1.0 / C, scalar2=None, op0=ALU.mult),
              waits=((s_cdve, CNT(s_cdve)),), inc=s_cdve)
            P("vector", lambda e, tt=tt: e.tensor_tensor(
                out=stat[:, 28 + tt:29 + tt], in0=stat[:, 20 + tt:21 + tt],
                in1=stat[:, 20 + tt:21 + tt], op=ALU.mult),
              waits=((s_cdve, CNT(s_cdve)),), inc=s_cdve)
            P("vector", lambda e, tt=tt: e.scalar_tensor_tensor(
                out=stat[:, 28 + tt:29 + tt], in0=stat[:, 12 + tt:13 + tt], scalar=1.0 / C,
                in1=stat[:, 28 + tt:29 + tt], op0=ALU.mult, op1=ALU.subtract),
              waits=((s_cact, asq3), (s_cdve, CNT(s_cdve))), inc=s_cdve)
            dv3 = CNT(s_cdve)
            ln3_st12[tt] = dv3
            P("scalar", lambda e, tt=tt: e.activation(
                out=stat[:, 28 + tt:29 + tt], in_=stat[:, 28 + tt:29 + tt],
                func=AF.Ln, bias=scal_sb[:, 0:1]),
              waits=((s_cdve, dv3), (s_cdve, ln3_st28.get(tt))))
            P("scalar", lambda e, tt=tt: e.activation(
                out=stat[:, 28 + tt:29 + tt], in_=stat[:, 28 + tt:29 + tt],
                func=AF.Exp, scale=-0.5), inc=s_cact)
            ar3 = CNT(s_cact)
            P("vector", lambda e, tt=tt, xb=xb: e.tensor_scalar(
                out=xh2_bf[:, tt * 1024:(tt + 1) * 1024],
                in0=x2t[:, xb * 4096 + tt * 1024:xb * 4096 + (tt + 1) * 1024],
                scalar1=stat[:, 20 + tt:21 + tt], scalar2=stat[:, 28 + tt:29 + tt],
                op0=ALU.subtract, op1=ALU.mult),
              waits=((s_cact, ar3),), inc=s_cdve)
            ln3_st28[tt] = CNT(s_cdve)
            cast_evt2[tt] = CNT(s_cdve)
        for tt in range(4):
            gg = i * 4 + tt
            tb = (gg % 2) * 1024
            for s in range(8):
                cc = s
                P("tensor", lambda e, tt=tt, cc=cc, s=s, tb=tb: e.transpose(
                    out=tp2_ps[:, tb + s * 128:tb + (s + 1) * 128],
                    in_=xh2_bf[:, tt * 1024 + cc * 128:tt * 1024 + (cc + 1) * 128],
                    identity=eye_sb[:]),
                  waits=((s_cdve, cast_evt2[tt]), (s_ctev, 8 * (gg - 1))),
                  inc=s_cpet)
            tr_done2 = CNT(s_cpet)
            for s in range(8):
                cc = s
                P("vector", lambda e, tt=tt, cc=cc, s=s, tb=tb: e.tensor_copy(
                    out=xh2T[:, cc * 512 + tt * 128:cc * 512 + tt * 128 + 128],
                    in_=tp2_ps[:, tb + s * 128:tb + (s + 1) * 128]),
                  waits=((s_cpet, tr_done2), (s_cpool, mixp_hist.get(i - 1))),
                  inc=s_ctev)
        if _pcut < 4:
            continue
        tev2_done = CNT(s_ctev)
        for cc in range(8):
            s0 = cc * 512
            w0 = ((s_carry, 1),) if i == 0 and cc == 0 else ()
            P("vector", lambda e, s0=s0: e.tensor_tensor(
                out=xrT[:, s0 + 1:s0 + 512], in0=xh2T[:, s0 + 1:s0 + 512],
                in1=xh2T[:, s0:s0 + 511], op=ALU.subtract), waits=w0)
            P("vector", lambda e, s0=s0, cc=cc: e.tensor_tensor(
                out=xrT[:, s0:s0 + 1], in0=xh2T[:, s0:s0 + 1],
                in1=carry[:, cc:cc + 1], op=ALU.subtract))
            P("vector", lambda e, s0=s0, cc=cc: e.scalar_tensor_tensor(
                out=xkT[:, s0 + 1:s0 + 512], in0=xrT[:, s0 + 1:s0 + 512],
                scalar=mu_sb[:, 8 + cc:9 + cc], in1=xh2T[:, s0:s0 + 511],
                op0=ALU.mult, op1=ALU.add))
            P("vector", lambda e, s0=s0, cc=cc: e.scalar_tensor_tensor(
                out=xkT[:, s0:s0 + 1], in0=xrT[:, s0:s0 + 1],
                scalar=mu_sb[:, 8 + cc:9 + cc], in1=carry[:, cc:cc + 1],
                op0=ALU.mult, op1=ALU.add),
              inc=s_cdve)
        mixk_done = CNT(s_cdve)
        for cc in range(8):
            s0 = cc * 512
            P("vector", lambda e, s0=s0, cc=cc: e.scalar_tensor_tensor(
                out=xrT[:, s0 + 1:s0 + 512], in0=xrT[:, s0 + 1:s0 + 512],
                scalar=mu_sb[:, cc:cc + 1], in1=xh2T[:, s0:s0 + 511],
                op0=ALU.mult, op1=ALU.add))
            P("vector", lambda e, s0=s0, cc=cc: e.scalar_tensor_tensor(
                out=xrT[:, s0:s0 + 1], in0=xrT[:, s0:s0 + 1],
                scalar=mu_sb[:, cc:cc + 1], in1=carry[:, cc:cc + 1],
                op0=ALU.mult, op1=ALU.add))
            P("vector", lambda e, s0=s0, cc=cc: e.tensor_copy(
                out=carry[:, cc:cc + 1], in_=xh2T[:, s0 + 511:s0 + 512]),
              inc=s_cdve)
        mix_done = CNT(s_cdve)
        mixp_done = CNT(s_cpool)
        mixp_hist[i] = mixp_done
        if _pcut < 5:
            continue
        for dd in range(8):
            bank = dd % 2
            for cc in range(8):
                P("tensor", lambda e, bank=bank, cc=cc, dd=dd: e.matmul(
                    out=bc_ps[:, bank * 512:(bank + 1) * 512],
                    lhsT=wkT_sb[:, cc * 1024 + dd * 128:cc * 1024 + (dd + 1) * 128],
                    rhs=xkT[:, cc * 512:(cc + 1) * 512],
                    start=(cc == 0), stop=(cc == 7)),
                  waits=((s_cdve, mixk_done),
                         (s_cact, relu_evt.get((i, dd - 2)))),
                  inc=(s_cpe if cc == 7 else None))
            pe_k = CNT(s_cpe)
            P("scalar", lambda e, bank=bank, dd=dd: e.activation(
                out=krel_sb[:, (dd % 2) * 512:(dd % 2 + 1) * 512],
                in_=bc_ps[:, bank * 512:(bank + 1) * 512], func=AF.Relu),
              waits=((s_cpe, pe_k), (s_cpool, sq_evt.get((i, dd - 2)))),
              inc=s_cact)
            relu_evt[(i, dd)] = CNT(s_cact)
            P("gpsimd", lambda e, dd=dd: e.tensor_tensor(
                out=kact_sb[:, dd * 512:(dd + 1) * 512],
                in0=krel_sb[:, (dd % 2) * 512:(dd % 2 + 1) * 512],
                in1=krel_sb[:, (dd % 2) * 512:(dd % 2 + 1) * 512], op=ALU.mult),
              waits=((s_cact, relu_evt[(i, dd)]),
                     (s_cpe, ffn_pe_done.get(i - 1) if dd == 0 else None)),
              inc=s_cpool)
            sq_evt[(i, dd)] = CNT(s_cpool)
        kact_done = CNT(s_cpool)
        if _pcut < 6:
            continue
        pe_r_evt = {}
        pe_v_evt = {}

        def emit_rmm(tt, i=i, mix_done=mix_done):
            for half in range(2):
                bank = (tt % 2) * 2 + half
                for cc in range(8):
                    P("tensor", lambda e, bank=bank, cc=cc, tt=tt, half=half: e.matmul(
                        out=big_ps[:, bank * 512:(bank + 1) * 512],
                        lhsT=xrT[:, cc * 512 + tt * 128:cc * 512 + tt * 128 + 128],
                        rhs=wrT_sb[:, cc * 1024 + half * 512:cc * 1024 + (half + 1) * 512],
                        start=(cc == 0), stop=(cc == 7)),
                      waits=((s_cdve, mix_done),
                             (s_cdve, anrm_evt.get((i, tt + 2)) if tt < 2 else None),
                             (s_cdve, ffn_evt.get((i, tt - 2)) if tt >= 2 else None)),
                      inc=(s_cpe if cc == 7 else None))
            pe_r_evt[tt] = CNT(s_cpe)

        def emit_sig(tt, i=i, kact_done=kact_done):
            bank0 = (tt % 2) * 2
            P("scalar", lambda e, bank0=bank0: e.activation(
                out=r_sb[:], in_=big_ps[:, bank0 * 512:(bank0 + 2) * 512],
                func=AF.Exp, scale=-1.0),
              waits=((s_cpe, pe_r_evt[tt]),
                     (s_cdve, rmul_evt.get((i, tt - 1)))),
              inc=s_cact)
            sig_evt[(i, tt)] = CNT(s_cact)
            P("scalar", lambda e: e.activation(
                out=r_sb[:], in_=r_sb[:], func=AF.Ln, bias=scal_sb[:, 1:2]))
            P("scalar", lambda e: e.activation(
                out=r_sb[:], in_=r_sb[:], func=AF.Exp, scale=-1.0), inc=s_cact)
            sig_evt[(i, tt, "r")] = CNT(s_cact)

        def emit_vmm(tt, i=i, kact_done=kact_done):
            for half in range(2):
                bank = (tt % 2) * 2 + half
                for cc in range(8):
                    P("tensor", lambda e, bank=bank, cc=cc, tt=tt, half=half: e.matmul(
                        out=big_ps[:, bank * 512:(bank + 1) * 512],
                        lhsT=kact_sb[:, cc * 512 + tt * 128:cc * 512 + tt * 128 + 128],
                        rhs=wvT_sb[:, cc * 1024 + half * 512:cc * 1024 + (half + 1) * 512],
                        start=(cc == 0), stop=(cc == 7)),
                      waits=((s_cpool, kact_done), (s_cact, sig_evt[(i, tt)]),),
                      inc=(s_cpe if cc == 7 else None))
            pe_v_evt[tt] = CNT(s_cpe)

        def emit_ffnmul(tt, i=i):
            bank0 = (tt % 2) * 2
            P("vector", lambda e, tt=tt, bank0=bank0: e.tensor_tensor(
                out=ffn_sb[:, tt * 1024:(tt + 1) * 1024], in0=r_sb[:],
                in1=big_ps[:, bank0 * 512:(bank0 + 2) * 512], op=ALU.mult),
              waits=((s_cpe, pe_v_evt[tt]), (s_cact, sig_evt[(i, tt, "r")])),
              inc=s_cdve)
            ffn_evt[(i, tt)] = CNT(s_cdve)
            rmul_evt[(i, tt)] = ffn_evt[(i, tt)]

        emit_rmm(0)
        emit_sig(0)
        emit_rmm(1)
        emit_vmm(0)
        emit_ffnmul(0)
        emit_sig(1)
        emit_vmm(1)
        emit_ffnmul(1)
        emit_rmm(2)
        emit_sig(2)
        emit_rmm(3)
        emit_vmm(2)
        emit_ffnmul(2)
        emit_sig(3)
        emit_vmm(3)
        emit_ffnmul(3)
        ffn_pe_done[i] = CNT(s_cpe)
        if _pcut < 7:
            continue
        for tt in range(4):
            P("vector", lambda e, tt=tt: e.reduce_sum(
                out=stat[:, 32 + tt:33 + tt], in_=ffn_sb[:, tt * 1024:(tt + 1) * 1024],
                axis=AXX),
              waits=((s_cdve, ffn_evt[(i, tt)]),), inc=s_cdve)
            P("scalar", lambda e, tt=tt: e.activation(
                out=junk_a[:], in_=ffn_sb[:, tt * 1024:(tt + 1) * 1024],
                func=AF.Square, accum_out=stat[:, 36 + tt:37 + tt]),
              waits=((s_cdve, ffn_evt[(i, tt)]), (s_cdve, ln4_var_free[0])), inc=s_cact)
        asq3 = CNT(s_cact)
        P("vector", lambda e: e.tensor_scalar(
            out=stat[:, 40:44], in0=stat[:, 32:36], scalar1=1.0 / C, scalar2=None,
            op0=ALU.mult),
          waits=((s_cdve, CNT(s_cdve)),), inc=s_cdve)
        P("vector", lambda e: e.tensor_tensor(
            out=stat[:, 44:48], in0=stat[:, 40:44], in1=stat[:, 40:44], op=ALU.mult),
          waits=((s_cdve, CNT(s_cdve)),), inc=s_cdve)
        P("vector", lambda e: e.scalar_tensor_tensor(
            out=stat[:, 44:48], in0=stat[:, 36:40], scalar=1.0 / C, in1=stat[:, 44:48],
            op0=ALU.mult, op1=ALU.subtract),
          waits=((s_cact, asq3), (s_cdve, CNT(s_cdve))), inc=s_cdve)
        dv3 = CNT(s_cdve)
        ln4_var_free[0] = dv3
        P("scalar", lambda e: e.activation(out=stat[:, 44:48], in_=stat[:, 44:48],
                                           func=AF.Ln, bias=scal_sb[:, 0:1]),
          waits=((s_cdve, dv3), (s_cdve, ln4_rstd_free[0])))
        P("scalar", lambda e: e.activation(out=stat[:, 44:48], in_=stat[:, 44:48],
                                           func=AF.Exp, scale=-0.5), inc=s_cact)
        ar3 = CNT(s_cact)
        for tt in range(4):
            ost = (i * 4 + tt) % 2
            P("vector", lambda e, tt=tt: e.tensor_scalar(
                out=anrm_sb[:], in0=ffn_sb[:, tt * 1024:(tt + 1) * 1024],
                scalar1=stat[:, 40 + tt:41 + tt], scalar2=stat[:, 44 + tt:45 + tt],
                op0=ALU.subtract, op1=ALU.mult),
              waits=((s_cact, ar3),))
            P("vector", lambda e, tt=tt, xb=xb: e.tensor_tensor(
                out=x2t[:, xb * 4096 + tt * 1024:xb * 4096 + (tt + 1) * 1024],
                in0=anrm_sb[:],
                in1=x2t[:, xb * 4096 + tt * 1024:xb * 4096 + (tt + 1) * 1024], op=ALU.add),
              inc=s_cdve)
            od = CNT(s_cdve)
            prog.dma("gpsimd", lambda e, t0=t0, tt=tt, xb=xb: e.dma_start(
                out=out_d[t0 + tt * 128:t0 + (tt + 1) * 128, :],
                in_=x2t[:, xb * 4096 + tt * 1024:xb * 4096 + (tt + 1) * 1024]),
              s_ost[ost], waits=((s_cdve, od),))
        x_free[i] = CNT(s_cdve)
        ln4_rstd_free[0] = CNT(s_cdve)
        store_done[i] = ((s_ost[0], CNT(s_ost[0])), (s_ost[1], CNT(s_ost[1])))

    if _pcut >= 7:
        P("sync", lambda e: e.nop(),
          waits=((s_ost[0], CNT(s_ost[0])), (s_ost[1], CNT(s_ost[1]))))
    else:
        P("sync", lambda e: e.nop(), waits=((s_cdve, CNT(s_cdve)),))

    prog.flush()
    es_c.close()
    es0.close()
    nc.finalize()
    return nc


_NC_CACHE = {}


def get_nc(T):
    if T not in _NC_CACHE:
        _NC_CACHE[T] = build_nc(T)
    return _NC_CACHE[T]


def kernel(L0_attn_x, L0_attn_w_qkv, L0_attn_w_out, L0_attn_head_scales,
           L0_ffn_W_r, L0_ffn_W_k, L0_ffn_W_v, L0_ffn_mu_r, L0_ffn_mu_k):
    x = np.ascontiguousarray(np.asarray(L0_attn_x, dtype=np.float32))
    B, T, _ = x.shape
    nc = get_nc(T)
    consts = host_consts()
    base = {
        "w_qkv": np.ascontiguousarray(np.asarray(L0_attn_w_qkv, np.float32)),
        "w_out": np.ascontiguousarray(np.asarray(L0_attn_w_out, np.float32)),
        "head_scales": np.ascontiguousarray(
            np.asarray(L0_attn_head_scales, np.float32).reshape(1, 8)),
        "w_r": np.ascontiguousarray(np.asarray(L0_ffn_W_r, np.float32).T),
        "w_k": np.ascontiguousarray(np.asarray(L0_ffn_W_k, np.float32).T),
        "w_v": np.ascontiguousarray(np.asarray(L0_ffn_W_v, np.float32).T),
        "mu_r": np.ascontiguousarray(np.asarray(L0_ffn_mu_r, np.float32)),
        "mu_k": np.ascontiguousarray(np.asarray(L0_ffn_mu_k, np.float32)),
        "eye_bf": consts["eye_bf"], "ones_bf": consts["ones_bf"],
        "ones_f": consts["ones_f"], "sel8": consts["sel8"],
        "dirs33": consts["dirs33"], "d3s": consts["d3s"],
        "masks": consts["masks"], "scal": consts["scal"],
    }
    in_maps = [dict(base, x=np.ascontiguousarray(x[b])) for b in range(B)]
    res = run_bass_kernel_spmd(nc, in_maps, core_ids=list(range(B)))
    return np.stack([np.asarray(r["out"], np.float32) for r in res.results], axis=0)
